# version 13 (final)
# speedup vs baseline: 1.2189x; 1.0071x over previous
import numpy as np
from contextlib import ExitStack
import concourse.bass as bass
import concourse.mybir as mybir
from concourse.bass_utils import run_bass_kernel_spmd

F32 = mybir.dt.float32
BF16 = mybir.dt.bfloat16
I16 = mybir.dt.int16
U16 = mybir.dt.uint16
ALU = mybir.AluOpType
AF = mybir.ActivationFunctionType
AX = mybir.AxisListType

STAGE = 4
STOP_AFTER = None
DEBUG = []

B = 4; T = 8192; D = 1024; M = 64
TC = 4096
NH = 2048
L = 32
CLO = 64
NT = 32
KD = 8
OC = 24
I3 = 3072
DT = 0.01
QMAX = 32767.0
LNEPS = 1e-5

_BUILT = {}


class _StopBuild(Exception):
    pass


class Sem:
    def __init__(self, stk, nc, name):
        self.h = stk.enter_context(nc.semaphore(name))
        self.v = 0

    def inc(self, instr, n=1):
        instr.then_inc(self.h, n)
        self.v += n
        return instr

    def dinc(self, instr):
        return self.inc(instr, 16)

    def wait(self, eng, val=None):
        eng.wait_ge(self.h, self.v if val is None else val)


def _ap(t, off, dims):
    return bass.AP(t, off, dims)


def build(stage=STAGE, debug=(), stop_after=None):
    nc = bass.Bass("TRN2", target_bir_lowering=False, debug=False)

    d_u = nc.dram_tensor("u_s", [TC, D], F32, kind="ExternalInput").ap()
    d_woz = nc.dram_tensor("Woz", [128, D], F32, kind="ExternalInput").ap()
    d_wb = nc.dram_tensor("WB", [128, D], F32, kind="ExternalInput").ap()
    d_c = nc.dram_tensor("C_", [D, 128], F32, kind="ExternalInput").ap()
    d_dm = nc.dram_tensor("D_", [D, D], F32, kind="ExternalInput").ap()
    d_wg = nc.dram_tensor("w_gate", [I3, D], F32, kind="ExternalInput").ap()
    d_wu = nc.dram_tensor("w_up", [I3, D], F32, kind="ExternalInput").ap()
    d_wd = nc.dram_tensor("w_down", [D, I3], F32, kind="ExternalInput").ap()
    d_bom = nc.dram_tensor("bias_om2", [128, 1], F32, kind="ExternalInput").ap()
    d_bze = nc.dram_tensor("bias_ze2", [128, 1], F32, kind="ExternalInput").ap()
    d_bbz = nc.dram_tensor("bias_Bz2", [128, 1], F32, kind="ExternalInput").ap()
    d_bby = nc.dram_tensor("bias_By2", [128, 1], F32, kind="ExternalInput").ap()
    d_slot = nc.dram_tensor("slotmask", [1, 1024], F32, kind="ExternalInput").ap()
    d_neig = nc.dram_tensor("neighmask", [1, 1024], F32, kind="ExternalInput").ap()
    d_ones = nc.dram_tensor("ones_row", [1, 128], F32, kind="ExternalInput").ap()

    d_out = nc.dram_tensor("out_s", [TC, D], F32, kind="ExternalOutput").ap()
    dbg = {}
    def mkdbg(name, shape, dtype=F32):
        if name in debug:
            dbg[name] = nc.dram_tensor("dbg_" + name, list(shape), dtype,
                                       kind="ExternalOutput").ap()
        return dbg.get(name)

    dbg_efg = mkdbg("efg", [128, 3 * NH])
    dbg_xs = mkdbg("xs", [128, TC])
    dbg_x1 = mkdbg("x1", [TC, D])
    dbg_scan = mkdbg("scan", [128, 512])
    dbg_st = mkdbg("st", [128, 4 * NT])
    dbg_ffn = mkdbg("ffn", [1, 64])
    dbg_x2 = mkdbg("x2", [1, 256])
    dbg_wq = mkdbg("wq", [I3, D], BF16)
    dbg_xq = mkdbg("xq", [TC, D], BF16)
    dbg_G = mkdbg("G", [128, TC], BF16)
    dbg_h = mkdbg("h", [128, TC], BF16)
    dbg_ui = mkdbg("ui", [128, TC], I16)
    dbg_g2 = mkdbg("g2", [128, TC], I16)
    dbg_sx = mkdbg("sx", [1, 16])

    scr_ubf = nc.dram_tensor("scr_ubf", [TC, D], BF16).ap()
    scr_woz = nc.dram_tensor("scr_woz", [128, D], BF16).ap()
    scr_wb = nc.dram_tensor("scr_wb", [128, D], BF16).ap()
    scr_c = nc.dram_tensor("scr_c", [D, 128], BF16).ap()
    scr_d = nc.dram_tensor("scr_d", [D, D], BF16).ap()
    scr_xst = nc.dram_tensor("scr_xst", [1, 128], F32).ap()
    scr_x128 = nc.dram_tensor("scr_x128", [1, 128], F32).ap()
    scr_red = nc.dram_tensor("scr_red", [1, 128], F32).ap()
    cc1_in = nc.dram_tensor("cc1_in", [1, 1024], F32).ap()
    cc1_out = nc.dram_tensor("cc1_out", [1, 1024], F32).ap()
    cc2_in = nc.dram_tensor("cc2_in", [1, 16], F32).ap()
    cc2_out = nc.dram_tensor("cc2_out", [1, 16], F32).ap()
    cc3_in = nc.dram_tensor("cc3_in", [1, 16], F32).ap()
    cc3_out = nc.dram_tensor("cc3_out", [1, 16], F32).ap()
    cc4_in = nc.dram_tensor("cc4_in", [1, 16], F32).ap()
    cc4_out = nc.dram_tensor("cc4_out", [1, 16], F32).ap()
    cc5_in = nc.dram_tensor("cc5_in", [1, 16], F32).ap()
    cc5_out = nc.dram_tensor("cc5_out", [1, 16], F32).ap()
    scr_x1 = nc.dram_tensor("scr_x1", [TC, D], F32).ap()
    scr_xq = nc.dram_tensor("scr_xq", [TC, D], BF16).ap()
    scr_wqg = nc.dram_tensor("scr_wqg", [I3, D], BF16).ap()
    scr_wqu = nc.dram_tensor("scr_wqu", [I3, D], BF16).ap()
    scr_wqd = nc.dram_tensor("scr_wqd", [D, I3], BF16).ap()
    scr_G = nc.dram_tensor("scr_G", [I3, TC], BF16).ap()
    scr_h = nc.dram_tensor("scr_h", [I3, TC], BF16).ap()
    scr_U = nc.dram_tensor("scr_U", [I3, TC], BF16).ap()
    scr_g2i = nc.dram_tensor("scr_g2i", [I3, TC], I16).ap()
    scr_ui = nc.dram_tensor("scr_ui", [I3, TC], I16).ap()

    st = ExitStack()
    with st:
        def sb(name, shape, dtype=F32):
            return st.enter_context(nc.sbuf_tensor(name, list(shape), dtype))

        st_utb = ExitStack()
        utb = st_utb.enter_context(
            nc.sbuf_tensor("utb", [128, KD * TC], BF16, side="right"))

        bias_om2 = sb("sb_bias_om2", [128, 1])
        bias_ze2 = sb("sb_bias_ze2", [128, 1])
        bias_Bz2 = sb("sb_bias_Bz2", [128, 1])
        bias_By2 = sb("sb_bias_By2", [128, 1])

        onesr = sb("onesr", [1, 128])
        maxstage = sb("maxstage", [1, 16])
        bars = sb("bars", [128, 16])
        scol = sb("scol", [128, 8])
        scol2 = sb("scol2", [128, 8])

        mxcols = sb("mxcols", [128, NT])
        statS = sb("statS", [128, 2 * NT])
        statQ = sb("statQ", [128, 2 * NT])
        stMid = ExitStack()
        def sbM(name, shape, dtype=F32):
            return stMid.enter_context(nc.sbuf_tensor(name, list(shape), dtype))
        wozT = sbM("wozT", [128, KD * 128], BF16)
        wbT = sbM("wbT", [128, KD * 128], BF16)
        cT = sbM("cT", [128, D], BF16)
        dTt = sbM("dTt", [128, KD * D], BF16)
        xs_bf = sbM("xs_bf", [128, TC], BF16)
        slotm = sbM("slotm", [1, 1024])
        neigm = sbM("neigm", [1, 1024])

        ps = [st.enter_context(nc.psum_tensor(f"ps{i}", [128, 512], F32))
              for i in range(8)]

        s_in = Sem(st, nc, "s_in")
        s_gd = Sem(st, nc, "s_gd")
        s_cc = Sem(st, nc, "s_cc")
        s_mm = Sem(st, nc, "s_mm")
        s_ve = Sem(st, nc, "s_ve")
        s_ac = Sem(st, nc, "s_ac")
        s_gp = Sem(st, nc, "s_gp")
        s_out = Sem(st, nc, "s_out")
        s_q = [Sem(st, nc, f"s_q{i}") for i in range(12)]

        marks = {}

        def W(eng, mk):
            if isinstance(mk, list):
                for sem, vv in mk:
                    sem.wait(eng, vv)
            else:
                mk[0].wait(eng, mk[1])

        with nc.Block("load") as blk:
            @blk.vector
            def _(v):
                v.memset(maxstage[:], 0.0)

            @blk.gpsimd
            def _(g):
                s_gd.dinc(g.dma_start(scr_ubf, d_u))
                s_gd.dinc(g.dma_start(scr_woz, d_woz))
                s_gd.dinc(g.dma_start(scr_wb, d_wb))
                s_gd.dinc(g.dma_start(scr_c, d_c))
                s_gd.dinc(g.dma_start(scr_d, d_dm))
                marks["casts"] = s_gd.v

            @blk.sync
            def _(sy):
                s_in.dinc(sy.dma_start(bias_om2[:], d_bom))
                s_in.dinc(sy.dma_start(bias_ze2[:], d_bze))
                s_in.dinc(sy.dma_start(bias_Bz2[:], d_bbz))
                s_in.dinc(sy.dma_start(bias_By2[:], d_bby))
                s_in.dinc(sy.dma_start(slotm[:], d_slot))
                s_in.dinc(sy.dma_start(neigm[:], d_neig))
                s_in.dinc(sy.dma_start(onesr[:], d_ones))
                marks["smalls"] = s_in.v
                s_gd.wait(sy, marks["casts"])
                for k in range(KD):
                    s_in.dinc(sy.dma_start(
                        utb[:, k * TC:(k + 1) * TC],
                        scr_ubf[:, k * 128:(k + 1) * 128], transpose=True))
                for k in range(KD):
                    s_in.dinc(sy.dma_start(
                        wozT[:, k * 128:(k + 1) * 128],
                        scr_woz[:, k * 128:(k + 1) * 128], transpose=True))
                    s_in.dinc(sy.dma_start(
                        wbT[:, k * 128:(k + 1) * 128],
                        scr_wb[:, k * 128:(k + 1) * 128], transpose=True))
                s_in.dinc(sy.dma_start(cT[:], scr_c, transpose=True))
                for k in range(KD):
                    s_in.dinc(sy.dma_start(
                        dTt[:, k * D:(k + 1) * D],
                        scr_d[:, k * 128:(k + 1) * 128], transpose=True))
                marks["loads"] = s_in.v

        _stops = ["load", "proj", "elem", "scan1", "scanx", "apply", "dmm", "p2"]
        def _go(ph):
            return stop_after is None or _stops.index(ph) <= _stops.index(stop_after)
        try:
          with ExitStack() as stA:
              def sbA(name, shape, dtype=F32):
                  return stA.enter_context(nc.sbuf_tensor(name, list(shape), dtype))
              om_hm = sbA("om_hm", [128, NH]); zt_hm = sbA("zt_hm", [128, NH])
              fz_hm = sbA("fz_hm", [128, NH]); fy_hm = sbA("fy_hm", [128, NH])
              tA = sbA("tA", [128, NH]); tB = sbA("tB", [128, NH])
              e_ = om_hm; f_ = zt_hm; g_ = tB

              if not _go("proj"):
                  raise _StopBuild
              with nc.Block("proj") as blk:
                  evac_mark = [0] * 8

                  @blk.tensor
                  def _(t):
                      s_in.wait(t, marks["loads"])
                      for tc8 in range(8):
                          pO = ps[tc8 % 2]
                          pF = ps[2 + tc8 % 2]
                          if tc8 >= 2:
                              s_ac.wait(t, evac_mark[tc8 - 2])
                          for k in range(KD):
                              t.matmul(pO[:], wozT[:, k * 128:(k + 1) * 128],
                                       utb[:, k * TC + tc8 * 512: k * TC + tc8 * 512 + 512],
                                       start=(k == 0), stop=(k == KD - 1))
                          for k in range(KD):
                              mm = t.matmul(pF[:], wbT[:, k * 128:(k + 1) * 128],
                                            utb[:, k * TC + tc8 * 512: k * TC + tc8 * 512 + 512],
                                            start=(k == 0), stop=(k == KD - 1))
                          s_mm.inc(mm)
                          marks[f"proj{tc8}"] = s_mm.v
                          evac_mark[tc8] = s_ac.v + (tc8 + 1) * 1

                  @blk.scalar
                  def _(sc):
                      s_in.wait(sc, marks["smalls"])
                      for tc8 in range(8):
                          pO = ps[tc8 % 2]
                          pF = ps[2 + tc8 % 2]
                          s_mm.wait(sc, marks[f"proj{tc8}"])
                          h = tc8 // 4
                          tl = tc8 * 512 - h * NH
                          sc.add(om_hm[64 * h:64 * h + 64, tl:tl + 512],
                                 pO[0:64, :], bias_om2[64 * h:64 * h + 64, :])
                          sc.add(zt_hm[64 * h:64 * h + 64, tl:tl + 512],
                                 pO[64:128, :], bias_ze2[64 * h:64 * h + 64, :])
                          sc.add(fz_hm[64 * h:64 * h + 64, tl:tl + 512],
                                 pF[0:64, :], bias_Bz2[64 * h:64 * h + 64, :])
                          ins = sc.add(fy_hm[64 * h:64 * h + 64, tl:tl + 512],
                                       pF[64:128, :], bias_By2[64 * h:64 * h + 64, :])
                          s_ac.inc(ins)
                          evac_mark[tc8] = s_ac.v
                      marks["proj_evac"] = s_ac.v

              if not _go("elem"):
                  raise _StopBuild
              with nc.Block("elem") as blk:
                  @blk.scalar
                  def _(sc):
                      s_ac.wait(sc, marks["proj_evac"])
                      sc.activation(tA[:], om_hm[:], AF.Abs)
                      sc.activation(tA[:], tA[:], AF.Exp, scale=-1.0)
                      sc.activation(tA[:], tA[:], AF.Ln, bias=1.0)
                      s_ac.inc(sc.activation(tB[:], om_hm[:], AF.Relu))
                      marks["sp1"] = s_ac.v

                  @blk.vector
                  def _(v):
                      s_ac.wait(v, marks["sp1"])
                      v.tensor_tensor(om_hm[:], tA[:], tB[:], ALU.add)
                      s_ve.inc(v.tensor_scalar(om_hm[:], om_hm[:], 1e-4, 100.0,
                                               ALU.max, ALU.min))
                      marks["omclip"] = s_ve.v

                  @blk.scalar
                  def _(sc):
                      s_ve.wait(sc, marks["omclip"])
                      s_ac.inc(sc.activation(tA[:], om_hm[:], AF.Square))
                      marks["Asq"] = s_ac.v

                  @blk.vector
                  def _(v):
                      s_ac.wait(v, marks["Asq"])
                      s_ve.inc(v.tensor_scalar(tB[:], tA[:], DT * DT, 1.0,
                                               ALU.mult, ALU.add))
                      marks["sden"] = s_ve.v

                  @blk.scalar
                  def _(sc):
                      s_ve.wait(sc, marks["sden"])
                      sc.activation(tB[:], tB[:], AF.Ln)
                      sc.activation(tB[:], tB[:], AF.Exp, scale=-1.0)
                      s_ac.inc(sc.activation(zt_hm[:], zt_hm[:], AF.Sigmoid))
                      marks["sigS"] = s_ac.v

                  @blk.vector
                  def _(v):
                      s_ac.wait(v, marks["sigS"])
                      v.tensor_scalar(zt_hm[:], zt_hm[:], -1.0, 1.0, ALU.mult, ALU.add)
                      v.tensor_tensor(e_[:], zt_hm[:], tB[:], ALU.mult)
                      v.tensor_scalar(f_[:], e_[:], DT, None, ALU.mult)
                      s_ve.inc(v.scalar_tensor_tensor(g_[:], tA[:], -DT, e_[:],
                                                      ALU.mult, ALU.mult))
                      marks["efg"] = s_ve.v

                  if dbg_efg is not None:
                      @blk.sync
                      def _(sy):
                          s_ve.wait(sy, marks["efg"])
                          s_out.dinc(sy.dma_start(dbg_efg[:, 0:NH], e_[:]))
                          s_out.dinc(sy.dma_start(dbg_efg[:, NH:2 * NH], f_[:]))
                          s_out.dinc(sy.dma_start(dbg_efg[:, 2 * NH:], g_[:]))
                          marks["dbg_efg_dma"] = s_out.v

              with ExitStack() as st2:
                  def sb2(name, shape, dtype=F32):
                      return st2.enter_context(nc.sbuf_tensor(name, list(shape), dtype))
                  topall = sb2("topall", [128, L * 192], BF16)
                  botall = sb2("botall", [128, L * 192], BF16)
                  tc0 = sb2("tc0", [128, 192]); tc1 = sb2("tc1", [128, 192])
                  bc0 = sb2("bc0", [128, 192]); bc1 = sb2("bc1", [128, 192])
                  t1s = sb2("t1s", [128, 192]); t2s = sb2("t2s", [128, 192])
                  t3s = sb2("t3s", [128, 192]); t4s = sb2("t4s", [128, 192])
                  spt = sb2("spt", [128, 192]); spb = sb2("spb", [128, 192])
                  spt2 = sb2("spt2", [128, 192]); spb2 = sb2("spb2", [128, 192])
                  scq = sb2("scq", [128, 8])
                  xrow = sb2("xrow", [1, 128])
                  z1 = sb2("z1", [1, 1024]); agx = sb2("agx", [1, 1024])
                  xcol = sb2("xcol", [128, 1])
                  sXz = sb2("sXz", [128, 1]); sXy = sb2("sXy", [128, 1])
                  szp = sb2("szp", [128, 64]); syp = sb2("syp", [128, 64])

                  def q_sl(tq, j):
                      return _ap(tq, j, [[NH, 128], [L, CLO]])

                  def q_bc(tq, j):
                      return _ap(tq, j, [[NH, 128], [0, 3], [L, CLO]])

                  def cview(tt):
                      return tt[:].rearrange("p (a c) -> p a c", a=3)

                  if not _go("scan1"):
                      raise _StopBuild
                  with nc.Block("scan") as blk:
                      @blk.vector
                      def _(v):
                          s_ve.wait(v, marks["efg"])
                          v.tensor_copy(tc0[:, 0:64], q_sl(e_, 0))
                          v.tensor_copy(tc0[:, 64:128], q_sl(g_, 0))
                          v.tensor_copy(tc0[:, 128:192], q_sl(fz_hm, 0))
                          v.tensor_copy(bc0[:, 0:64], q_sl(f_, 0))
                          v.tensor_copy(bc0[:, 64:128], q_sl(e_, 0))
                          v.tensor_copy(bc0[:, 128:192], q_sl(fy_hm, 0))
                          v.tensor_copy(topall[:, 0:192], tc0[:])
                          v.tensor_copy(botall[:, 0:192], bc0[:])
                          cur_t, cur_b = tc0, bc0
                          nxt_t, nxt_b = tc1, bc1
                          for j in range(1, L):
                              v.tensor_tensor(cview(t1s), cview(cur_t), q_bc(e_, j), ALU.mult)
                              v.tensor_tensor(cview(t2s), cview(cur_b), q_bc(g_, j), ALU.mult)
                              v.tensor_tensor(nxt_t[:], t1s[:], t2s[:], ALU.add)
                              v.tensor_tensor(nxt_t[:, 128:192], nxt_t[:, 128:192],
                                              q_sl(fz_hm, j), ALU.add)
                              v.tensor_tensor(cview(t1s), cview(cur_t), q_bc(f_, j), ALU.mult)
                              v.tensor_tensor(cview(t2s), cview(cur_b), q_bc(e_, j), ALU.mult)
                              v.tensor_tensor(nxt_b[:], t1s[:], t2s[:], ALU.add)
                              v.tensor_tensor(nxt_b[:, 128:192], nxt_b[:, 128:192],
                                              q_sl(fy_hm, j), ALU.add)
                              v.tensor_copy(topall[:, j * 192:(j + 1) * 192], nxt_t[:])
                              v.tensor_copy(botall[:, j * 192:(j + 1) * 192], nxt_b[:])
                              cur_t, nxt_t = nxt_t, cur_t
                              cur_b, nxt_b = nxt_b, cur_b
                          v.tensor_copy(spt[:], cur_t[:])
                          v.tensor_copy(spb[:], cur_b[:])
                          a, b2 = (spt, spb), (spt2, spb2)
                          for k in (1, 2, 4, 8, 16, 32):
                              (pt, pb), (qt, qb) = a, b2
                              w = CLO - k
                              bcst = lambda t, c0: t[:, c0 + k:c0 + 64].unsqueeze(1).broadcast_to([128, 3, w])
                              v.tensor_tensor(cview(t1s)[:, :, k:], cview(pt)[:, :, 0:w], bcst(pt, 0), ALU.mult)
                              v.tensor_tensor(cview(t2s)[:, :, k:], cview(pb)[:, :, 0:w], bcst(pt, 64), ALU.mult)
                              v.tensor_tensor(cview(t3s)[:, :, k:], cview(pt)[:, :, 0:w], bcst(pb, 0), ALU.mult)
                              v.tensor_tensor(cview(t4s)[:, :, k:], cview(pb)[:, :, 0:w], bcst(pb, 64), ALU.mult)
                              v.tensor_tensor(cview(qt)[:, :, k:], cview(t1s)[:, :, k:], cview(t2s)[:, :, k:], ALU.add)
                              v.tensor_tensor(cview(qb)[:, :, k:], cview(t3s)[:, :, k:], cview(t4s)[:, :, k:], ALU.add)
                              v.tensor_tensor(qt[:, 128 + k:192], qt[:, 128 + k:192], pt[:, 128 + k:192], ALU.add)
                              v.tensor_tensor(qb[:, 128 + k:192], qb[:, 128 + k:192], pb[:, 128 + k:192], ALU.add)
                              v.tensor_copy(cview(qt)[:, :, 0:k], cview(pt)[:, :, 0:k])
                              v.tensor_copy(cview(qb)[:, :, 0:k], cview(pb)[:, :, 0:k])
                              a, b2 = b2, a
                          sp_t, sp_b = a
                          marks["sp_final"] = (sp_t, sp_b)
                          for i, (tt, col) in enumerate(((sp_t, 0), (sp_t, 64), (sp_t, 128),
                                                         (sp_b, 0), (sp_b, 64), (sp_b, 128))):
                              v.tensor_copy(scq[0:64, i:i + 1], tt[0:64, col + 63:col + 64])
                          v.tensor_copy(t1s[:, 2:3], t1s[:, 2:3])
                          v.tensor_copy(t1s[:, 3:4], t1s[:, 3:4])
                          v.tensor_copy(scq[64:128, 0:6], scq[0:64, 0:6])
                          h1 = slice(64, 128)
                          QQ = lambda i: scq[64:128, i:i + 1]
                          v.tensor_scalar(t1s[64:128, 0:64], sp_t[h1, 0:64], QQ(2), None, ALU.mult)
                          v.tensor_scalar(t2s[64:128, 0:64], sp_t[h1, 64:128], QQ(5), None, ALU.mult)
                          v.tensor_scalar(t3s[64:128, 0:64], sp_b[h1, 0:64], QQ(2), None, ALU.mult)
                          v.tensor_scalar(t4s[64:128, 0:64], sp_b[h1, 64:128], QQ(5), None, ALU.mult)
                          v.tensor_tensor(sp_t[h1, 128:192], sp_t[h1, 128:192], t1s[64:128, 0:64], ALU.add)
                          v.tensor_tensor(sp_b[h1, 128:192], sp_b[h1, 128:192], t3s[64:128, 0:64], ALU.add)
                          v.tensor_tensor(sp_t[h1, 128:192], sp_t[h1, 128:192], t2s[64:128, 0:64], ALU.add)
                          v.tensor_tensor(sp_b[h1, 128:192], sp_b[h1, 128:192], t4s[64:128, 0:64], ALU.add)
                          v.tensor_scalar(t1s[64:128, 0:64], sp_t[h1, 0:64], QQ(0), None, ALU.mult)
                          v.tensor_scalar(t1s[64:128, 64:128], sp_t[h1, 64:128], QQ(3), None, ALU.mult)
                          v.tensor_scalar(t2s[64:128, 0:64], sp_t[h1, 0:64], QQ(1), None, ALU.mult)
                          v.tensor_scalar(t2s[64:128, 64:128], sp_t[h1, 64:128], QQ(4), None, ALU.mult)
                          v.tensor_scalar(t3s[64:128, 0:64], sp_b[h1, 0:64], QQ(0), None, ALU.mult)
                          v.tensor_scalar(t3s[64:128, 64:128], sp_b[h1, 64:128], QQ(3), None, ALU.mult)
                          v.tensor_scalar(t4s[64:128, 0:64], sp_b[h1, 0:64], QQ(1), None, ALU.mult)
                          v.tensor_scalar(t4s[64:128, 64:128], sp_b[h1, 64:128], QQ(4), None, ALU.mult)
                          v.tensor_tensor(sp_t[h1, 0:64], t1s[64:128, 0:64], t1s[64:128, 64:128], ALU.add)
                          v.tensor_tensor(sp_b[h1, 0:64], t3s[64:128, 0:64], t3s[64:128, 64:128], ALU.add)
                          v.tensor_tensor(sp_t[h1, 64:128], t2s[64:128, 0:64], t2s[64:128, 64:128], ALU.add)
                          v.tensor_tensor(sp_b[h1, 64:128], t4s[64:128, 0:64], t4s[64:128, 64:128], ALU.add)
                          v.tensor_copy(t1s[:, 0:1], sp_t[:, 191:192])
                          s_ve.inc(v.tensor_copy(t1s[:, 1:2], sp_b[:, 191:192]))
                          marks["scan_sum"] = s_ve.v

                      if _go("scanx"):
                        @blk.sync
                        def _(sy):
                            sp_t, sp_b = marks["sp_final"]
                            s_ve.wait(sy, marks["scan_sum"])
                            s_in.dinc(sy.dma_start(scr_xst[0:1, 0:64], sp_t[64:128, 191:192]))
                            s_in.dinc(sy.dma_start(scr_xst[0:1, 64:128], sp_b[64:128, 191:192]))
                            marks["xst_out"] = s_in.v
                            sy.wait_ge(s_in.h, marks["xst_out"])
                            s_in.dinc(sy.dma_start(xrow[:], scr_xst))
                            marks["xrow_in"] = s_in.v

                        @blk.vector
                        def _(v):
                            s_in.wait(v, marks["xrow_in"])
                            s_ve.inc(v.tensor_tensor(
                                z1[:].rearrange("p (s m) -> p s m", s=8),
                                xrow[:].unsqueeze(1).broadcast_to([1, 8, 128]),
                                slotm[:].rearrange("p (s m) -> p s m", s=8),
                                ALU.mult))
                            marks["z1"] = s_ve.v

                        @blk.gpsimd
                        def _(g):
                            s_ve.wait(g, marks["z1"])
                            s_gd.dinc(g.dma_start(cc1_in, z1[:]))
                            g.wait_ge(s_gd.h, s_gd.v)
                            s_cc.inc(g.collective_compute(
                                "AllReduce", ALU.add,
                                replica_groups=[list(range(8))],
                                ins=[cc1_in.opt()], outs=[cc1_out.opt()]))
                            g.wait_ge(s_cc.h, s_cc.v)
                            s_gd.dinc(g.dma_start(agx[:], cc1_out))
                            marks["agx"] = s_gd.v

                        @blk.vector
                        def _(v):
                            sp_t, sp_b = marks["sp_final"]
                            s_gd.wait(v, marks["agx"])
                            v.tensor_tensor(z1[:], agx[:], neigm[:], ALU.mult)
                            s_ve.inc(v.reduce_sum(
                                xrow[:],
                                _ap(z1, 0, [[1024, 1], [1, 128], [128, 8]]),
                                AX.X))
                            marks["x128"] = s_ve.v

                        @blk.sync
                        def _(sy):
                            s_ve.wait(sy, marks["x128"])
                            s_in.dinc(sy.dma_start(scr_x128, xrow[:]))
                            m = s_in.v
                            sy.wait_ge(s_in.h, m)
                            s_in.dinc(sy.dma_start(xcol[:], scr_x128[0:1, :].rearrange("a p -> p a")))
                            marks["xcol"] = s_in.v


                      if _go("apply"):
                        @blk.vector
                        def _(v):
                            sp_t, sp_b = marks["sp_final"]
                            s_in.wait(v, marks["xcol"])
                            v.tensor_copy(sXz[0:64, :], xcol[0:64, :])
                            v.tensor_copy(sXz[64:128, :], xcol[0:64, :])
                            v.tensor_copy(sXy[0:64, :], xcol[64:128, :])
                            v.tensor_copy(sXy[64:128, :], xcol[64:128, :])
                            v.tensor_scalar(t1s[:, 0:64], sp_t[:, 0:64], sXz[:], None, ALU.mult)
                            v.tensor_scalar(t2s[:, 0:64], sp_t[:, 64:128], sXy[:], None, ALU.mult)
                            v.tensor_scalar(t3s[:, 0:64], sp_b[:, 0:64], sXz[:], None, ALU.mult)
                            v.tensor_scalar(t4s[:, 0:64], sp_b[:, 64:128], sXy[:], None, ALU.mult)
                            v.tensor_tensor(sp_t[:, 128:192], sp_t[:, 128:192], t1s[:, 0:64], ALU.add)
                            v.tensor_tensor(sp_b[:, 128:192], sp_b[:, 128:192], t3s[:, 0:64], ALU.add)
                            v.tensor_tensor(sp_t[:, 128:192], sp_t[:, 128:192], t2s[:, 0:64], ALU.add)
                            v.tensor_tensor(sp_b[:, 128:192], sp_b[:, 128:192], t4s[:, 0:64], ALU.add)
                            v.tensor_copy(szp[:, 1:64], sp_t[:, 128:191])
                            v.tensor_copy(szp[0:64, 0:1], sXz[0:64, :])
                            v.tensor_copy(szp[64:128, 0:1], sp_t[0:64, 191:192])
                            v.tensor_copy(syp[:, 1:64], sp_b[:, 128:191])
                            v.tensor_copy(syp[0:64, 0:1], sXy[0:64, :])
                            s_ve.inc(v.tensor_copy(syp[64:128, 0:1], sp_b[0:64, 191:192]))
                            marks["sprev_done"] = s_ve.v

                        if dbg_scan is not None:
                            @blk.sync
                            def _(sy):
                                sp_t, sp_b = marks["sp_final"]
                                s_ve.wait(sy, marks["sprev_done"])
                                s_out.dinc(sy.dma_start(dbg_scan[:, 0:192], sp_t[:]))
                                s_out.dinc(sy.dma_start(dbg_scan[:, 192:384], sp_b[:]))
                                s_out.dinc(sy.dma_start(dbg_scan[:, 384:448], szp[:]))
                                s_out.dinc(sy.dma_start(dbg_scan[:, 448:512], syp[:]))
                                marks["dbg_scan_dma"] = s_out.v

                        @blk.vector
                        def _(v):
                            sp_t, sp_b = marks["sp_final"]
                            s_ve.wait(v, marks["sprev_done"])
                            if dbg_scan is not None:
                                s_out.wait(v, marks["dbg_scan_dma"])
                            if "dbg_efg_dma" in marks:
                                s_out.wait(v, marks["dbg_efg_dma"])
                            for zy, tall in ((0, topall), (1, botall)):
                                for h in (0, 1):
                                    pbase = 64 * h
                                    p11 = _ap(tall, pbase * (L * 192) + 0 * 64,
                                              [[L * 192, 64], [1, CLO], [192, L]])
                                    p12 = _ap(tall, pbase * (L * 192) + 1 * 64,
                                              [[L * 192, 64], [1, CLO], [192, L]])
                                    vv = _ap(tall, pbase * (L * 192) + 2 * 64,
                                             [[L * 192, 64], [1, CLO], [192, L]])
                                    sz = _ap(szp, pbase * 64, [[64, 64], [1, CLO], [0, L]])
                                    sy2 = _ap(syp, pbase * 64, [[64, 64], [1, CLO], [0, L]])
                                    ta = _ap(e_, pbase * NH, [[NH, 64], [1, CLO], [CLO, L]])
                                    tb = _ap(f_, pbase * NH, [[NH, 64], [1, CLO], [CLO, L]])
                                    out = _ap(xs_bf, zy * 64 * TC + h * NH,
                                              [[TC, 64], [L, CLO], [1, L]])
                                    v.tensor_tensor(ta, p11, sz, ALU.mult)
                                    v.tensor_tensor(tb, p12, sy2, ALU.mult)
                                    v.tensor_tensor(ta, ta, tb, ALU.add)
                                    last_xs = v.tensor_tensor(out, ta, vv, ALU.add)
                            s_ve.inc(last_xs)
                            marks["xs"] = s_ve.v

                        if dbg_xs is not None:
                            @blk.vector
                            def _(v):
                                v.tensor_copy(om_hm[:], xs_bf[:, 0:NH])
                                s_ve.inc(v.tensor_copy(zt_hm[:], xs_bf[:, NH:]))
                                marks["dbgxs"] = s_ve.v

                            @blk.sync
                            def _(sy):
                                s_ve.wait(sy, marks["dbgxs"])
                                s_out.dinc(sy.dma_start(dbg_xs[:, 0:NH], om_hm[:]))
                                s_out.dinc(sy.dma_start(dbg_xs[:, NH:], zt_hm[:]))


          if stage >= 2 and _go("dmm"):
              with ExitStack() as stD:
                  def sbD(name, shape, dtype=F32):
                      return stD.enter_context(nc.sbuf_tensor(name, list(shape), dtype))
                  u_res = sbD("u_res", [128, 4 * D]); r1b = sbD("r1b", [128, 3 * D])
                  x1b = sbD("x1b", [128, 3 * D]); stm = sbD("stm", [128, 3])
                  stq = sbD("stq", [128, 3]); trash = sbD("trash", [128, 512])

                  base_mm, base_ve, base_ac = s_mm.v, s_ve.v, s_ac.v
                  mmk = lambda t: base_mm + t + 1
                  _vorder = []
                  for t in range(NT + 2):
                      if t < NT:
                          _vorder.append(("ev", t))
                      if 1 <= t <= NT:
                          _vorder.append(("st", t - 1))
                      if 2 <= t <= NT + 1:
                          _vorder.append(("x1", t - 2))
                  _vm = {}
                  c = base_ve
                  for it in _vorder:
                      c += 1
                      _vm[it] = c
                  evk = lambda t: _vm[("ev", t)]
                  stk = lambda t: _vm[("st", t)]
                  x1k = lambda t: _vm[("x1", t)]
                  _aorder = []
                  for t in range(NT + 1):
                      if t < NT:
                          _aorder.append(("sq", t))
                      if 1 <= t:
                          _aorder.append(("rs", t - 1))
                  _am = {}
                  c = base_ac
                  for it in _aorder:
                      c += 1
                      _am[it] = c
                  sqk = lambda t: _am[("sq", t)]
                  rsk = lambda t: _am[("rs", t)]
                  _ur_b = [s_q[i].v for i in range(4)]
                  _x1_b = [s_q[4 + i].v for i in range(3)]
                  urk = lambda t: (s_q[t % 4], _ur_b[t % 4] + 16 * (t // 4 + 1))
                  x1dk = lambda t: (s_q[4 + t % 3], _x1_b[t % 3] + 16 * (t // 3 + 1))

                  with nc.Block("dmm") as blk:
                      @blk.sync
                      def _(sy):
                          for t in range(NT + 2):
                              if t < NT:
                                  if t >= 4:
                                      s_ve.wait(sy, evk(t - 4))
                                  s_q[t % 4].dinc(sy.dma_start(
                                      u_res[:, (t % 4) * D:(t % 4 + 1) * D],
                                      d_u[t * 128:(t + 1) * 128, :]))
                                  assert (s_q[t % 4].h, s_q[t % 4].v) == (urk(t)[0].h, urk(t)[1])
                              if t >= 2:
                                  ti = t - 2
                                  s_ve.wait(sy, x1k(ti))
                                  s_q[4 + ti % 3].dinc(sy.dma_start(
                                      scr_x1[ti * 128:(ti + 1) * 128, :],
                                      x1b[:, (ti % 3) * D:(ti % 3 + 1) * D]))
                                  assert s_q[4 + ti % 3].v == x1dk(ti)[1]
                          marks["x1_done"] = [(s_q[4 + i], s_q[4 + i].v) for i in range(3)]

                      @blk.tensor
                      def _(t):
                          s_ve.wait(t, marks["xs"])
                          for ti in range(NT):
                              bp = ti % 3
                              if ti >= 3:
                                  s_ve.wait(t, evk(ti - 3))
                              for f in range(2):
                                  pb = ps[2 * bp + f]
                                  for k in range(KD):
                                      t.matmul(pb[:],
                                               utb[:, k * TC + ti * 128: k * TC + ti * 128 + 128],
                                               dTt[:, k * D + f * 512: k * D + f * 512 + 512],
                                               start=(k == 0), stop=False)
                                  mm = t.matmul(pb[:],
                                                xs_bf[:, ti * 128:(ti + 1) * 128],
                                                cT[:, f * 512:(f + 1) * 512],
                                                start=False, stop=True)
                              s_mm.inc(mm)

                      @blk.vector
                      def _(v):
                          def ev(ti):
                              bp = ti % 3
                              s_mm.wait(v, mmk(ti))
                              W(v, urk(ti))
                              ur = u_res[:, (ti % 4) * D:(ti % 4 + 1) * D]
                              r1 = r1b[:, (ti % 3) * D:(ti % 3 + 1) * D]
                              if ti >= 3:
                                  s_ac.wait(v, sqk(ti - 3))
                              v.scalar_tensor_tensor(r1[:, 0:512], ps[2 * bp][:], 1.0,
                                                     ur[:, 0:512], ALU.mult, ALU.add,
                                                     accum_out=statS[:, 2 * ti:2 * ti + 1])
                              s_ve.inc(v.scalar_tensor_tensor(
                                  r1[:, 512:1024], ps[2 * bp + 1][:], 1.0,
                                  ur[:, 512:1024], ALU.mult, ALU.add,
                                  accum_out=statS[:, 2 * ti + 1:2 * ti + 2]))

                          def stats(ti):
                              s_ac.wait(v, sqk(ti))
                              sm = stm[:, ti % 3:ti % 3 + 1]
                              sq = stq[:, ti % 3:ti % 3 + 1]
                              v.tensor_tensor(sm, statS[:, 2 * ti:2 * ti + 1],
                                              statS[:, 2 * ti + 1:2 * ti + 2], ALU.add)
                              v.tensor_tensor(sq, statQ[:, 2 * ti:2 * ti + 1],
                                              statQ[:, 2 * ti + 1:2 * ti + 2], ALU.add)
                              v.tensor_copy(trash[:, 4:5], trash[:, 4:5])
                              v.tensor_tensor(trash[:, 0:1], sm, sm, ALU.mult)
                              v.tensor_scalar(sq, sq, 1.0 / D, LNEPS, ALU.mult, ALU.add)
                              v.tensor_scalar(sm, sm, 1.0 / D, None, ALU.mult)
                              v.tensor_scalar(trash[:, 0:1], trash[:, 0:1],
                                              1.0 / (D * D), None, ALU.mult)
                              v.tensor_copy(trash[:, 5:6], trash[:, 5:6])
                              s_ve.inc(v.tensor_tensor(sq, sq, trash[:, 0:1], ALU.subtract))

                          def x1f(ti):
                              s_ac.wait(v, rsk(ti))
                              if ti >= 3:
                                  W(v, x1dk(ti - 3))
                              r1 = r1b[:, (ti % 3) * D:(ti % 3 + 1) * D]
                              x1 = x1b[:, (ti % 3) * D:(ti % 3 + 1) * D]
                              v.tensor_scalar(x1, r1, stm[:, ti % 3:ti % 3 + 1],
                                              stq[:, ti % 3:ti % 3 + 1],
                                              ALU.subtract, ALU.mult)
                              s_ve.inc(v.reduce_max(mxcols[:, ti:ti + 1], x1, AX.X,
                                                    apply_absolute_value=True))

                          for t in range(NT + 2):
                              if t < NT:
                                  ev(t)
                              if 1 <= t <= NT:
                                  stats(t - 1)
                              if 2 <= t <= NT + 1:
                                  x1f(t - 2)
                          assert s_ve.v == _vm[_vorder[-1]]

                      @blk.scalar
                      def _(sc):
                          def sq_(ti):
                              s_ve.wait(sc, evk(ti))
                              r1 = r1b[:, (ti % 3) * D:(ti % 3 + 1) * D]
                              sc.activation(trash[:], r1[:, 0:512], AF.Square,
                                            accum_out=statQ[:, 2 * ti:2 * ti + 1])
                              s_ac.inc(sc.activation(trash[:], r1[:, 512:1024], AF.Square,
                                                     accum_out=statQ[:, 2 * ti + 1:2 * ti + 2]))

                          def rs_(ti):
                              s_ve.wait(sc, stk(ti))
                              sq = stq[:, ti % 3:ti % 3 + 1]
                              sc.activation(sq, sq, AF.Ln)
                              sc.copy(trash[:, 6:7], trash[:, 6:7])
                              s_ac.inc(sc.activation(sq, sq, AF.Exp, scale=-0.5))

                          for t in range(NT + 1):
                              if t < NT:
                                  sq_(t)
                              if 1 <= t:
                                  rs_(t - 1)
                          assert s_ac.v == _am[_aorder[-1]]

                  if dbg_x1 is not None:
                      with nc.Block("dbgx1") as blk:
                          @blk.sync
                          def _(sy):
                              W(sy, marks["x1_done"])
                              s_out.dinc(sy.dma_start(dbg_x1, scr_x1))


          stMid.close()
          if stage >= 3:
            wslot = st.enter_context(
                nc.sbuf_tensor("wslot", [128, 24576], BF16))
            gmx = st.enter_context(nc.sbuf_tensor("gmx", [128, 48], F32))
            hmx = st.enter_context(nc.sbuf_tensor("hmx", [128, 48], F32))
            umxc = st.enter_context(nc.sbuf_tensor("umxc", [128, 192], F32))
            pmxc = st.enter_context(nc.sbuf_tensor("pmxc", [128, 48], F32))
            redrow = st.enter_context(nc.sbuf_tensor("redrow", [1, 128], F32))
            redcol = st.enter_context(nc.sbuf_tensor("redcol", [128, 1], F32))
            arst = st.enter_context(nc.sbuf_tensor("arst", [1, 16], F32))
            padt = st.enter_context(nc.sbuf_tensor("padt", [128, 8], F32))

            def vpad(v, n=2):
                for i in range(n):
                    v.tensor_copy(padt[:, i:i + 1], padt[:, i:i + 1])


            def finalize_max(v, sy, v2, partials, ncols, col, tag):
                if v is not None:
                    vpad(v)
                    s_ve.inc(v.reduce_max(redcol[:], partials[:, 0:ncols], AX.X,
                                          apply_absolute_value=True))
                    marks[tag + "_r"] = s_ve.v
                    return
                if sy is not None:
                    s_ve.wait(sy, marks[tag + "_r"])
                    s_in.dinc(sy.dma_start(scr_red[0:1, :].rearrange("a p -> p a"), redcol[:]))
                    sy.wait_ge(s_in.h, s_in.v)
                    s_in.dinc(sy.dma_start(redrow[:], scr_red))
                    marks[tag + "_row"] = s_in.v
                    return
                s_in.wait(v2, marks[tag + "_row"])
                s_ve.inc(v2.reduce_max(maxstage[0:1, col:col + 1], redrow[:], AX.X,
                                       apply_absolute_value=True))
                marks[tag + "_ms"] = s_ve.v

            def barrier_g(g, cin, cout, wait_mark, tag):
                s_ve.wait(g, wait_mark)
                s_gd.dinc(g.dma_start(cin, maxstage[:]))
                g.wait_ge(s_gd.h, s_gd.v)
                s_cc.inc(g.collective_compute(
                    "AllReduce", ALU.max, replica_groups=[list(range(8))],
                    ins=[cin.opt()], outs=[cout.opt()]))
                g.wait_ge(s_cc.h, s_cc.v)
                s_gd.dinc(g.dma_start(arst[:], cout))
                marks[tag + "_ag"] = s_gd.v

            def barrier_t(t, tag):
                s_gd.wait(t, marks[tag + "_ag"])
                s_mm.inc(t.matmul(ps[7][:, 0:16], onesr[:], arst[:], start=True, stop=True))
                marks[tag + "_bc"] = s_mm.v

            def barrier_v(v, tag, cols):
                s_mm.wait(v, marks[tag + "_bc"])
                s_ve.inc(v.tensor_copy(bars[:], ps[7][:, 0:16]))
                marks[tag + "_bars"] = s_ve.v
                vpad(v, 3)
                last = None
                for (src_col, s_dst, i_dst) in cols:
                    v.tensor_scalar(s_dst, bars[:, src_col:src_col + 1], 1.0 / QMAX, 1e-12,
                                    ALU.mult, ALU.max)
                    vpad(v)
                    last = v.reciprocal(i_dst, s_dst)
                s_ve.inc(last)
                marks[tag + "_sc"] = s_ve.v
                vpad(v)

            def dq_tree(v, wi, wo, aw, tm, bm, sbm, inv, q16, I):
                nb = I // 32
                a3 = aw[:, 0:I].rearrange("p (c j) -> p c j", j=32)
                t3 = tm[:, 0:I // 2].rearrange("p (c j) -> p c j", j=16)
                v.tensor_tensor(t3, a3[:, :, 0:16], a3[:, :, 16:32], ALU.max)
                t4 = tm[:, 0:I // 4].rearrange("p (c j) -> p c j", j=8)
                v.tensor_tensor(t4, t3[:, :, 0:8], t3[:, :, 8:16], ALU.max)
                t5 = tm[:, 0:I // 8].rearrange("p (c j) -> p c j", j=4)
                v.tensor_tensor(t5, t4[:, :, 0:4], t4[:, :, 4:8], ALU.max)
                t6 = tm[:, 0:I // 16].rearrange("p (c j) -> p c j", j=2)
                v.tensor_tensor(t6, t5[:, :, 0:2], t5[:, :, 2:4], ALU.max)
                v.tensor_tensor(bm[:, 0:nb], t6[:, :, 0:1].squeeze(2),
                                t6[:, :, 1:2].squeeze(2), ALU.max)
                vpad(v)
                v.tensor_scalar(sbm[:, 0:nb], bm[:, 0:nb], 1.0 / 7.0, 1e-12, ALU.mult, ALU.max)
                vpad(v)
                v.reciprocal(inv[:, 0:nb], sbm[:, 0:nb])
                vpad(v)
                w3 = wi.rearrange("p (c j) -> p c j", j=32)
                q3 = q16[:, 0:I].rearrange("p (c j) -> p c j", j=32)
                v.tensor_tensor(q3, w3, inv[:, 0:nb].unsqueeze(2).broadcast_to([128, nb, 32]),
                                ALU.mult)
                o3 = wo.rearrange("p (c j) -> p c j", j=32)
                return v.tensor_tensor(o3, q16[:, 0:I].rearrange("p (c j) -> p c j", j=32),
                                       sbm[:, 0:nb].unsqueeze(2).broadcast_to([128, nb, 32]), ALU.mult)

            base_mm_dmm = s_mm.v
            with ExitStack() as stq:
                win2 = stq.enter_context(nc.sbuf_tensor("win2", [128, 2 * D], F32))
                aw = stq.enter_context(nc.sbuf_tensor("aw", [128, D], F32))
                tmq = stq.enter_context(nc.sbuf_tensor("tmq", [128, D // 2], F32))
                bmq = stq.enter_context(nc.sbuf_tensor("bmq", [128, 32], F32))
                sbmq = stq.enter_context(nc.sbuf_tensor("sbmq", [128, 32], F32))
                invq = stq.enter_context(nc.sbuf_tensor("invq", [128, 32], F32))
                q16q = stq.enter_context(nc.sbuf_tensor("q16q", [128, D], I16))
                wqbq = stq.enter_context(nc.sbuf_tensor("wqbq", [128, 2 * D], BF16))
                x1in = stq.enter_context(nc.sbuf_tensor("x1in", [128, 3 * D], F32))
                xqi = stq.enter_context(nc.sbuf_tensor("xqi", [128, 2 * D], I16))
                xqb = stq.enter_context(nc.sbuf_tensor("xqb", [128, 2 * D], BF16))

                _w2_b = [s_q[8 + i].v for i in range(2)]
                _wqs_b = [s_q[10 + i].v for i in range(2)]
                dq_l = {ci: (s_q[8 + ci % 2], _w2_b[ci % 2] + 16 * (ci // 2 + 1))
                        for ci in range(OC)}
                dq_s = {ci: (s_q[10 + ci % 2], _wqs_b[ci % 2] + 16 * (ci // 2 + 1))
                        for ci in range(OC)}
                dq_a = {}; dq_v = {}
                c = s_ac.v
                for ci in range(OC):
                    c += 1
                    dq_a[ci] = c
                c = s_ve.v
                for ci in range(OC):
                    c += 1
                    dq_v[ci] = c
                base_dq_ve_end = c

                with nc.Block("wdqg") as blk:
                    @blk.sync
                    def _(sy):
                        for ci in range(OC):
                            if ci >= 2:
                                s_ve.wait(sy, dq_v[ci - 2])
                            s_q[8 + ci % 2].dinc(sy.dma_start(
                                win2[:, (ci % 2) * D:(ci % 2 + 1) * D],
                                d_wg[ci * 128:(ci + 1) * 128, :]))
                            assert s_q[8 + ci % 2].v == dq_l[ci][1]
                            if ci >= 1:
                                s_ve.wait(sy, dq_v[ci - 1])
                                s_q[10 + (ci - 1) % 2].dinc(sy.dma_start(
                                    scr_wqg[(ci - 1) * 128:ci * 128, :],
                                    wqbq[:, ((ci - 1) % 2) * D:((ci - 1) % 2 + 1) * D]))
                                assert s_q[10 + (ci - 1) % 2].v == dq_s[ci - 1][1]
                        s_ve.wait(sy, dq_v[OC - 1])
                        s_q[10 + (OC - 1) % 2].dinc(sy.dma_start(
                            scr_wqg[(OC - 1) * 128:OC * 128, :],
                            wqbq[:, ((OC - 1) % 2) * D:((OC - 1) % 2 + 1) * D]))
                        assert s_q[10 + (OC - 1) % 2].v == dq_s[OC - 1][1]
                        marks["wqg_done"] = [(s_q[10], s_q[10].v), (s_q[11], s_q[11].v)]

                    @blk.scalar
                    def _(sc):
                        for ci in range(OC):
                            W(sc, dq_l[ci])
                            if ci >= 1:
                                s_ve.wait(sc, dq_v[ci - 1])
                            s_ac.inc(sc.activation(aw[:], win2[:, (ci % 2) * D:(ci % 2 + 1) * D],
                                                   AF.Abs))
                            assert s_ac.v == dq_a[ci]

                    @blk.vector
                    def _(v):
                        for ci in range(OC):
                            s_ac.wait(v, dq_a[ci])
                            if ci >= 2:
                                W(v, dq_s[ci - 2])
                            wi = win2[:, (ci % 2) * D:(ci % 2 + 1) * D]
                            wo = wqbq[:, (ci % 2) * D:(ci % 2 + 1) * D]
                            s_ve.inc(dq_tree(v, wi, wo, aw, tmq, bmq, sbmq, invq, q16q, D))
                            assert s_ve.v == dq_v[ci]

                base_xv = s_ve.v
                xq_v = {ti: base_xv + 4 + ti + 1 for ti in range(NT)}
                base_xa = s_ac.v
                xq_a = {ti: base_xa + ti + 1 for ti in range(NT)}
                _xl_b = [s_q[i].v for i in range(3)]
                _xs_b = [s_q[3].v, s_q[7].v]
                xq_l = {t: (s_q[t % 3], _xl_b[t % 3] + 16 * (t // 3 + 1))
                        for t in range(NT)}
                xq_s = {t: (s_q[3 + 4 * (t % 2)], _xs_b[t % 2] + 16 * (t // 2 + 1))
                        for t in range(NT)}

                with nc.Block("xq") as blk:
                    @blk.vector
                    def _(v):
                        finalize_max(v, None, None, mxcols, NT, 0, "xmx")

                    @blk.sync
                    def _(sy):
                        finalize_max(None, sy, None, mxcols, NT, 0, "xmx")

                    @blk.vector
                    def _(v):
                        finalize_max(None, None, v, mxcols, NT, 0, "xmx")

                    @blk.gpsimd
                    def _(g):
                        barrier_g(g, cc2_in, cc2_out, marks["xmx_ms"], "x")

                    @blk.tensor
                    def _(t):
                        barrier_t(t, "x")

                    @blk.vector
                    def _(v):
                        barrier_v(v, "x", [(0, scol[:, 0:1], scol[:, 1:2])])
                        assert s_ve.v == base_xv + 4
                        for ti in range(NT):
                            W(v, xq_l[ti])
                            if ti >= 2:
                                s_ac.wait(v, xq_a[ti - 2])
                            x1t = x1in[:, (ti % 3) * D:(ti % 3 + 1) * D]
                            s_ve.inc(v.tensor_scalar(
                                xqi[:, (ti % 2) * D:(ti % 2 + 1) * D],
                                x1t, scol[:, 1:2], None, ALU.mult))
                            assert s_ve.v == xq_v[ti]

                    if dbg_x2 is not None:
                        @blk.sync
                        def _(sy):
                            s_ve.wait(sy, marks["x_sc"])
                            s_out.dinc(sy.dma_start(dbg_x2[0:1, 0:16], arst[:]))
                            s_out.dinc(sy.dma_start(dbg_x2[0:1, 16:32], bars[0:1, :]))
                            s_out.dinc(sy.dma_start(dbg_x2[0:1, 32:40], scol[0:1, :]))
                            s_out.dinc(sy.dma_start(dbg_x2[0:1, 40:56], maxstage[:]))
                            s_out.dinc(sy.dma_start(dbg_x2[0:1, 64:192], redrow[:]))

                        @blk.gpsimd
                        def _(g):
                            s_ve.wait(g, marks["x_sc"])
                            s_gd.dinc(g.dma_start(dbg_x2[0:1, 56:64].rearrange("a b -> a b"), cc2_out[0:1, 0:8]))

                    @blk.scalar
                    def _(sc):
                        W(sc, marks["wqg_done"])
                        for k in range(KD):
                            s_q[8].dinc(sc.dma_start(
                                wslot[:, k * I3:(k + 1) * I3],
                                scr_wqg[:, k * 128:(k + 1) * 128],
                                transpose=True))
                        marks["wgT_done"] = [(s_q[8], s_q[8].v)]
                        for ti in range(NT):
                            s_ve.wait(sc, xq_v[ti])
                            if ti >= 2:
                                W(sc, xq_s[ti - 2])
                            s_ac.inc(sc.activation(
                                xqb[:, (ti % 2) * D:(ti % 2 + 1) * D],
                                xqi[:, (ti % 2) * D:(ti % 2 + 1) * D],
                                AF.Copy, bias=0.0, scale=scol[:, 0:1]))
                            assert s_ac.v == xq_a[ti]

                    @blk.sync
                    def _(sy):
                        W(sy, marks["x1_done"])
                        for t in range(NT + 2):
                            if t < NT:
                                if t >= 3:
                                    s_ve.wait(sy, xq_v[t - 3])
                                s_q[t % 3].dinc(sy.dma_start(
                                    x1in[:, (t % 3) * D:(t % 3 + 1) * D],
                                    scr_x1[t * 128:(t + 1) * 128, :]))
                                assert s_q[t % 3].v == xq_l[t][1]
                            if t >= 2:
                                tj = t - 2
                                s_ac.wait(sy, xq_a[tj])
                                s_q[3 + 4 * (tj % 2)].dinc(sy.dma_start(
                                    scr_xq[tj * 128:(tj + 1) * 128, :],
                                    xqb[:, (tj % 2) * D:(tj % 2 + 1) * D]))
                                assert s_q[3 + 4 * (tj % 2)].v == xq_s[tj][1]
                        marks["xq_done"] = [(s_q[3], s_q[3].v), (s_q[7], s_q[7].v)]
                        W(sy, marks["xq_done"])
                        s_mm.wait(sy, base_mm_dmm)
                        for k in range(KD):
                            s_in.dinc(sy.dma_start(utb[:, k * TC:(k + 1) * TC],
                                                   scr_xq[:, k * 128:(k + 1) * 128],
                                                   transpose=True))
                        marks["xqT_done"] = s_in.v

            with ExitStack() as stp1:
                gstage = stp1.enter_context(nc.sbuf_tensor("gstage", [128, 2 * NH], BF16))
                win2b = stp1.enter_context(nc.sbuf_tensor("win2b", [128, 2 * I3], F32))
                awb = stp1.enter_context(nc.sbuf_tensor("awb", [128, I3], F32))
                tmb = stp1.enter_context(nc.sbuf_tensor("tmb", [128, I3 // 2], F32))
                bmb = stp1.enter_context(nc.sbuf_tensor("bmb", [128, 96], F32))
                sbmb = stp1.enter_context(nc.sbuf_tensor("sbmb", [128, 96], F32))
                invb = stp1.enter_context(nc.sbuf_tensor("invb", [128, 96], F32))
                q16b = stp1.enter_context(nc.sbuf_tensor("q16b", [128, I3], I16))
                wqbb = stp1.enter_context(nc.sbuf_tensor("wqbb", [128, 2 * I3], BF16))

                DQ2 = [("u", ci, D) for ci in range(OC)] + [("d", ci, I3) for ci in range(8)]
                p1_mm = {}
                c = s_mm.v
                for idx in range(OC * 8):
                    c += 1
                    p1_mm[idx] = c
                p1_ac = {}
                c = s_ac.v
                for idx in range(OC * 8):
                    c += 1
                    p1_ac[idx] = c
                chunks_for_oc = [[] for _ in range(OC)]
                for i2, item in enumerate(DQ2):
                    chunks_for_oc[min(OC - 1, (i2 * OC) // len(DQ2))].append((i2, item))
                dq2_a = {}; dq2_v = {}; dq2_l = {}; dq2_s = {}
                c = s_ac.v + OC * 8
                for oc in range(OC):
                    for (i2, item) in chunks_for_oc[oc]:
                        pass
                p1_ac = {}; dq2_a = {}
                c = s_ac.v
                for oc in range(OC):
                    for t8 in range(8):
                        c += 1
                        p1_ac[oc * 8 + t8] = c
                    for (i2, item) in chunks_for_oc[oc]:
                        c += 1
                        dq2_a[i2] = c
                p1_ve = {}; dq2_v = {}
                c = s_ve.v
                for oc in range(OC):
                    for hf in range(2):
                        c += 1
                        p1_ve[oc * 2 + hf] = c
                    for (i2, item) in chunks_for_oc[oc]:
                        c += 1
                        dq2_v[i2] = c
                base_p1_ve_end = c
                _gs_b = [s_q[8 + i].v for i in range(2)]
                _dl_b = [s_q[10 + i].v for i in range(2)]
                _ds_b = [s_q[i].v for i in range(2)]
                p1_st = {hc: (s_q[8 + hc % 2], _gs_b[hc % 2] + 16 * (hc // 2 + 1))
                         for hc in range(2 * OC)}
                dq2_l = {i2: (s_q[10 + i2 % 2], _dl_b[i2 % 2] + 16 * (i2 // 2 + 1))
                         for i2 in range(len(DQ2))}
                dq2_s = {i2: (s_q[i2 % 2], _ds_b[i2 % 2] + 16 * (i2 // 2 + 1))
                         for i2 in range(len(DQ2))}

                def dq2_src(i2):
                    kind, ci, I = DQ2[i2]
                    if kind == "u":
                        return d_wu[ci * 128:(ci + 1) * 128, :], scr_wqu[ci * 128:(ci + 1) * 128, :], I
                    return d_wd[ci * 128:(ci + 1) * 128, :], scr_wqd[ci * 128:(ci + 1) * 128, :], I

                with nc.Block("p1") as blk:
                    @blk.tensor
                    def _(t):
                        s_in.wait(t, marks["xqT_done"])
                        W(t, marks["wgT_done"])
                        for oc in range(OC):
                            for t8 in range(8):
                                idx = oc * 8 + t8
                                bank = ps[idx % 8]
                                if idx >= 8:
                                    s_ac.wait(t, p1_ac[idx - 8])
                                for k in range(KD):
                                    mm = t.matmul(bank[:],
                                                  wslot[:, k * I3 + oc * 128: k * I3 + oc * 128 + 128],
                                                  utb[:, k * TC + t8 * 512: k * TC + t8 * 512 + 512],
                                                  start=(k == 0), stop=(k == KD - 1))
                                s_mm.inc(mm)
                                assert s_mm.v == p1_mm[idx]

                    @blk.scalar
                    def _(sc):
                        for oc in range(OC):
                            for t8 in range(8):
                                idx = oc * 8 + t8
                                hc = idx // 4
                                bank = ps[idx % 8]
                                s_mm.wait(sc, p1_mm[idx])
                                if t8 % 4 == 0 and hc >= 2:
                                    W(sc, p1_st[hc - 2])
                                    s_ve.wait(sc, p1_ve[hc - 2])
                                s_ac.inc(sc.activation(
                                    gstage[:, (hc % 2) * NH + (t8 % 4) * 512:
                                           (hc % 2) * NH + (t8 % 4) * 512 + 512],
                                    bank[:], AF.Copy, bias=0.0, scale=1.0))
                                assert s_ac.v == p1_ac[idx]
                            for (i2, item) in chunks_for_oc[oc]:
                                dw, dscr, I = dq2_src(i2)
                                W(sc, dq2_l[i2])
                                if i2 >= 1:
                                    s_ve.wait(sc, dq2_v[i2 - 1])
                                s_ac.inc(sc.activation(awb[:, 0:I],
                                                       win2b[:, (i2 % 2) * I3:(i2 % 2) * I3 + I],
                                                       AF.Abs))
                                assert s_ac.v == dq2_a[i2]

                    @blk.vector
                    def _(v):
                        for oc in range(OC):
                            for hf in range(2):
                                hc = oc * 2 + hf
                                s_ac.wait(v, p1_ac[oc * 8 + hf * 4 + 3])
                                s_ve.inc(v.reduce_max(
                                    gmx[:, hc:hc + 1],
                                    gstage[:, (hc % 2) * NH:(hc % 2 + 1) * NH],
                                    AX.X, apply_absolute_value=True))
                                assert s_ve.v == p1_ve[hc]
                            for (i2, item) in chunks_for_oc[oc]:
                                dw, dscr, I = dq2_src(i2)
                                s_ac.wait(v, dq2_a[i2])
                                if i2 >= 2:
                                    W(v, dq2_s[i2 - 2])
                                wi = win2b[:, (i2 % 2) * I3:(i2 % 2) * I3 + I]
                                wo = wqbb[:, (i2 % 2) * I3:(i2 % 2) * I3 + I]
                                s_ve.inc(dq_tree(v, wi, wo, awb, tmb, bmb, sbmb, invb, q16b, I))
                                assert s_ve.v == dq2_v[i2]

                    @blk.sync
                    def _(sy):
                        for oc in range(OC):
                            for hf in range(2):
                                hc = oc * 2 + hf
                                s_ac.wait(sy, p1_ac[oc * 8 + hf * 4 + 3])
                                s_q[8 + hc % 2].dinc(sy.dma_start(
                                    scr_G.rearrange("(a p) t -> a p t", p=128)[oc, :, hf * NH:(hf + 1) * NH],
                                    gstage[:, (hc % 2) * NH:(hc % 2 + 1) * NH]))
                                assert s_q[8 + hc % 2].v == p1_st[hc][1]
                            for (i2, item) in chunks_for_oc[oc]:
                                dw, dscr, I = dq2_src(i2)
                                if i2 >= 2:
                                    s_ve.wait(sy, dq2_v[i2 - 2])
                                s_q[10 + i2 % 2].dinc(sy.dma_start(
                                    win2b[:, (i2 % 2) * I3:(i2 % 2) * I3 + I], dw))
                                assert s_q[10 + i2 % 2].v == dq2_l[i2][1]
                                if i2 >= 1:
                                    s_ve.wait(sy, dq2_v[i2 - 1])
                                    _, dscr_p, I_p = dq2_src(i2 - 1)
                                    s_q[(i2 - 1) % 2].dinc(sy.dma_start(
                                        dscr_p, wqbb[:, ((i2 - 1) % 2) * I3:((i2 - 1) % 2) * I3 + I_p]))
                                    assert s_q[(i2 - 1) % 2].v == dq2_s[i2 - 1][1]
                        i2 = len(DQ2) - 1
                        s_ve.wait(sy, dq2_v[i2])
                        _, dscr_p, I_p = dq2_src(i2)
                        s_q[i2 % 2].dinc(sy.dma_start(
                            dscr_p, wqbb[:, (i2 % 2) * I3:(i2 % 2) * I3 + I_p]))
                        assert s_q[i2 % 2].v == dq2_s[i2][1]
                        marks["wq_ud_done"] = [(s_q[0], s_q[0].v), (s_q[1], s_q[1].v)]

                with nc.Block("p1fin") as blk:
                    @blk.vector
                    def _(v):
                        s_ve.wait(v, base_p1_ve_end)
                        finalize_max(v, None, None, gmx, 2 * OC, 1, "gmx")

                    @blk.sync
                    def _(sy):
                        finalize_max(None, sy, None, gmx, 2 * OC, 1, "gmx")

                    @blk.vector
                    def _(v):
                        finalize_max(None, None, v, gmx, 2 * OC, 1, "gmx")

                    @blk.gpsimd
                    def _(g):
                        barrier_g(g, cc3_in, cc3_out, marks["gmx_ms"], "g")

                    @blk.tensor
                    def _(t):
                        barrier_t(t, "g")

                    @blk.vector
                    def _(v):
                        barrier_v(v, "g", [(1, scol[:, 2:3], scol[:, 3:4])])

            NHC = 2 * OC
            with ExitStack() as stp2:
                gin = stp2.enter_context(nc.sbuf_tensor("gin", [128, 2 * NH], BF16))
                gi16 = stp2.enter_context(nc.sbuf_tensor("gi16", [128, 2 * NH], I16))
                sigf = stp2.enter_context(nc.sbuf_tensor("sigf", [128, NH], F32))
                u16b = stp2.enter_context(nc.sbuf_tensor("u16b", [128, 2 * NH], U16))
                gatef = stp2.enter_context(nc.sbuf_tensor("gatef", [128, 2 * NH], F32))
                hhb = stp2.enter_context(nc.sbuf_tensor("hhb", [128, 2 * NH], F32))
                hb = stp2.enter_context(nc.sbuf_tensor("hb", [128, 2 * NH], BF16))
                ubst = stp2.enter_context(nc.sbuf_tensor("ubst", [128, 2 * TC], BF16))

                p2_mm = {}
                c = s_mm.v
                for idx in range(OC * 8):
                    c += 1
                    p2_mm[idx] = c
                p2_ve = {}; f2_i16 = {}; f2_h = {}; f2_hm = {}
                c = s_ve.v
                for oc in range(OC + 1):
                    if oc < OC:
                        c += 1
                        p2_ve[oc] = c
                    if oc >= 1:
                        for hf in range(2):
                            hc = (oc - 1) * 2 + hf
                            c += 1
                            f2_i16[hc] = c
                            c += 1
                            f2_h[hc] = c
                            c += 1
                            f2_hm[hc] = c
                base_p2_ve_end = c
                f2_ev = {}; f2_sig = {}; f2_u16 = {}; f2_gate = {}
                c = s_ac.v
                for oc in range(OC + 1):
                    if oc < OC:
                        for t8 in range(8):
                            c += 1
                            f2_ev[oc * 8 + t8] = c
                    if oc >= 1:
                        for hf in range(2):
                            hc = (oc - 1) * 2 + hf
                            c += 1
                            f2_sig[hc] = c
                            c += 1
                            f2_u16[hc] = c
                            c += 1
                            f2_gate[hc] = c
                f2_hh = {}
                c = s_gp.v
                for hc in range(NHC):
                    c += 1
                    f2_hh[hc] = c
                base_p2_trs = s_in.v + 8 * 16
                _gl_b = [s_q[2 + i].v for i in range(2)]
                _hs_b = [s_q[4 + i].v for i in range(2)]
                _uus_b = [s_q[6 + i].v for i in range(2)]
                f2_l = {t: (s_q[2 + t % 2], _gl_b[t % 2] + 16 * (t // 2 + 1))
                        for t in range(NHC)}
                f2_s = {t: (s_q[4 + t % 2], _hs_b[t % 2] + 16 * (t // 2 + 1))
                        for t in range(NHC)}
                f2_us = {oc: (s_q[6 + oc % 2], _uus_b[oc % 2] + 16 * (oc // 2 + 1))
                         for oc in range(OC)}

                with nc.Block("p2") as blk:
                    @blk.sync
                    def _(sy):
                        s_mm.wait(sy, p1_mm[OC * 8 - 1])
                        W(sy, marks["wq_ud_done"])
                        for k in range(KD):
                            s_in.dinc(sy.dma_start(wslot[:, k * I3:(k + 1) * I3],
                                                   scr_wqu[:, k * 128:(k + 1) * 128],
                                                   transpose=True))
                        assert s_in.v == base_p2_trs
                        marks["wuT_done"] = s_in.v
                        for t in range(NHC + 2):
                            if t < NHC:
                                if t >= 2:
                                    s_ve.wait(sy, f2_h[t - 2])
                                W(sy, p1_st[t])
                                s_q[2 + t % 2].dinc(sy.dma_start(
                                    gin[:, (t % 2) * NH:(t % 2 + 1) * NH],
                                    scr_G.rearrange("(a p) t -> a p t", p=128)[t // 2, :, (t % 2) * NH:(t % 2 + 1) * NH]))
                                assert s_q[2 + t % 2].v == f2_l[t][1]
                                if t % 2 == 1:
                                    oc = t // 2
                                    s_ac.wait(sy, f2_ev[oc * 8 + 7])
                                    s_q[6 + oc % 2].dinc(sy.dma_start(
                                        scr_U[oc * 128:(oc + 1) * 128, :],
                                        ubst[:, (oc % 2) * TC:(oc % 2 + 1) * TC]))
                                    assert s_q[6 + oc % 2].v == f2_us[oc][1]
                            if t >= 2:
                                tj = t - 2
                                s_ve.wait(sy, f2_hm[tj])
                                s_q[4 + tj % 2].dinc(sy.dma_start(
                                    scr_h.rearrange("(a p) t -> a p t", p=128)[tj // 2, :, (tj % 2) * NH:(tj % 2 + 1) * NH],
                                    hb[:, (tj % 2) * NH:(tj % 2 + 1) * NH]))
                                assert s_q[4 + tj % 2].v == f2_s[tj][1]
                        marks["h_done"] = [(s_q[4], s_q[4].v), (s_q[5], s_q[5].v)]

                    @blk.tensor
                    def _(t):
                        s_in.wait(t, marks["wuT_done"])
                        for oc in range(OC):
                            for t8 in range(8):
                                idx = oc * 8 + t8
                                bank = ps[idx % 8]
                                if idx >= 8:
                                    s_ac.wait(t, f2_ev[idx - 8])
                                for k in range(KD):
                                    mm = t.matmul(bank[:],
                                                  wslot[:, k * I3 + oc * 128: k * I3 + oc * 128 + 128],
                                                  utb[:, k * TC + t8 * 512: k * TC + t8 * 512 + 512],
                                                  start=(k == 0), stop=(k == KD - 1))
                                s_mm.inc(mm)
                                assert s_mm.v == p2_mm[idx]

                    @blk.vector
                    def _(v):
                        s_ve.wait(v, marks["g_sc"])
                        for oc in range(OC + 1):
                            if oc < OC:
                                s_ac.wait(v, f2_ev[oc * 8 + 7])
                                s_ve.inc(v.reduce_max(
                                    umxc[:, oc:oc + 1],
                                    ubst[:, (oc % 2) * TC:(oc % 2 + 1) * TC],
                                    AX.X, apply_absolute_value=True))
                                assert s_ve.v == p2_ve[oc]
                            if oc < 1:
                                continue
                            for hf in range(2):
                                hc = (oc - 1) * 2 + hf
                                W(v, f2_l[hc])
                                if hc >= 2:
                                    s_ac.wait(v, f2_gate[hc - 2])
                                g_t = gin[:, (hc % 2) * NH:(hc % 2 + 1) * NH]
                                i_t = gi16[:, (hc % 2) * NH:(hc % 2 + 1) * NH]
                                s_ve.inc(v.tensor_scalar(i_t, g_t, scol[:, 3:4], None, ALU.mult))
                                assert s_ve.v == f2_i16[hc]
                                s_gp.wait(v, f2_hh[hc])
                                if hc >= 2:
                                    W(v, f2_s[hc - 2])
                                hh_t = hhb[:, (hc % 2) * NH:(hc % 2 + 1) * NH]
                                h_t = hb[:, (hc % 2) * NH:(hc % 2 + 1) * NH]
                                s_ve.inc(v.tensor_scalar(h_t, hh_t, 2.0 ** -16, None, ALU.mult))
                                assert s_ve.v == f2_h[hc]
                                s_ve.inc(v.reduce_max(hmx[:, hc:hc + 1], h_t, AX.X,
                                                      apply_absolute_value=True))
                                assert s_ve.v == f2_hm[hc]

                    @blk.scalar
                    def _(sc):
                        for oc in range(OC + 1):
                            if oc < OC:
                                for t8 in range(8):
                                    idx = oc * 8 + t8
                                    bank = ps[idx % 8]
                                    s_mm.wait(sc, p2_mm[idx])
                                    if t8 == 0 and oc >= 2:
                                        W(sc, f2_us[oc - 2])
                                        s_ve.wait(sc, p2_ve[oc - 2])
                                    s_ac.inc(sc.activation(
                                        ubst[:, (oc % 2) * TC + t8 * 512:
                                             (oc % 2) * TC + t8 * 512 + 512],
                                        bank[:], AF.Copy, bias=0.0, scale=1.0))
                                    assert s_ac.v == f2_ev[idx]
                            if oc < 1:
                                continue
                            for hf in range(2):
                                hc = (oc - 1) * 2 + hf
                                s_ve.wait(sc, f2_i16[hc])
                                i_t = gi16[:, (hc % 2) * NH:(hc % 2 + 1) * NH]
                                s_ac.inc(sc.activation(sigf[:], i_t, AF.Sigmoid,
                                                       bias=0.0, scale=scol[:, 2:3]))
                                assert s_ac.v == f2_sig[hc]
                                if hc >= 2:
                                    s_gp.wait(sc, f2_hh[hc - 2])
                                u_t = u16b[:, (hc % 2) * NH:(hc % 2 + 1) * NH]
                                s_ac.inc(sc.activation(u_t, sigf[:], AF.Copy,
                                                       bias=0.0, scale=65536.0))
                                assert s_ac.v == f2_u16[hc]
                                ga_t = gatef[:, (hc % 2) * NH:(hc % 2 + 1) * NH]
                                s_ac.inc(sc.activation(ga_t, i_t, AF.Copy,
                                                       bias=0.0, scale=scol[:, 2:3]))
                                assert s_ac.v == f2_gate[hc]

                    @blk.gpsimd
                    def _(g):
                        for hc in range(NHC):
                            s_ac.wait(g, f2_gate[hc])
                            if hc >= 2:
                                s_ve.wait(g, f2_h[hc - 2])
                            u_t = u16b[:, (hc % 2) * NH:(hc % 2 + 1) * NH]
                            ga_t = gatef[:, (hc % 2) * NH:(hc % 2 + 1) * NH]
                            hh_t = hhb[:, (hc % 2) * NH:(hc % 2 + 1) * NH]
                            s_gp.inc(g.tensor_tensor(hh_t, ga_t, u_t, ALU.mult))
                            assert s_gp.v == f2_hh[hc]

                with nc.Block("p2fin") as blk:
                    @blk.vector
                    def _(v):
                        s_ve.wait(v, base_p2_ve_end)
                        finalize_max(v, None, None, umxc, OC, 2, "umx")

                    @blk.sync
                    def _(sy):
                        finalize_max(None, sy, None, umxc, OC, 2, "umx")

                    @blk.vector
                    def _(v):
                        finalize_max(None, None, v, umxc, OC, 2, "umx")
                        finalize_max(v, None, None, hmx, NHC, 3, "hmx")

                    @blk.sync
                    def _(sy):
                        finalize_max(None, sy, None, hmx, NHC, 3, "hmx")

                    @blk.vector
                    def _(v):
                        finalize_max(None, None, v, hmx, NHC, 3, "hmx")

                    @blk.gpsimd
                    def _(g):
                        barrier_g(g, cc4_in, cc4_out, marks["hmx_ms"], "uh")

                    @blk.tensor
                    def _(t):
                        barrier_t(t, "uh")

                    @blk.vector
                    def _(v):
                        barrier_v(v, "uh", [(2, scol[:, 4:5], scol[:, 5:6]),
                                            (3, scol[:, 6:7], scol[:, 7:8])])

            if stop_after == "p2":
                raise _StopBuild

            st_utb.close()
            with ExitStack() as stp3:
                uib = stp3.enter_context(nc.sbuf_tensor("uib", [128, 2 * TC], I16))
                Uin = stp3.enter_context(
                    nc.sbuf_tensor("Uin", [128, 2 * TC], BF16, side="right"))
                upb = stp3.enter_context(
                    nc.sbuf_tensor("upb", [128, 3 * NH], F32, side="right"))
                hin = stp3.enter_context(nc.sbuf_tensor("hin", [128, 4 * NH], BF16))
                g2ib = stp3.enter_context(nc.sbuf_tensor("g2ib", [128, 4 * NH], I16))
                gate2b = stp3.enter_context(
                    nc.sbuf_tensor("gate2b", [128, 3 * NH], F32, side="right"))
                pb2 = stp3.enter_context(nc.sbuf_tensor("pb2", [128, 2 * NH], F32))

                p3_ui = {}; f3_g2i = {}; f3_pm = {}
                c = s_ve.v
                for oc in range(OC + 1):
                    if oc < OC:
                        c += 1
                        p3_ui[oc] = c
                    if oc >= 1:
                        for hf in range(2):
                            hc = (oc - 1) * 2 + hf
                            c += 1
                            f3_g2i[hc] = c
                            c += 1
                            f3_pm[hc] = c
                base_p2b_ve_end = c
                f3_up = {}; f3_g2 = {}
                c = s_ac.v
                for hc in range(NHC):
                    c += 1
                    f3_up[hc] = c
                    c += 1
                    f3_g2[hc] = c
                f3_p = {}
                c = s_gp.v
                for hc in range(NHC):
                    c += 1
                    f3_p[hc] = c
                _ul_b = [s_q[i].v for i in range(2)]
                _hl_b = [s_q[2 + i].v for i in range(2)]
                _us_b = [s_q[8 + i].v for i in range(2)]
                _gs2_b = [s_q[10 + i].v for i in range(2)]
                f3_ul = {oc: (s_q[oc % 2], _ul_b[oc % 2] + 16 * (oc // 2 + 1))
                         for oc in range(OC)}
                f3_l = {t: (s_q[2 + t % 2], _hl_b[t % 2] + 16 * (t // 2 + 1))
                        for t in range(NHC)}
                f3_us = {oc: (s_q[8 + oc % 2], _us_b[oc % 2] + 16 * (oc // 2 + 1))
                         for oc in range(OC)}
                f3_gs = {t: (s_q[10 + t % 2], _gs2_b[t % 2] + 16 * (t // 2 + 1))
                         for t in range(NHC)}

                with nc.Block("p2b") as blk:
                    @blk.vector
                    def _(v):
                        s_ve.wait(v, marks["uh_sc"])
                        for oc in range(OC + 1):
                            if oc < OC:
                                W(v, f3_ul[oc])
                                if oc >= 2:
                                    W(v, f3_us[oc - 2])
                                    s_ac.wait(v, f3_up[(oc - 2) * 2 + 1])
                                s_ve.inc(v.tensor_scalar(
                                    uib[:, (oc % 2) * TC:(oc % 2 + 1) * TC],
                                    Uin[:, (oc % 2) * TC:(oc % 2 + 1) * TC],
                                    scol[:, 5:6], None, ALU.mult))
                                assert s_ve.v == p3_ui[oc]
                            if oc < 1:
                                continue
                            for hf in range(2):
                                hc = (oc - 1) * 2 + hf
                                W(v, f3_l[hc])
                                if hc >= 4:
                                    s_ac.wait(v, f3_g2[hc - 4])
                                    W(v, f3_gs[hc - 4])
                                hi_t = hin[:, (hc % 4) * NH:(hc % 4 + 1) * NH]
                                g2_t = g2ib[:, (hc % 4) * NH:(hc % 4 + 1) * NH]
                                s_ve.inc(v.tensor_scalar(g2_t, hi_t, scol[:, 7:8], None, ALU.mult))
                                assert s_ve.v == f3_g2i[hc]
                                s_gp.wait(v, f3_p[hc])
                                p_t = pb2[:, (hc % 2) * NH:(hc % 2 + 1) * NH]
                                s_ve.inc(v.reduce_max(pmxc[:, hc:hc + 1], p_t, AX.X,
                                                      apply_absolute_value=True))
                                assert s_ve.v == f3_pm[hc]

                    @blk.scalar
                    def _(sc):
                        for hc in range(NHC):
                            oc, hf = hc // 2, hc % 2
                            s_ve.wait(sc, p3_ui[oc])
                            if hc >= 3:
                                s_gp.wait(sc, f3_p[hc - 3])
                            up_t = upb[:, (hc % 3) * NH:(hc % 3 + 1) * NH]
                            ui_sl = uib[:, (oc % 2) * TC + hf * NH:(oc % 2) * TC + (hf + 1) * NH]
                            s_ac.inc(sc.activation(up_t, ui_sl, AF.Copy,
                                                   bias=0.0, scale=scol[:, 4:5]))
                            assert s_ac.v == f3_up[hc]
                            s_ve.wait(sc, f3_g2i[hc])
                            g2_t = g2ib[:, (hc % 4) * NH:(hc % 4 + 1) * NH]
                            gf_t = gate2b[:, (hc % 3) * NH:(hc % 3 + 1) * NH]
                            s_ac.inc(sc.activation(gf_t, g2_t, AF.Copy,
                                                   bias=0.0, scale=scol[:, 6:7]))
                            assert s_ac.v == f3_g2[hc]

                    @blk.gpsimd
                    def _(g):
                        for hc in range(NHC):
                            s_ac.wait(g, f3_g2[hc])
                            if hc >= 2:
                                s_ve.wait(g, f3_pm[hc - 2])
                            up_t = upb[:, (hc % 3) * NH:(hc % 3 + 1) * NH]
                            gf_t = gate2b[:, (hc % 3) * NH:(hc % 3 + 1) * NH]
                            p_t = pb2[:, (hc % 2) * NH:(hc % 2 + 1) * NH]
                            s_gp.inc(g.tensor_tensor(p_t, gf_t, up_t, ALU.mult))
                            assert s_gp.v == f3_p[hc]

                    @blk.sync
                    def _(sy):
                        W(sy, marks["h_done"])
                        s_mm.wait(sy, p2_mm[OC * 8 - 1])
                        for t in range(NHC + 2):
                            if t < NHC:
                                if t % 2 == 0:
                                    oc = t // 2
                                    W(sy, f2_us[oc])
                                    if oc >= 2:
                                        s_ve.wait(sy, p3_ui[oc - 2])
                                    s_q[oc % 2].dinc(sy.dma_start(
                                        Uin[:, (oc % 2) * TC:(oc % 2 + 1) * TC],
                                        scr_U[oc * 128:(oc + 1) * 128, :]))
                                    assert s_q[oc % 2].v == f3_ul[oc][1]
                                if t >= 4:
                                    s_ve.wait(sy, f3_g2i[t - 4])
                                s_q[2 + t % 2].dinc(sy.dma_start(
                                    hin[:, (t % 4) * NH:(t % 4 + 1) * NH],
                                    scr_h.rearrange("(a p) t -> a p t", p=128)[t // 2, :, (t % 2) * NH:(t % 2 + 1) * NH]))
                                assert s_q[2 + t % 2].v == f3_l[t][1]
                                if t % 2 == 1:
                                    oc = t // 2
                                    s_ve.wait(sy, p3_ui[oc])
                                    s_q[8 + oc % 2].dinc(sy.dma_start(
                                        scr_ui[oc * 128:(oc + 1) * 128, :],
                                        uib[:, (oc % 2) * TC:(oc % 2 + 1) * TC]))
                                    assert s_q[8 + oc % 2].v == f3_us[oc][1]
                            if t >= 2:
                                tj = t - 2
                                s_ac.wait(sy, f3_g2[tj])
                                s_q[10 + tj % 2].dinc(sy.dma_start(
                                    scr_g2i.rearrange("(a p) t -> a p t", p=128)[tj // 2, :, (tj % 2) * NH:(tj % 2 + 1) * NH],
                                    g2ib[:, (tj % 4) * NH:(tj % 4 + 1) * NH]))
                                assert s_q[10 + tj % 2].v == f3_gs[tj][1]
                        marks["ui_done"] = [(s_q[8], s_q[8].v), (s_q[9], s_q[9].v)]
                        marks["g2i_done"] = [(s_q[10], s_q[10].v), (s_q[11], s_q[11].v)]
                        s_mm.wait(sy, p2_mm[OC * 8 - 1])
                        for k in range(24):
                            s_in.dinc(sy.dma_start(wslot[:, k * 1024:(k + 1) * 1024],
                                                   scr_wqd[:, k * 128:(k + 1) * 128],
                                                   transpose=True))
                        marks["wdT_done"] = s_in.v

                with nc.Block("p2bfin") as blk:
                    @blk.vector
                    def _(v):
                        s_ve.wait(v, base_p2b_ve_end)
                        finalize_max(v, None, None, pmxc, NHC, 4, "pmx")

                    @blk.sync
                    def _(sy):
                        finalize_max(None, sy, None, pmxc, NHC, 4, "pmx")

                    @blk.vector
                    def _(v):
                        finalize_max(None, None, v, pmxc, NHC, 4, "pmx")

                    @blk.gpsimd
                    def _(g):
                        barrier_g(g, cc5_in, cc5_out, marks["pmx_ms"], "p")

                    @blk.tensor
                    def _(t):
                        barrier_t(t, "p")

                    @blk.vector
                    def _(v):
                        barrier_v(v, "p", [(4, scol2[:, 0:1], scol2[:, 1:2])])


          if stage >= 4:
            NG = 16
            GW = 256
            with ExitStack() as stp4:
                g2blk = stp4.enter_context(nc.sbuf_tensor("g2blk", [128, 2 * OC * GW], I16))
                uiblk = stp4.enter_context(nc.sbuf_tensor("uiblk", [128, 2 * OC * GW], I16))
                oblk = stp4.enter_context(nc.sbuf_tensor("oblk", [128, 2 * OC * GW], BF16))
                up4 = stp4.enter_context(nc.sbuf_tensor("up4", [128, 2 * 1024], F32))
                p4 = stp4.enter_context(nc.sbuf_tensor("p4", [128, 2 * 1024], F32))
                k4 = stp4.enter_context(nc.sbuf_tensor("k4", [128, 2 * 1024], I16))
                x1in4 = stp4.enter_context(
                    nc.sbuf_tensor("x1in4", [128, 6 * D], F32, side="right"))
                r2b = stp4.enter_context(nc.sbuf_tensor("r2b", [128, 2 * D], F32))
                outst = stp4.enter_context(nc.sbuf_tensor("outst", [128, 2 * D], F32))
                stm2 = stp4.enter_context(nc.sbuf_tensor("stm2", [128, 2], F32))
                stq2 = stp4.enter_context(nc.sbuf_tensor("stq2", [128, 2], F32))
                trash2 = stp4.enter_context(nc.sbuf_tensor("trash2", [128, 520], F32))

                NOP = OC // 4
                NQ = NG * NOP
                f4_up = {}; f4_o = {}; p4_sq = {}; p4_rs = {}
                _sc_items = []
                for qi in range(NQ + 2):
                    if qi < NQ:
                        _sc_items.append(("up", qi))
                    if qi >= 2:
                        qo = qi - 2
                        _sc_items.append(("o", qo))
                        if qo % NOP == NOP - 1 and qo // NOP >= 2:
                            g_t = qo // NOP - 2
                            for tt in range(2):
                                _sc_items.append(("sq", g_t, tt))
                                _sc_items.append(("rs", g_t, tt))
                            for tt in range(2):
                                _sc_items.append(("os", g_t, tt))
                for g_t in range(NG - 2, NG):
                    for tt in range(2):
                        _sc_items.append(("sq", g_t, tt))
                        _sc_items.append(("rs", g_t, tt))
                    for tt in range(2):
                        _sc_items.append(("os", g_t, tt))
                c = s_ac.v
                for it in _sc_items:
                    if it[0] == "os":
                        continue
                    c += 1
                    if it[0] == "up":
                        f4_up[it[1]] = c
                    elif it[0] == "o":
                        f4_o[it[1]] = c
                    elif it[0] == "sq":
                        p4_sq[(it[1], it[2])] = c
                    else:
                        p4_rs[(it[1], it[2])] = c
                f4_p = {}
                c = s_gp.v
                for qi in range(NQ):
                    c += 1
                    f4_p[qi] = c
                f4_k = {}; p4_ev = {}; p4_st = {}; p4_out = {}
                _ve_items = []
                for g in range(NG + 2):
                    if g < NG:
                        for q in range(NOP):
                            _ve_items.append(("k", g * NOP + q))
                    if g >= 2:
                        for tt in range(2):
                            _ve_items.append(("ev", g - 2, tt))
                            _ve_items.append(("st", g - 2, tt))
                            _ve_items.append(("out", g - 2, tt))
                marks["p_sc2"] = s_ve.v + 1
                c = s_ve.v + 1
                for it in _ve_items:
                    c += 1
                    if it[0] == "k":
                        f4_k[it[1]] = c
                    elif it[0] == "ev":
                        p4_ev[(it[1], it[2])] = c
                    elif it[0] == "st":
                        p4_st[(it[1], it[2])] = c
                    else:
                        p4_out[(it[1], it[2])] = c
                p4_mm = {}
                c = s_mm.v
                for g in range(NG):
                    for tt in range(2):
                        for f in range(2):
                            c += 1
                            p4_mm[(g, tt, f)] = c
                _pg_b = [s_q[i].v for i in range(2)]
                _pu_b = [s_q[2 + i].v for i in range(2)]
                _px_b = [s_q[4 + i].v for i in range(3)]
                _os_b = [s_q[10 + i].v for i in range(2)]
                f4_lg = {g: (s_q[g % 2], _pg_b[g % 2] + 64 * (g // 2 + 1))
                         for g in range(NG)}
                f4_lu = {g: (s_q[2 + g % 2], _pu_b[g % 2] + 64 * (g // 2 + 1))
                         for g in range(NG)}
                p4_x1 = {}; p4_os = {}
                for g in range(NG):
                    for tt in range(2):
                        ti = 2 * g + tt
                        p4_x1[(g, tt)] = (s_q[4 + ti % 3],
                                          _px_b[ti % 3] + 16 * (ti // 3 + 1))
                        p4_os[(g, tt)] = (s_q[10 + ti % 2],
                                          _os_b[ti % 2] + 16 * (ti // 2 + 1))
                marks["out_done"] = [(s_q[10], _os_b[0] + 16 * NG),
                                     (s_q[11], _os_b[1] + 16 * NG)]

                with nc.Block("p3") as blk:
                    @blk.sync
                    def _(sy):
                        s_in.wait(sy, marks["wdT_done"])
                        W(sy, marks["ui_done"])
                        W(sy, marks["g2i_done"])
                        for g in range(NG):
                            if g >= 2:
                                s_gp.wait(sy, f4_p[(g - 2) * NOP + NOP - 1])
                                s_ac.wait(sy, f4_up[(g - 2) * NOP + NOP - 1])
                            for part in range(4):
                                src = _ap(scr_g2i.tensor, (part * 6 * 128) * TC + g * GW,
                                          [[TC, 128], [128 * TC, 6], [1, GW]])
                                dst = _ap(g2blk, (g % 2) * (OC * GW) + part * 6 * GW,
                                          [[2 * OC * GW, 128], [GW, 6], [1, GW]])
                                s_q[g % 2].dinc(sy.dma_start(dst, src))
                            assert s_q[g % 2].v == f4_lg[g][1]
                            for part in range(4):
                                src = _ap(scr_ui.tensor, (part * 6 * 128) * TC + g * GW,
                                          [[TC, 128], [128 * TC, 6], [1, GW]])
                                dst = _ap(uiblk, (g % 2) * (OC * GW) + part * 6 * GW,
                                          [[2 * OC * GW, 128], [GW, 6], [1, GW]])
                                s_q[2 + g % 2].dinc(sy.dma_start(dst, src))
                            assert s_q[2 + g % 2].v == f4_lu[g][1]
                            for tt in range(2):
                                ti = g * 2 + tt
                                if ti >= 6:
                                    s_ve.wait(sy, p4_ev[((ti - 6) // 2, (ti - 6) % 2)])
                                s_q[4 + ti % 3].dinc(sy.dma_start(
                                    x1in4[:, (ti % 6) * D:(ti % 6 + 1) * D],
                                    scr_x1[ti * 128:(ti + 1) * 128, :]))
                                assert s_q[4 + ti % 3].v == p4_x1[(g, tt)][1]

                    @blk.scalar
                    def _(sc):
                        s_ve.wait(sc, marks["p_sc2"])
                        def up_(qi):
                            g, q = qi // NOP, qi % NOP
                            if q == 0:
                                W(sc, f4_lu[g])
                            if qi >= 2:
                                s_gp.wait(sc, f4_p[qi - 2])
                            sl = slice((g % 2) * (OC * GW) + q * 1024,
                                       (g % 2) * (OC * GW) + (q + 1) * 1024)
                            s_ac.inc(sc.activation(
                                up4[:, (qi % 2) * 1024:(qi % 2 + 1) * 1024],
                                uiblk[:, sl], AF.Copy,
                                bias=0.0, scale=scol2[:, 2:3]))
                            assert s_ac.v == f4_up[qi]

                        def o_(qo):
                            g, q = qo // NOP, qo % NOP
                            if q == 0 and g >= 2:
                                s_mm.wait(sc, p4_mm[(g - 2, 1, 1)])
                            sl = slice((g % 2) * (OC * GW) + q * 1024,
                                       (g % 2) * (OC * GW) + (q + 1) * 1024)
                            s_ve.wait(sc, f4_k[qo])
                            s_ac.inc(sc.activation(
                                oblk[:, sl], k4[:, (qo % 2) * 1024:(qo % 2 + 1) * 1024],
                                AF.Copy, bias=0.0, scale=scol2[:, 0:1]))
                            assert s_ac.v == f4_o[qo]

                        def tile_(g, tt, kind):
                            if kind == "sq":
                                s_ve.wait(sc, p4_ev[(g, tt)])
                                r2 = r2b[:, (tt % 2) * D:(tt % 2 + 1) * D]
                                sc.activation(trash2[:, 8:520], r2[:, 0:512], AF.Square,
                                              accum_out=statQ[:, 4 * (g % 8) + 2 * tt:4 * (g % 8) + 2 * tt + 1])
                                s_ac.inc(sc.activation(trash2[:, 8:520], r2[:, 512:1024], AF.Square,
                                                       accum_out=statQ[:, 4 * (g % 8) + 2 * tt + 1:4 * (g % 8) + 2 * tt + 2]))
                                assert s_ac.v == p4_sq[(g, tt)]
                            else:
                                s_ve.wait(sc, p4_st[(g, tt)])
                                sq2 = stq2[:, tt:tt + 1]
                                sc.activation(sq2, sq2, AF.Ln)
                                sc.copy(trash2[:, 1:2], trash2[:, 1:2])
                                s_ac.inc(sc.activation(sq2, sq2, AF.Exp, scale=-0.5))
                                assert s_ac.v == p4_rs[(g, tt)]

                        def os_(g, tt):
                            ti = 2 * g + tt
                            s_ve.wait(sc, p4_out[(g, tt)])
                            s_q[10 + ti % 2].dinc(sc.dma_start(
                                d_out[ti * 128:(ti + 1) * 128, :],
                                outst[:, (ti % 2) * D:(ti % 2 + 1) * D]))
                            assert s_q[10 + ti % 2].v == p4_os[(g, tt)][1]

                        for it in _sc_items:
                            if it[0] == "up":
                                up_(it[1])
                            elif it[0] == "o":
                                o_(it[1])
                            elif it[0] == "os":
                                os_(it[1], it[2])
                            else:
                                tile_(it[1], it[2], it[0])

                    @blk.gpsimd
                    def _(g_):
                        for qi in range(NQ):
                            g, q = qi // NOP, qi % NOP
                            s_ac.wait(g_, f4_up[qi])
                            if q == 0:
                                W(g_, f4_lg[g])
                            if qi >= 2:
                                s_ve.wait(g_, f4_k[qi - 2])
                            sl = slice((g % 2) * (OC * GW) + q * 1024,
                                       (g % 2) * (OC * GW) + (q + 1) * 1024)
                            s_gp.inc(g_.tensor_tensor(
                                p4[:, (qi % 2) * 1024:(qi % 2 + 1) * 1024],
                                g2blk[:, sl],
                                up4[:, (qi % 2) * 1024:(qi % 2 + 1) * 1024], ALU.mult))
                            assert s_gp.v == f4_p[qi]

                    @blk.vector
                    def _(v):
                        s_ve.wait(v, marks["p_sc"])
                        s_ve.inc(v.tensor_tensor(scol2[:, 2:3], scol[:, 4:5],
                                                 scol[:, 6:7], ALU.mult))
                        assert s_ve.v == marks["p_sc2"]
                        vpad(v)

                        def k_(qi):
                            s_gp.wait(v, f4_p[qi])
                            if qi >= 2:
                                s_ac.wait(v, f4_o[qi - 2])
                            s_ve.inc(v.tensor_scalar(
                                k4[:, (qi % 2) * 1024:(qi % 2 + 1) * 1024],
                                p4[:, (qi % 2) * 1024:(qi % 2 + 1) * 1024],
                                scol2[:, 1:2], None, ALU.mult))
                            assert s_ve.v == f4_k[qi]

                        def tile_(g, tt, kind):
                            ti = g * 2 + tt
                            if kind == "ev":
                                s_mm.wait(v, p4_mm[(g, tt, 1)])
                                W(v, p4_x1[(g, tt)])
                                if g >= 1:
                                    s_ac.wait(v, p4_sq[(g - 1, tt)])
                                bank0 = ps[(g % 2) * 4 + tt * 2]
                                bank1 = ps[(g % 2) * 4 + tt * 2 + 1]
                                x1t = x1in4[:, (ti % 6) * D:(ti % 6 + 1) * D]
                                r2 = r2b[:, (tt % 2) * D:(tt % 2 + 1) * D]
                                sS = statS[:, 4 * (g % 8) + 2 * tt:4 * (g % 8) + 2 * tt + 1]
                                sS2 = statS[:, 4 * (g % 8) + 2 * tt + 1:4 * (g % 8) + 2 * tt + 2]
                                v.scalar_tensor_tensor(r2[:, 0:512], bank0[:], 1.0,
                                                       x1t[:, 0:512], ALU.mult, ALU.add,
                                                       accum_out=sS)
                                s_ve.inc(v.scalar_tensor_tensor(
                                    r2[:, 512:1024], bank1[:], 1.0,
                                    x1t[:, 512:1024], ALU.mult, ALU.add, accum_out=sS2))
                                assert s_ve.v == p4_ev[(g, tt)]
                            elif kind == "st":
                                s_ac.wait(v, p4_sq[(g, tt)])
                                sm = stm2[:, tt:tt + 1]
                                sq2 = stq2[:, tt:tt + 1]
                                sS = statS[:, 4 * (g % 8) + 2 * tt:4 * (g % 8) + 2 * tt + 1]
                                sS2 = statS[:, 4 * (g % 8) + 2 * tt + 1:4 * (g % 8) + 2 * tt + 2]
                                qQ = statQ[:, 4 * (g % 8) + 2 * tt:4 * (g % 8) + 2 * tt + 1]
                                qQ2 = statQ[:, 4 * (g % 8) + 2 * tt + 1:4 * (g % 8) + 2 * tt + 2]
                                v.tensor_tensor(sm, sS, sS2, ALU.add)
                                v.tensor_tensor(sq2, qQ, qQ2, ALU.add)
                                v.tensor_copy(trash2[:, 2:3], trash2[:, 2:3])
                                v.tensor_tensor(trash2[:, 3:4], sm, sm, ALU.mult)
                                v.tensor_scalar(sq2, sq2, 1.0 / D, LNEPS, ALU.mult, ALU.add)
                                v.tensor_scalar(sm, sm, 1.0 / D, None, ALU.mult)
                                v.tensor_scalar(trash2[:, 3:4], trash2[:, 3:4],
                                                1.0 / (D * D), None, ALU.mult)
                                v.tensor_copy(trash2[:, 4:5], trash2[:, 4:5])
                                s_ve.inc(v.tensor_tensor(sq2, sq2, trash2[:, 3:4], ALU.subtract))
                                assert s_ve.v == p4_st[(g, tt)]
                            else:
                                s_ac.wait(v, p4_rs[(g, tt)])
                                sm = stm2[:, tt:tt + 1]
                                sq2 = stq2[:, tt:tt + 1]
                                r2 = r2b[:, (tt % 2) * D:(tt % 2 + 1) * D]
                                tj = ti
                                if ti >= 2:
                                    W(v, p4_os[((ti - 2) // 2, (ti - 2) % 2)])
                                s_ve.inc(v.tensor_scalar(
                                    outst[:, (tj % 2) * D:(tj % 2 + 1) * D], r2,
                                    sm, sq2, ALU.subtract, ALU.mult))
                                assert s_ve.v == p4_out[(g, tt)]

                        for it in _ve_items:
                            if it[0] == "k":
                                k_(it[1])
                            else:
                                tile_(it[1], it[2], it[0])

                    @blk.tensor
                    def _(t):
                        for g in range(NG):
                            for tt in range(2):
                                for f in range(2):
                                    bank = ps[(g % 2) * 4 + tt * 2 + f]
                                    if g >= 2:
                                        s_ve.wait(t, p4_ev[(g - 2, tt)])
                                    for oc in range(OC):
                                        q = oc // 4
                                        if oc % 4 == 0:
                                            s_ac.wait(t, f4_o[g * NOP + q])
                                        mm = t.matmul(
                                            bank[:],
                                            oblk[:, (g % 2) * (OC * GW) + oc * GW + tt * 128:
                                                  (g % 2) * (OC * GW) + oc * GW + tt * 128 + 128],
                                            wslot[:, oc * 1024 + f * 512: oc * 1024 + f * 512 + 512],
                                            start=(oc == 0), stop=(oc == OC - 1))
                                    s_mm.inc(mm)
                                    assert s_mm.v == p4_mm[(g, tt, f)]

        except _StopBuild:
            pass



        if stage >= 4 and (dbg_ffn is not None or dbg_wq is not None or dbg_xq is not None or dbg_G is not None):
            with nc.Block("dbgffn") as blk:
                @blk.sync
                def _(sy):
                    W(sy, marks["out_done"])
                    if dbg_ffn is None:
                        return
                    s_out.dinc(sy.dma_start(dbg_ffn[0:1, 0:16], maxstage[:]))
                    s_out.dinc(sy.dma_start(dbg_ffn[0:1, 16:24], scol[0:1, :]))
                    s_out.dinc(sy.dma_start(dbg_ffn[0:1, 24:26], scol2[0:1, 0:2]))
                    s_out.dinc(sy.dma_start(dbg_ffn[0:1, 32:48], bars[0:1, :]))

                @blk.gpsimd
                def _(g):
                    W(g, marks["out_done"])
                    if dbg_wq is not None:
                        s_gd.dinc(g.dma_start(dbg_wq, scr_wqg))
                    if dbg_xq is not None:
                        s_gd.dinc(g.dma_start(dbg_xq, scr_xq))
                    if dbg_G is not None:
                        s_gd.dinc(g.dma_start(dbg_G, scr_G[0:128, :]))
                        s_gd.dinc(g.dma_start(dbg_h, scr_h[0:128, :]))
                        s_gd.dinc(g.dma_start(dbg_ui, scr_ui[0:128, :]))
                        s_gd.dinc(g.dma_start(dbg_g2, scr_g2i[0:128, :]))
                    g.wait_ge(s_gd.h, s_gd.v)

        with nc.Block("fin") as blk:
            @blk.sync
            def _(sy):
                if "out_done" in marks:
                    W(sy, marks["out_done"])
                if s_out.v:
                    s_out.wait(sy)
                s_in.wait(sy)
                for sq_ in s_q:
                    if sq_.v:
                        sq_.wait(sy)

        st.pop_all()
    return nc


def _host_inputs(inputs, core):
    b, h = core // 2, core % 2
    u = inputs["L0_attn_u"]
    woz = np.concatenate([inputs["L0_attn_W_omega"], inputs["L0_attn_W_zeta"]], axis=0)
    bom = inputs["L0_attn_b_omega"]
    bze = inputs["L0_attn_b_zeta"]
    bb = inputs["L0_attn_b_B"]
    slot = np.zeros((1, 1024), np.float32)
    slot[0, core * 128:(core + 1) * 128] = 1.0
    neig = np.zeros((1, 1024), np.float32)
    if core % 2 == 1:
        neig[0, (core - 1) * 128:core * 128] = 1.0
    return {
        "u_s": np.ascontiguousarray(u[b, h * TC:(h + 1) * TC, :]),
        "Woz": np.ascontiguousarray(woz),
        "WB": np.ascontiguousarray(inputs["L0_attn_W_B"]),
        "C_": np.ascontiguousarray(inputs["L0_attn_C"]),
        "D_": np.ascontiguousarray(inputs["L0_attn_D_mat"]),
        "w_gate": np.ascontiguousarray(inputs["L0_ffn_w_gate"]),
        "w_up": np.ascontiguousarray(inputs["L0_ffn_w_up"]),
        "w_down": np.ascontiguousarray(inputs["L0_ffn_w_down"]),
        "bias_om2": np.ascontiguousarray(np.concatenate([bom, bom])[:, None].astype(np.float32)),
        "bias_ze2": np.ascontiguousarray(np.concatenate([bze, bze])[:, None].astype(np.float32)),
        "bias_Bz2": np.ascontiguousarray(np.concatenate([bb[0:64], bb[0:64]])[:, None].astype(np.float32)),
        "bias_By2": np.ascontiguousarray(np.concatenate([bb[64:128], bb[64:128]])[:, None].astype(np.float32)),
        "slotmask": slot,
        "neighmask": neig,
        "ones_row": np.ones((1, 128), np.float32),
    }


def kernel(**inputs):
    key = (STAGE, tuple(sorted(DEBUG)), STOP_AFTER)
    if key not in _BUILT:
        _BUILT[key] = build(STAGE, tuple(DEBUG), STOP_AFTER)
    nc = _BUILT[key]
    in_maps = [_host_inputs(inputs, k) for k in range(8)]
    res = run_bass_kernel_spmd(nc, in_maps, core_ids=list(range(8)))
    out = np.zeros((B, T, D), np.float32)
    for k in range(8):
        b, h = k // 2, k % 2
        if "out_s" in res.results[k]:
            out[b, h * TC:(h + 1) * TC, :] = np.asarray(res.results[k]["out_s"])
    kernel.last_results = res.results
    return out


# version 14
# speedup vs baseline: 1.2307x; 1.0097x over previous
import numpy as np
from contextlib import ExitStack
import concourse.bass as bass
import concourse.mybir as mybir
from concourse.bass_utils import run_bass_kernel_spmd

F32 = mybir.dt.float32
BF16 = mybir.dt.bfloat16
I16 = mybir.dt.int16
U16 = mybir.dt.uint16
ALU = mybir.AluOpType
AF = mybir.ActivationFunctionType
AX = mybir.AxisListType

STAGE = 4
STOP_AFTER = None
DEBUG = []

B = 4; T = 8192; D = 1024; M = 64
TC = 4096
NH = 2048
L = 32
CLO = 64
NT = 32
KD = 8
OC = 24
I3 = 3072
DT = 0.01
QMAX = 32767.0
LNEPS = 1e-5

_BUILT = {}


class _StopBuild(Exception):
    pass


class Sem:
    def __init__(self, stk, nc, name):
        self.h = stk.enter_context(nc.semaphore(name))
        self.v = 0

    def inc(self, instr, n=1):
        instr.then_inc(self.h, n)
        self.v += n
        return instr

    def dinc(self, instr):
        return self.inc(instr, 16)

    def wait(self, eng, val=None):
        eng.wait_ge(self.h, self.v if val is None else val)


def _ap(t, off, dims):
    return bass.AP(t, off, dims)


def build(stage=STAGE, debug=(), stop_after=None):
    nc = bass.Bass("TRN2", target_bir_lowering=False, debug=False)

    d_u = nc.dram_tensor("u_s", [TC, D], F32, kind="ExternalInput").ap()
    d_woz = nc.dram_tensor("Woz", [128, D], F32, kind="ExternalInput").ap()
    d_wb = nc.dram_tensor("WB", [128, D], F32, kind="ExternalInput").ap()
    d_c = nc.dram_tensor("C_", [D, 128], F32, kind="ExternalInput").ap()
    d_dm = nc.dram_tensor("D_", [D, D], F32, kind="ExternalInput").ap()
    d_wg = nc.dram_tensor("w_gate", [I3, D], F32, kind="ExternalInput").ap()
    d_wu = nc.dram_tensor("w_up", [I3, D], F32, kind="ExternalInput").ap()
    d_wd = nc.dram_tensor("w_down", [D, I3], F32, kind="ExternalInput").ap()
    d_bom = nc.dram_tensor("bias_om2", [128, 1], F32, kind="ExternalInput").ap()
    d_bze = nc.dram_tensor("bias_ze2", [128, 1], F32, kind="ExternalInput").ap()
    d_bbz = nc.dram_tensor("bias_Bz2", [128, 1], F32, kind="ExternalInput").ap()
    d_bby = nc.dram_tensor("bias_By2", [128, 1], F32, kind="ExternalInput").ap()
    d_slot = nc.dram_tensor("slotmask", [1, 1024], F32, kind="ExternalInput").ap()
    d_neig = nc.dram_tensor("neighmask", [1, 1024], F32, kind="ExternalInput").ap()
    d_ones = nc.dram_tensor("ones_row", [1, 128], F32, kind="ExternalInput").ap()

    d_out = nc.dram_tensor("out_s", [TC, D], F32, kind="ExternalOutput").ap()
    dbg = {}
    def mkdbg(name, shape, dtype=F32):
        if name in debug:
            dbg[name] = nc.dram_tensor("dbg_" + name, list(shape), dtype,
                                       kind="ExternalOutput").ap()
        return dbg.get(name)

    dbg_efg = mkdbg("efg", [128, 3 * NH])
    dbg_xs = mkdbg("xs", [128, TC])
    dbg_x1 = mkdbg("x1", [TC, D])
    dbg_scan = mkdbg("scan", [128, 512])
    dbg_st = mkdbg("st", [128, 4 * NT])
    dbg_ffn = mkdbg("ffn", [1, 64])
    dbg_x2 = mkdbg("x2", [1, 256])
    dbg_wq = mkdbg("wq", [I3, D], BF16)
    dbg_xq = mkdbg("xq", [TC, D], BF16)
    dbg_G = mkdbg("G", [128, TC], BF16)
    dbg_h = mkdbg("h", [128, TC], BF16)
    dbg_ui = mkdbg("ui", [128, TC], I16)
    dbg_g2 = mkdbg("g2", [128, TC], I16)
    dbg_sx = mkdbg("sx", [1, 16])

    scr_ubf = nc.dram_tensor("scr_ubf", [TC, D], BF16).ap()
    scr_woz = nc.dram_tensor("scr_woz", [128, D], BF16).ap()
    scr_wb = nc.dram_tensor("scr_wb", [128, D], BF16).ap()
    scr_c = nc.dram_tensor("scr_c", [D, 128], BF16).ap()
    scr_d = nc.dram_tensor("scr_d", [D, D], BF16).ap()
    scr_xst = nc.dram_tensor("scr_xst", [1, 128], F32).ap()
    scr_x128 = nc.dram_tensor("scr_x128", [1, 128], F32).ap()
    scr_red = nc.dram_tensor("scr_red", [1, 128], F32).ap()
    cc1_in = nc.dram_tensor("cc1_in", [1, 1024], F32).ap()
    cc1_out = nc.dram_tensor("cc1_out", [1, 1024], F32).ap()
    cc2_in = nc.dram_tensor("cc2_in", [1, 16], F32).ap()
    cc2_out = nc.dram_tensor("cc2_out", [1, 16], F32).ap()
    cc3_in = nc.dram_tensor("cc3_in", [1, 16], F32).ap()
    cc3_out = nc.dram_tensor("cc3_out", [1, 16], F32).ap()
    cc4_in = nc.dram_tensor("cc4_in", [1, 16], F32).ap()
    cc4_out = nc.dram_tensor("cc4_out", [1, 16], F32).ap()
    cc5_in = nc.dram_tensor("cc5_in", [1, 16], F32).ap()
    cc5_out = nc.dram_tensor("cc5_out", [1, 16], F32).ap()
    scr_x1 = nc.dram_tensor("scr_x1", [TC, D], F32).ap()
    scr_xq = nc.dram_tensor("scr_xq", [TC, D], BF16).ap()
    scr_wqg = nc.dram_tensor("scr_wqg", [I3, D], BF16).ap()
    scr_wqu = nc.dram_tensor("scr_wqu", [I3, D], BF16).ap()
    scr_wqd = nc.dram_tensor("scr_wqd", [D, I3], BF16).ap()
    scr_G = nc.dram_tensor("scr_G", [I3, TC], BF16).ap()
    scr_h = nc.dram_tensor("scr_h", [I3, TC], BF16).ap()
    scr_U = nc.dram_tensor("scr_U", [I3, TC], BF16).ap()
    scr_g2i = nc.dram_tensor("scr_g2i", [I3, TC], I16).ap()
    scr_ui = nc.dram_tensor("scr_ui", [I3, TC], I16).ap()

    st = ExitStack()
    with st:
        def sb(name, shape, dtype=F32):
            return st.enter_context(nc.sbuf_tensor(name, list(shape), dtype))

        st_utb = ExitStack()
        utb = st_utb.enter_context(
            nc.sbuf_tensor("utb", [128, KD * TC], BF16, side="right"))

        bias_om2 = sb("sb_bias_om2", [128, 1])
        bias_ze2 = sb("sb_bias_ze2", [128, 1])
        bias_Bz2 = sb("sb_bias_Bz2", [128, 1])
        bias_By2 = sb("sb_bias_By2", [128, 1])

        onesr = sb("onesr", [1, 128])
        maxstage = sb("maxstage", [1, 16])
        bars = sb("bars", [128, 16])
        scol = sb("scol", [128, 8])
        scol2 = sb("scol2", [128, 8])

        mxcols = sb("mxcols", [128, NT])
        statS = sb("statS", [128, 2 * NT])
        statQ = sb("statQ", [128, 2 * NT])
        stMid = ExitStack()
        def sbM(name, shape, dtype=F32):
            return stMid.enter_context(nc.sbuf_tensor(name, list(shape), dtype))
        wozT = sbM("wozT", [128, KD * 128], BF16)
        wbT = sbM("wbT", [128, KD * 128], BF16)
        cT = sbM("cT", [128, D], BF16)
        dTt = sbM("dTt", [128, KD * D], BF16)
        xs_bf = sbM("xs_bf", [128, TC], BF16)
        slotm = sbM("slotm", [1, 1024])
        neigm = sbM("neigm", [1, 1024])

        ps = [st.enter_context(nc.psum_tensor(f"ps{i}", [128, 512], F32))
              for i in range(8)]

        s_in = Sem(st, nc, "s_in")
        s_gd = Sem(st, nc, "s_gd")
        s_cc = Sem(st, nc, "s_cc")
        s_mm = Sem(st, nc, "s_mm")
        s_ve = Sem(st, nc, "s_ve")
        s_ac = Sem(st, nc, "s_ac")
        s_gp = Sem(st, nc, "s_gp")
        s_out = Sem(st, nc, "s_out")
        s_q = [Sem(st, nc, f"s_q{i}") for i in range(12)]

        marks = {}

        def W(eng, mk):
            if isinstance(mk, list):
                for sem, vv in mk:
                    sem.wait(eng, vv)
            else:
                mk[0].wait(eng, mk[1])

        with nc.Block("load") as blk:
            @blk.vector
            def _(v):
                v.memset(maxstage[:], 0.0)

            @blk.gpsimd
            def _(g):
                s_gd.dinc(g.dma_start(scr_ubf, d_u))
                s_gd.dinc(g.dma_start(scr_woz, d_woz))
                s_gd.dinc(g.dma_start(scr_wb, d_wb))
                s_gd.dinc(g.dma_start(scr_c, d_c))
                s_gd.dinc(g.dma_start(scr_d, d_dm))
                marks["casts"] = s_gd.v

            @blk.sync
            def _(sy):
                s_in.dinc(sy.dma_start(bias_om2[:], d_bom))
                s_in.dinc(sy.dma_start(bias_ze2[:], d_bze))
                s_in.dinc(sy.dma_start(bias_Bz2[:], d_bbz))
                s_in.dinc(sy.dma_start(bias_By2[:], d_bby))
                s_in.dinc(sy.dma_start(slotm[:], d_slot))
                s_in.dinc(sy.dma_start(neigm[:], d_neig))
                s_in.dinc(sy.dma_start(onesr[:], d_ones))
                marks["smalls"] = s_in.v
                s_gd.wait(sy, marks["casts"])
                for k in range(KD):
                    s_in.dinc(sy.dma_start(
                        utb[:, k * TC:(k + 1) * TC],
                        scr_ubf[:, k * 128:(k + 1) * 128], transpose=True))
                for k in range(KD):
                    s_in.dinc(sy.dma_start(
                        wozT[:, k * 128:(k + 1) * 128],
                        scr_woz[:, k * 128:(k + 1) * 128], transpose=True))
                    s_in.dinc(sy.dma_start(
                        wbT[:, k * 128:(k + 1) * 128],
                        scr_wb[:, k * 128:(k + 1) * 128], transpose=True))
                s_in.dinc(sy.dma_start(cT[:], scr_c, transpose=True))
                for k in range(KD):
                    s_in.dinc(sy.dma_start(
                        dTt[:, k * D:(k + 1) * D],
                        scr_d[:, k * 128:(k + 1) * 128], transpose=True))
                marks["loads"] = s_in.v

        _stops = ["load", "proj", "elem", "scan1", "scanx", "apply", "dmm", "p2"]
        def _go(ph):
            return stop_after is None or _stops.index(ph) <= _stops.index(stop_after)
        try:
          with ExitStack() as stA:
              def sbA(name, shape, dtype=F32):
                  return stA.enter_context(nc.sbuf_tensor(name, list(shape), dtype))
              om_hm = sbA("om_hm", [128, NH]); zt_hm = sbA("zt_hm", [128, NH])
              fz_hm = sbA("fz_hm", [128, NH]); fy_hm = sbA("fy_hm", [128, NH])
              tA = sbA("tA", [128, NH]); tB = sbA("tB", [128, NH])
              e_ = om_hm; f_ = zt_hm; g_ = tB

              if not _go("proj"):
                  raise _StopBuild
              with nc.Block("proj") as blk:
                  evac_mark = [0] * 8

                  @blk.tensor
                  def _(t):
                      s_in.wait(t, marks["loads"])
                      for tc8 in range(8):
                          pO = ps[tc8 % 2]
                          pF = ps[2 + tc8 % 2]
                          if tc8 >= 2:
                              s_ac.wait(t, evac_mark[tc8 - 2])
                          for k in range(KD):
                              t.matmul(pO[:], wozT[:, k * 128:(k + 1) * 128],
                                       utb[:, k * TC + tc8 * 512: k * TC + tc8 * 512 + 512],
                                       start=(k == 0), stop=(k == KD - 1))
                          for k in range(KD):
                              mm = t.matmul(pF[:], wbT[:, k * 128:(k + 1) * 128],
                                            utb[:, k * TC + tc8 * 512: k * TC + tc8 * 512 + 512],
                                            start=(k == 0), stop=(k == KD - 1))
                          s_mm.inc(mm)
                          marks[f"proj{tc8}"] = s_mm.v
                          evac_mark[tc8] = s_ac.v + (tc8 + 1) * 1

                  @blk.scalar
                  def _(sc):
                      s_in.wait(sc, marks["smalls"])
                      for tc8 in range(8):
                          pO = ps[tc8 % 2]
                          pF = ps[2 + tc8 % 2]
                          s_mm.wait(sc, marks[f"proj{tc8}"])
                          h = tc8 // 4
                          tl = tc8 * 512 - h * NH
                          sc.add(om_hm[64 * h:64 * h + 64, tl:tl + 512],
                                 pO[0:64, :], bias_om2[64 * h:64 * h + 64, :])
                          sc.add(zt_hm[64 * h:64 * h + 64, tl:tl + 512],
                                 pO[64:128, :], bias_ze2[64 * h:64 * h + 64, :])
                          sc.add(fz_hm[64 * h:64 * h + 64, tl:tl + 512],
                                 pF[0:64, :], bias_Bz2[64 * h:64 * h + 64, :])
                          ins = sc.add(fy_hm[64 * h:64 * h + 64, tl:tl + 512],
                                       pF[64:128, :], bias_By2[64 * h:64 * h + 64, :])
                          s_ac.inc(ins)
                          evac_mark[tc8] = s_ac.v
                      marks["proj_evac"] = s_ac.v

              if not _go("elem"):
                  raise _StopBuild
              with nc.Block("elem") as blk:
                  @blk.scalar
                  def _(sc):
                      s_ac.wait(sc, marks["proj_evac"])
                      sc.activation(tA[:], om_hm[:], AF.Abs)
                      sc.activation(tA[:], tA[:], AF.Exp, scale=-1.0)
                      sc.activation(tA[:], tA[:], AF.Ln, bias=1.0)
                      s_ac.inc(sc.activation(tB[:], om_hm[:], AF.Relu))
                      marks["sp1"] = s_ac.v

                  @blk.vector
                  def _(v):
                      s_ac.wait(v, marks["sp1"])
                      v.tensor_tensor(om_hm[:], tA[:], tB[:], ALU.add)
                      s_ve.inc(v.tensor_scalar(om_hm[:], om_hm[:], 1e-4, 100.0,
                                               ALU.max, ALU.min))
                      marks["omclip"] = s_ve.v

                  @blk.scalar
                  def _(sc):
                      s_ve.wait(sc, marks["omclip"])
                      s_ac.inc(sc.activation(tA[:], om_hm[:], AF.Square))
                      marks["Asq"] = s_ac.v

                  @blk.vector
                  def _(v):
                      s_ac.wait(v, marks["Asq"])
                      s_ve.inc(v.tensor_scalar(tB[:], tA[:], DT * DT, 1.0,
                                               ALU.mult, ALU.add))
                      marks["sden"] = s_ve.v

                  @blk.scalar
                  def _(sc):
                      s_ve.wait(sc, marks["sden"])
                      sc.activation(tB[:], tB[:], AF.Ln)
                      sc.activation(tB[:], tB[:], AF.Exp, scale=-1.0)
                      s_ac.inc(sc.activation(zt_hm[:], zt_hm[:], AF.Sigmoid))
                      marks["sigS"] = s_ac.v

                  @blk.vector
                  def _(v):
                      s_ac.wait(v, marks["sigS"])
                      v.tensor_scalar(zt_hm[:], zt_hm[:], -1.0, 1.0, ALU.mult, ALU.add)
                      v.tensor_tensor(e_[:], zt_hm[:], tB[:], ALU.mult)
                      v.tensor_scalar(f_[:], e_[:], DT, None, ALU.mult)
                      s_ve.inc(v.scalar_tensor_tensor(g_[:], tA[:], -DT, e_[:],
                                                      ALU.mult, ALU.mult))
                      marks["efg"] = s_ve.v

                  if dbg_efg is not None:
                      @blk.sync
                      def _(sy):
                          s_ve.wait(sy, marks["efg"])
                          s_out.dinc(sy.dma_start(dbg_efg[:, 0:NH], e_[:]))
                          s_out.dinc(sy.dma_start(dbg_efg[:, NH:2 * NH], f_[:]))
                          s_out.dinc(sy.dma_start(dbg_efg[:, 2 * NH:], g_[:]))
                          marks["dbg_efg_dma"] = s_out.v

              with ExitStack() as st2:
                  def sb2(name, shape, dtype=F32):
                      return st2.enter_context(nc.sbuf_tensor(name, list(shape), dtype))
                  topall = sb2("topall", [128, L * 192], BF16)
                  botall = sb2("botall", [128, L * 192], BF16)
                  tc0 = sb2("tc0", [128, 192]); tc1 = sb2("tc1", [128, 192])
                  bc0 = sb2("bc0", [128, 192]); bc1 = sb2("bc1", [128, 192])
                  t1s = sb2("t1s", [128, 192]); t2s = sb2("t2s", [128, 192])
                  t3s = sb2("t3s", [128, 192]); t4s = sb2("t4s", [128, 192])
                  spt = sb2("spt", [128, 192]); spb = sb2("spb", [128, 192])
                  spt2 = sb2("spt2", [128, 192]); spb2 = sb2("spb2", [128, 192])
                  scq = sb2("scq", [128, 8])
                  xrow = sb2("xrow", [1, 128])
                  z1 = sb2("z1", [1, 1024]); agx = sb2("agx", [1, 1024])
                  xcol = sb2("xcol", [128, 1])
                  sXz = sb2("sXz", [128, 1]); sXy = sb2("sXy", [128, 1])
                  szp = sb2("szp", [128, 64]); syp = sb2("syp", [128, 64])

                  def q_sl(tq, j):
                      return _ap(tq, j, [[NH, 128], [L, CLO]])

                  def q_bc(tq, j):
                      return _ap(tq, j, [[NH, 128], [0, 3], [L, CLO]])

                  def cview(tt):
                      return tt[:].rearrange("p (a c) -> p a c", a=3)

                  if not _go("scan1"):
                      raise _StopBuild
                  with nc.Block("scan") as blk:
                      @blk.vector
                      def _(v):
                          s_ve.wait(v, marks["efg"])
                          v.tensor_copy(tc0[:, 0:64], q_sl(e_, 0))
                          v.tensor_copy(tc0[:, 64:128], q_sl(g_, 0))
                          v.tensor_copy(tc0[:, 128:192], q_sl(fz_hm, 0))
                          v.tensor_copy(bc0[:, 0:64], q_sl(f_, 0))
                          v.tensor_copy(bc0[:, 64:128], q_sl(e_, 0))
                          v.tensor_copy(bc0[:, 128:192], q_sl(fy_hm, 0))
                          v.tensor_copy(topall[:, 0:192], tc0[:])
                          v.tensor_copy(botall[:, 0:192], bc0[:])
                          cur_t, cur_b = tc0, bc0
                          nxt_t, nxt_b = tc1, bc1
                          for j in range(1, L):
                              v.tensor_tensor(cview(t1s), cview(cur_t), q_bc(e_, j), ALU.mult)
                              v.tensor_tensor(cview(t2s), cview(cur_b), q_bc(g_, j), ALU.mult)
                              v.tensor_tensor(nxt_t[:], t1s[:], t2s[:], ALU.add)
                              v.tensor_tensor(nxt_t[:, 128:192], nxt_t[:, 128:192],
                                              q_sl(fz_hm, j), ALU.add)
                              v.tensor_tensor(cview(t1s), cview(cur_t), q_bc(f_, j), ALU.mult)
                              v.tensor_tensor(cview(t2s), cview(cur_b), q_bc(e_, j), ALU.mult)
                              v.tensor_tensor(nxt_b[:], t1s[:], t2s[:], ALU.add)
                              v.tensor_tensor(nxt_b[:, 128:192], nxt_b[:, 128:192],
                                              q_sl(fy_hm, j), ALU.add)
                              v.tensor_copy(topall[:, j * 192:(j + 1) * 192], nxt_t[:])
                              v.tensor_copy(botall[:, j * 192:(j + 1) * 192], nxt_b[:])
                              cur_t, nxt_t = nxt_t, cur_t
                              cur_b, nxt_b = nxt_b, cur_b
                          v.tensor_copy(spt[:], cur_t[:])
                          v.tensor_copy(spb[:], cur_b[:])
                          a, b2 = (spt, spb), (spt2, spb2)
                          for k in (1, 2, 4, 8, 16, 32):
                              (pt, pb), (qt, qb) = a, b2
                              w = CLO - k
                              bcst = lambda t, c0: t[:, c0 + k:c0 + 64].unsqueeze(1).broadcast_to([128, 3, w])
                              v.tensor_tensor(cview(t1s)[:, :, k:], cview(pt)[:, :, 0:w], bcst(pt, 0), ALU.mult)
                              v.tensor_tensor(cview(t2s)[:, :, k:], cview(pb)[:, :, 0:w], bcst(pt, 64), ALU.mult)
                              v.tensor_tensor(cview(t3s)[:, :, k:], cview(pt)[:, :, 0:w], bcst(pb, 0), ALU.mult)
                              v.tensor_tensor(cview(t4s)[:, :, k:], cview(pb)[:, :, 0:w], bcst(pb, 64), ALU.mult)
                              v.tensor_tensor(cview(qt)[:, :, k:], cview(t1s)[:, :, k:], cview(t2s)[:, :, k:], ALU.add)
                              v.tensor_tensor(cview(qb)[:, :, k:], cview(t3s)[:, :, k:], cview(t4s)[:, :, k:], ALU.add)
                              v.tensor_tensor(qt[:, 128 + k:192], qt[:, 128 + k:192], pt[:, 128 + k:192], ALU.add)
                              v.tensor_tensor(qb[:, 128 + k:192], qb[:, 128 + k:192], pb[:, 128 + k:192], ALU.add)
                              v.tensor_copy(cview(qt)[:, :, 0:k], cview(pt)[:, :, 0:k])
                              v.tensor_copy(cview(qb)[:, :, 0:k], cview(pb)[:, :, 0:k])
                              a, b2 = b2, a
                          sp_t, sp_b = a
                          marks["sp_final"] = (sp_t, sp_b)
                          for i, (tt, col) in enumerate(((sp_t, 0), (sp_t, 64), (sp_t, 128),
                                                         (sp_b, 0), (sp_b, 64), (sp_b, 128))):
                              v.tensor_copy(scq[0:64, i:i + 1], tt[0:64, col + 63:col + 64])
                          v.tensor_copy(t1s[:, 2:3], t1s[:, 2:3])
                          v.tensor_copy(t1s[:, 3:4], t1s[:, 3:4])
                          v.tensor_copy(scq[64:128, 0:6], scq[0:64, 0:6])
                          h1 = slice(64, 128)
                          QQ = lambda i: scq[64:128, i:i + 1]
                          v.tensor_scalar(t1s[64:128, 0:64], sp_t[h1, 0:64], QQ(2), None, ALU.mult)
                          v.tensor_scalar(t2s[64:128, 0:64], sp_t[h1, 64:128], QQ(5), None, ALU.mult)
                          v.tensor_scalar(t3s[64:128, 0:64], sp_b[h1, 0:64], QQ(2), None, ALU.mult)
                          v.tensor_scalar(t4s[64:128, 0:64], sp_b[h1, 64:128], QQ(5), None, ALU.mult)
                          v.tensor_tensor(sp_t[h1, 128:192], sp_t[h1, 128:192], t1s[64:128, 0:64], ALU.add)
                          v.tensor_tensor(sp_b[h1, 128:192], sp_b[h1, 128:192], t3s[64:128, 0:64], ALU.add)
                          v.tensor_tensor(sp_t[h1, 128:192], sp_t[h1, 128:192], t2s[64:128, 0:64], ALU.add)
                          v.tensor_tensor(sp_b[h1, 128:192], sp_b[h1, 128:192], t4s[64:128, 0:64], ALU.add)
                          v.tensor_scalar(t1s[64:128, 0:64], sp_t[h1, 0:64], QQ(0), None, ALU.mult)
                          v.tensor_scalar(t1s[64:128, 64:128], sp_t[h1, 64:128], QQ(3), None, ALU.mult)
                          v.tensor_scalar(t2s[64:128, 0:64], sp_t[h1, 0:64], QQ(1), None, ALU.mult)
                          v.tensor_scalar(t2s[64:128, 64:128], sp_t[h1, 64:128], QQ(4), None, ALU.mult)
                          v.tensor_scalar(t3s[64:128, 0:64], sp_b[h1, 0:64], QQ(0), None, ALU.mult)
                          v.tensor_scalar(t3s[64:128, 64:128], sp_b[h1, 64:128], QQ(3), None, ALU.mult)
                          v.tensor_scalar(t4s[64:128, 0:64], sp_b[h1, 0:64], QQ(1), None, ALU.mult)
                          v.tensor_scalar(t4s[64:128, 64:128], sp_b[h1, 64:128], QQ(4), None, ALU.mult)
                          v.tensor_tensor(sp_t[h1, 0:64], t1s[64:128, 0:64], t1s[64:128, 64:128], ALU.add)
                          v.tensor_tensor(sp_b[h1, 0:64], t3s[64:128, 0:64], t3s[64:128, 64:128], ALU.add)
                          v.tensor_tensor(sp_t[h1, 64:128], t2s[64:128, 0:64], t2s[64:128, 64:128], ALU.add)
                          v.tensor_tensor(sp_b[h1, 64:128], t4s[64:128, 0:64], t4s[64:128, 64:128], ALU.add)
                          v.tensor_copy(t1s[:, 0:1], sp_t[:, 191:192])
                          s_ve.inc(v.tensor_copy(t1s[:, 1:2], sp_b[:, 191:192]))
                          marks["scan_sum"] = s_ve.v

                      if _go("scanx"):
                        @blk.sync
                        def _(sy):
                            sp_t, sp_b = marks["sp_final"]
                            s_ve.wait(sy, marks["scan_sum"])
                            s_in.dinc(sy.dma_start(scr_xst[0:1, 0:64], sp_t[64:128, 191:192]))
                            s_in.dinc(sy.dma_start(scr_xst[0:1, 64:128], sp_b[64:128, 191:192]))
                            marks["xst_out"] = s_in.v
                            sy.wait_ge(s_in.h, marks["xst_out"])
                            s_in.dinc(sy.dma_start(xrow[:], scr_xst))
                            marks["xrow_in"] = s_in.v

                        @blk.vector
                        def _(v):
                            s_in.wait(v, marks["xrow_in"])
                            s_ve.inc(v.tensor_tensor(
                                z1[:].rearrange("p (s m) -> p s m", s=8),
                                xrow[:].unsqueeze(1).broadcast_to([1, 8, 128]),
                                slotm[:].rearrange("p (s m) -> p s m", s=8),
                                ALU.mult))
                            marks["z1"] = s_ve.v

                        @blk.gpsimd
                        def _(g):
                            s_ve.wait(g, marks["z1"])
                            s_gd.dinc(g.dma_start(cc1_in, z1[:]))
                            g.wait_ge(s_gd.h, s_gd.v)
                            s_cc.inc(g.collective_compute(
                                "AllReduce", ALU.add,
                                replica_groups=[list(range(8))],
                                ins=[cc1_in.opt()], outs=[cc1_out.opt()]))
                            g.wait_ge(s_cc.h, s_cc.v)
                            s_gd.dinc(g.dma_start(agx[:], cc1_out))
                            marks["agx"] = s_gd.v

                        @blk.vector
                        def _(v):
                            sp_t, sp_b = marks["sp_final"]
                            s_gd.wait(v, marks["agx"])
                            v.tensor_tensor(z1[:], agx[:], neigm[:], ALU.mult)
                            s_ve.inc(v.reduce_sum(
                                xrow[:],
                                _ap(z1, 0, [[1024, 1], [1, 128], [128, 8]]),
                                AX.X))
                            marks["x128"] = s_ve.v

                        @blk.sync
                        def _(sy):
                            s_ve.wait(sy, marks["x128"])
                            s_in.dinc(sy.dma_start(scr_x128, xrow[:]))
                            m = s_in.v
                            sy.wait_ge(s_in.h, m)
                            s_in.dinc(sy.dma_start(xcol[:], scr_x128[0:1, :].rearrange("a p -> p a")))
                            marks["xcol"] = s_in.v


                      if _go("apply"):
                        @blk.vector
                        def _(v):
                            sp_t, sp_b = marks["sp_final"]
                            s_in.wait(v, marks["xcol"])
                            v.tensor_copy(sXz[0:64, :], xcol[0:64, :])
                            v.tensor_copy(sXz[64:128, :], xcol[0:64, :])
                            v.tensor_copy(sXy[0:64, :], xcol[64:128, :])
                            v.tensor_copy(sXy[64:128, :], xcol[64:128, :])
                            v.tensor_scalar(t1s[:, 0:64], sp_t[:, 0:64], sXz[:], None, ALU.mult)
                            v.tensor_scalar(t2s[:, 0:64], sp_t[:, 64:128], sXy[:], None, ALU.mult)
                            v.tensor_scalar(t3s[:, 0:64], sp_b[:, 0:64], sXz[:], None, ALU.mult)
                            v.tensor_scalar(t4s[:, 0:64], sp_b[:, 64:128], sXy[:], None, ALU.mult)
                            v.tensor_tensor(sp_t[:, 128:192], sp_t[:, 128:192], t1s[:, 0:64], ALU.add)
                            v.tensor_tensor(sp_b[:, 128:192], sp_b[:, 128:192], t3s[:, 0:64], ALU.add)
                            v.tensor_tensor(sp_t[:, 128:192], sp_t[:, 128:192], t2s[:, 0:64], ALU.add)
                            v.tensor_tensor(sp_b[:, 128:192], sp_b[:, 128:192], t4s[:, 0:64], ALU.add)
                            v.tensor_copy(szp[:, 1:64], sp_t[:, 128:191])
                            v.tensor_copy(szp[0:64, 0:1], sXz[0:64, :])
                            v.tensor_copy(szp[64:128, 0:1], sp_t[0:64, 191:192])
                            v.tensor_copy(syp[:, 1:64], sp_b[:, 128:191])
                            v.tensor_copy(syp[0:64, 0:1], sXy[0:64, :])
                            s_ve.inc(v.tensor_copy(syp[64:128, 0:1], sp_b[0:64, 191:192]))
                            marks["sprev_done"] = s_ve.v

                        if dbg_scan is not None:
                            @blk.sync
                            def _(sy):
                                sp_t, sp_b = marks["sp_final"]
                                s_ve.wait(sy, marks["sprev_done"])
                                s_out.dinc(sy.dma_start(dbg_scan[:, 0:192], sp_t[:]))
                                s_out.dinc(sy.dma_start(dbg_scan[:, 192:384], sp_b[:]))
                                s_out.dinc(sy.dma_start(dbg_scan[:, 384:448], szp[:]))
                                s_out.dinc(sy.dma_start(dbg_scan[:, 448:512], syp[:]))
                                marks["dbg_scan_dma"] = s_out.v

                        @blk.vector
                        def _(v):
                            sp_t, sp_b = marks["sp_final"]
                            s_ve.wait(v, marks["sprev_done"])
                            if dbg_scan is not None:
                                s_out.wait(v, marks["dbg_scan_dma"])
                            if "dbg_efg_dma" in marks:
                                s_out.wait(v, marks["dbg_efg_dma"])
                            for zy, tall in ((0, topall), (1, botall)):
                                for h in (0, 1):
                                    pbase = 64 * h
                                    p11 = _ap(tall, pbase * (L * 192) + 0 * 64,
                                              [[L * 192, 64], [1, CLO], [192, L]])
                                    p12 = _ap(tall, pbase * (L * 192) + 1 * 64,
                                              [[L * 192, 64], [1, CLO], [192, L]])
                                    vv = _ap(tall, pbase * (L * 192) + 2 * 64,
                                             [[L * 192, 64], [1, CLO], [192, L]])
                                    sz = _ap(szp, pbase * 64, [[64, 64], [1, CLO], [0, L]])
                                    sy2 = _ap(syp, pbase * 64, [[64, 64], [1, CLO], [0, L]])
                                    ta = _ap(e_, pbase * NH, [[NH, 64], [1, CLO], [CLO, L]])
                                    tb = _ap(f_, pbase * NH, [[NH, 64], [1, CLO], [CLO, L]])
                                    out = _ap(xs_bf, zy * 64 * TC + h * NH,
                                              [[TC, 64], [L, CLO], [1, L]])
                                    v.tensor_tensor(ta, p11, sz, ALU.mult)
                                    v.tensor_tensor(tb, p12, sy2, ALU.mult)
                                    v.tensor_tensor(ta, ta, tb, ALU.add)
                                    last_xs = v.tensor_tensor(out, ta, vv, ALU.add)
                            s_ve.inc(last_xs)
                            marks["xs"] = s_ve.v

                        if dbg_xs is not None:
                            @blk.vector
                            def _(v):
                                v.tensor_copy(om_hm[:], xs_bf[:, 0:NH])
                                s_ve.inc(v.tensor_copy(zt_hm[:], xs_bf[:, NH:]))
                                marks["dbgxs"] = s_ve.v

                            @blk.sync
                            def _(sy):
                                s_ve.wait(sy, marks["dbgxs"])
                                s_out.dinc(sy.dma_start(dbg_xs[:, 0:NH], om_hm[:]))
                                s_out.dinc(sy.dma_start(dbg_xs[:, NH:], zt_hm[:]))


          if stage >= 2 and _go("dmm"):
              with ExitStack() as stD:
                  def sbD(name, shape, dtype=F32):
                      return stD.enter_context(nc.sbuf_tensor(name, list(shape), dtype))
                  u_res = sbD("u_res", [128, 4 * D]); r1b = sbD("r1b", [128, 3 * D])
                  x1b = sbD("x1b", [128, 3 * D]); stm = sbD("stm", [128, 3])
                  stq = sbD("stq", [128, 3]); trash = sbD("trash", [128, 512])

                  base_mm, base_ve, base_ac = s_mm.v, s_ve.v, s_ac.v
                  mmk = lambda t: base_mm + t + 1
                  _vorder = []
                  for t in range(NT + 2):
                      if t < NT:
                          _vorder.append(("ev", t))
                      if 1 <= t <= NT:
                          _vorder.append(("st", t - 1))
                      if 2 <= t <= NT + 1:
                          _vorder.append(("x1", t - 2))
                  _vm = {}
                  c = base_ve
                  for it in _vorder:
                      c += 1
                      _vm[it] = c
                  evk = lambda t: _vm[("ev", t)]
                  stk = lambda t: _vm[("st", t)]
                  x1k = lambda t: _vm[("x1", t)]
                  _aorder = []
                  for t in range(NT + 1):
                      if t < NT:
                          _aorder.append(("sq", t))
                      if 1 <= t:
                          _aorder.append(("rs", t - 1))
                  _am = {}
                  c = base_ac
                  for it in _aorder:
                      c += 1
                      _am[it] = c
                  sqk = lambda t: _am[("sq", t)]
                  rsk = lambda t: _am[("rs", t)]
                  _ur_b = [s_q[i].v for i in range(4)]
                  _x1_b = [s_q[4 + i].v for i in range(3)]
                  urk = lambda t: (s_q[t % 4], _ur_b[t % 4] + 16 * (t // 4 + 1))
                  x1dk = lambda t: (s_q[4 + t % 3], _x1_b[t % 3] + 16 * (t // 3 + 1))

                  with nc.Block("dmm") as blk:
                      @blk.sync
                      def _(sy):
                          for t in range(NT + 2):
                              if t < NT:
                                  if t >= 4:
                                      s_ve.wait(sy, evk(t - 4))
                                  s_q[t % 4].dinc(sy.dma_start(
                                      u_res[:, (t % 4) * D:(t % 4 + 1) * D],
                                      d_u[t * 128:(t + 1) * 128, :]))
                                  assert (s_q[t % 4].h, s_q[t % 4].v) == (urk(t)[0].h, urk(t)[1])
                              if t >= 2:
                                  ti = t - 2
                                  s_ve.wait(sy, x1k(ti))
                                  s_q[4 + ti % 3].dinc(sy.dma_start(
                                      scr_x1[ti * 128:(ti + 1) * 128, :],
                                      x1b[:, (ti % 3) * D:(ti % 3 + 1) * D]))
                                  assert s_q[4 + ti % 3].v == x1dk(ti)[1]
                          marks["x1_done"] = [(s_q[4 + i], s_q[4 + i].v) for i in range(3)]

                      @blk.tensor
                      def _(t):
                          s_ve.wait(t, marks["xs"])
                          for ti in range(NT):
                              bp = ti % 3
                              if ti >= 3:
                                  s_ve.wait(t, evk(ti - 3))
                              for f in range(2):
                                  pb = ps[2 * bp + f]
                                  for k in range(KD):
                                      t.matmul(pb[:],
                                               utb[:, k * TC + ti * 128: k * TC + ti * 128 + 128],
                                               dTt[:, k * D + f * 512: k * D + f * 512 + 512],
                                               start=(k == 0), stop=False)
                                  mm = t.matmul(pb[:],
                                                xs_bf[:, ti * 128:(ti + 1) * 128],
                                                cT[:, f * 512:(f + 1) * 512],
                                                start=False, stop=True)
                              s_mm.inc(mm)

                      @blk.vector
                      def _(v):
                          def ev(ti):
                              bp = ti % 3
                              s_mm.wait(v, mmk(ti))
                              W(v, urk(ti))
                              ur = u_res[:, (ti % 4) * D:(ti % 4 + 1) * D]
                              r1 = r1b[:, (ti % 3) * D:(ti % 3 + 1) * D]
                              if ti >= 3:
                                  s_ac.wait(v, sqk(ti - 3))
                              v.scalar_tensor_tensor(r1[:, 0:512], ps[2 * bp][:], 1.0,
                                                     ur[:, 0:512], ALU.mult, ALU.add,
                                                     accum_out=statS[:, 2 * ti:2 * ti + 1])
                              s_ve.inc(v.scalar_tensor_tensor(
                                  r1[:, 512:1024], ps[2 * bp + 1][:], 1.0,
                                  ur[:, 512:1024], ALU.mult, ALU.add,
                                  accum_out=statS[:, 2 * ti + 1:2 * ti + 2]))

                          def stats(ti):
                              s_ac.wait(v, sqk(ti))
                              sm = stm[:, ti % 3:ti % 3 + 1]
                              sq = stq[:, ti % 3:ti % 3 + 1]
                              v.tensor_tensor(sm, statS[:, 2 * ti:2 * ti + 1],
                                              statS[:, 2 * ti + 1:2 * ti + 2], ALU.add)
                              v.tensor_tensor(sq, statQ[:, 2 * ti:2 * ti + 1],
                                              statQ[:, 2 * ti + 1:2 * ti + 2], ALU.add)
                              v.tensor_copy(trash[:, 4:5], trash[:, 4:5])
                              v.tensor_tensor(trash[:, 0:1], sm, sm, ALU.mult)
                              v.tensor_scalar(sq, sq, 1.0 / D, LNEPS, ALU.mult, ALU.add)
                              v.tensor_scalar(sm, sm, 1.0 / D, None, ALU.mult)
                              v.tensor_scalar(trash[:, 0:1], trash[:, 0:1],
                                              1.0 / (D * D), None, ALU.mult)
                              v.tensor_copy(trash[:, 5:6], trash[:, 5:6])
                              s_ve.inc(v.tensor_tensor(sq, sq, trash[:, 0:1], ALU.subtract))

                          def x1f(ti):
                              s_ac.wait(v, rsk(ti))
                              if ti >= 3:
                                  W(v, x1dk(ti - 3))
                              r1 = r1b[:, (ti % 3) * D:(ti % 3 + 1) * D]
                              x1 = x1b[:, (ti % 3) * D:(ti % 3 + 1) * D]
                              v.tensor_scalar(x1, r1, stm[:, ti % 3:ti % 3 + 1],
                                              stq[:, ti % 3:ti % 3 + 1],
                                              ALU.subtract, ALU.mult)
                              s_ve.inc(v.reduce_max(mxcols[:, ti:ti + 1], x1, AX.X,
                                                    apply_absolute_value=True))

                          for t in range(NT + 2):
                              if t < NT:
                                  ev(t)
                              if 1 <= t <= NT:
                                  stats(t - 1)
                              if 2 <= t <= NT + 1:
                                  x1f(t - 2)
                          assert s_ve.v == _vm[_vorder[-1]]

                      @blk.scalar
                      def _(sc):
                          def sq_(ti):
                              s_ve.wait(sc, evk(ti))
                              r1 = r1b[:, (ti % 3) * D:(ti % 3 + 1) * D]
                              sc.activation(trash[:], r1[:, 0:512], AF.Square,
                                            accum_out=statQ[:, 2 * ti:2 * ti + 1])
                              s_ac.inc(sc.activation(trash[:], r1[:, 512:1024], AF.Square,
                                                     accum_out=statQ[:, 2 * ti + 1:2 * ti + 2]))

                          def rs_(ti):
                              s_ve.wait(sc, stk(ti))
                              sq = stq[:, ti % 3:ti % 3 + 1]
                              sc.activation(sq, sq, AF.Ln)
                              sc.copy(trash[:, 6:7], trash[:, 6:7])
                              s_ac.inc(sc.activation(sq, sq, AF.Exp, scale=-0.5))

                          for t in range(NT + 1):
                              if t < NT:
                                  sq_(t)
                              if 1 <= t:
                                  rs_(t - 1)
                          assert s_ac.v == _am[_aorder[-1]]

                  if dbg_x1 is not None:
                      with nc.Block("dbgx1") as blk:
                          @blk.sync
                          def _(sy):
                              W(sy, marks["x1_done"])
                              s_out.dinc(sy.dma_start(dbg_x1, scr_x1))


          stMid.close()
          if stage >= 3:
            wslot = st.enter_context(
                nc.sbuf_tensor("wslot", [128, 24576], BF16))
            gmx = st.enter_context(nc.sbuf_tensor("gmx", [128, 48], F32))
            hmx = st.enter_context(nc.sbuf_tensor("hmx", [128, 48], F32))
            umxc = st.enter_context(nc.sbuf_tensor("umxc", [128, 192], F32))
            pmxc = st.enter_context(nc.sbuf_tensor("pmxc", [128, 48], F32))
            redrow = st.enter_context(nc.sbuf_tensor("redrow", [1, 128], F32))
            redcol = st.enter_context(nc.sbuf_tensor("redcol", [128, 1], F32))
            arst = st.enter_context(nc.sbuf_tensor("arst", [1, 16], F32))
            padt = st.enter_context(nc.sbuf_tensor("padt", [128, 8], F32))

            def vpad(v, n=2):
                for i in range(n):
                    v.tensor_copy(padt[:, i:i + 1], padt[:, i:i + 1])


            def finalize_max(v, sy, v2, partials, ncols, col, tag):
                if v is not None:
                    vpad(v)
                    s_ve.inc(v.reduce_max(redcol[:], partials[:, 0:ncols], AX.X,
                                          apply_absolute_value=True))
                    marks[tag + "_r"] = s_ve.v
                    return
                if sy is not None:
                    s_ve.wait(sy, marks[tag + "_r"])
                    s_in.dinc(sy.dma_start(scr_red[0:1, :].rearrange("a p -> p a"), redcol[:]))
                    sy.wait_ge(s_in.h, s_in.v)
                    s_in.dinc(sy.dma_start(redrow[:], scr_red))
                    marks[tag + "_row"] = s_in.v
                    return
                s_in.wait(v2, marks[tag + "_row"])
                s_ve.inc(v2.reduce_max(maxstage[0:1, col:col + 1], redrow[:], AX.X,
                                       apply_absolute_value=True))
                marks[tag + "_ms"] = s_ve.v

            def barrier_g(g, cin, cout, wait_mark, tag):
                s_ve.wait(g, wait_mark)
                s_gd.dinc(g.dma_start(cin, maxstage[:]))
                g.wait_ge(s_gd.h, s_gd.v)
                s_cc.inc(g.collective_compute(
                    "AllReduce", ALU.max, replica_groups=[list(range(8))],
                    ins=[cin.opt()], outs=[cout.opt()]))
                g.wait_ge(s_cc.h, s_cc.v)
                s_gd.dinc(g.dma_start(arst[:], cout))
                marks[tag + "_ag"] = s_gd.v

            def barrier_t(t, tag):
                s_gd.wait(t, marks[tag + "_ag"])
                s_mm.inc(t.matmul(ps[7][:, 0:16], onesr[:], arst[:], start=True, stop=True))
                marks[tag + "_bc"] = s_mm.v

            def barrier_v(v, tag, cols):
                s_mm.wait(v, marks[tag + "_bc"])
                s_ve.inc(v.tensor_copy(bars[:], ps[7][:, 0:16]))
                marks[tag + "_bars"] = s_ve.v
                vpad(v, 3)
                last = None
                for (src_col, s_dst, i_dst) in cols:
                    v.tensor_scalar(s_dst, bars[:, src_col:src_col + 1], 1.0 / QMAX, 1e-12,
                                    ALU.mult, ALU.max)
                    vpad(v)
                    last = v.reciprocal(i_dst, s_dst)
                s_ve.inc(last)
                marks[tag + "_sc"] = s_ve.v
                vpad(v)

            def dq_tree(v, wi, wo, aw, tm, bm, sbm, inv, q16, I):
                nb = I // 32
                a3 = aw[:, 0:I].rearrange("p (c j) -> p c j", j=32)
                t3 = tm[:, 0:I // 2].rearrange("p (c j) -> p c j", j=16)
                v.tensor_tensor(t3, a3[:, :, 0:16], a3[:, :, 16:32], ALU.max)
                t4 = tm[:, 0:I // 4].rearrange("p (c j) -> p c j", j=8)
                v.tensor_tensor(t4, t3[:, :, 0:8], t3[:, :, 8:16], ALU.max)
                t5 = tm[:, 0:I // 8].rearrange("p (c j) -> p c j", j=4)
                v.tensor_tensor(t5, t4[:, :, 0:4], t4[:, :, 4:8], ALU.max)
                t6 = tm[:, 0:I // 16].rearrange("p (c j) -> p c j", j=2)
                v.tensor_tensor(t6, t5[:, :, 0:2], t5[:, :, 2:4], ALU.max)
                v.tensor_tensor(bm[:, 0:nb], t6[:, :, 0:1].squeeze(2),
                                t6[:, :, 1:2].squeeze(2), ALU.max)
                vpad(v)
                v.tensor_scalar(sbm[:, 0:nb], bm[:, 0:nb], 1.0 / 7.0, 1e-12, ALU.mult, ALU.max)
                vpad(v)
                v.reciprocal(inv[:, 0:nb], sbm[:, 0:nb])
                vpad(v)
                w3 = wi.rearrange("p (c j) -> p c j", j=32)
                q3 = q16[:, 0:I].rearrange("p (c j) -> p c j", j=32)
                v.tensor_tensor(q3, w3, inv[:, 0:nb].unsqueeze(2).broadcast_to([128, nb, 32]),
                                ALU.mult)
                o3 = wo.rearrange("p (c j) -> p c j", j=32)
                return v.tensor_tensor(o3, q16[:, 0:I].rearrange("p (c j) -> p c j", j=32),
                                       sbm[:, 0:nb].unsqueeze(2).broadcast_to([128, nb, 32]), ALU.mult)

            base_mm_dmm = s_mm.v
            with ExitStack() as stq:
                win2 = stq.enter_context(nc.sbuf_tensor("win2", [128, 2 * D], F32))
                aw = stq.enter_context(nc.sbuf_tensor("aw", [128, D], F32))
                tmq = stq.enter_context(nc.sbuf_tensor("tmq", [128, D // 2], F32))
                bmq = stq.enter_context(nc.sbuf_tensor("bmq", [128, 32], F32))
                sbmq = stq.enter_context(nc.sbuf_tensor("sbmq", [128, 32], F32))
                invq = stq.enter_context(nc.sbuf_tensor("invq", [128, 32], F32))
                q16q = stq.enter_context(nc.sbuf_tensor("q16q", [128, D], I16))
                wqbq = stq.enter_context(nc.sbuf_tensor("wqbq", [128, 2 * D], BF16))
                x1in = stq.enter_context(nc.sbuf_tensor("x1in", [128, 3 * D], F32))
                xqi = stq.enter_context(nc.sbuf_tensor("xqi", [128, 2 * D], I16))
                xqb = stq.enter_context(nc.sbuf_tensor("xqb", [128, 2 * D], BF16))

                _w2_b = [s_q[8 + i].v for i in range(2)]
                _wqs_b = [s_q[10 + i].v for i in range(2)]
                dq_l = {ci: (s_q[8 + ci % 2], _w2_b[ci % 2] + 16 * (ci // 2 + 1))
                        for ci in range(OC)}
                dq_s = {ci: (s_q[10 + ci % 2], _wqs_b[ci % 2] + 16 * (ci // 2 + 1))
                        for ci in range(OC)}
                dq_a = {}; dq_v = {}
                c = s_ac.v
                for ci in range(OC):
                    c += 1
                    dq_a[ci] = c
                c = s_ve.v
                for ci in range(OC):
                    c += 1
                    dq_v[ci] = c
                base_dq_ve_end = c

                with nc.Block("wdqg") as blk:
                    @blk.sync
                    def _(sy):
                        for ci in range(OC):
                            if ci >= 2:
                                s_ve.wait(sy, dq_v[ci - 2])
                            s_q[8 + ci % 2].dinc(sy.dma_start(
                                win2[:, (ci % 2) * D:(ci % 2 + 1) * D],
                                d_wg[ci * 128:(ci + 1) * 128, :]))
                            assert s_q[8 + ci % 2].v == dq_l[ci][1]
                            if ci >= 1:
                                s_ve.wait(sy, dq_v[ci - 1])
                                s_q[10 + (ci - 1) % 2].dinc(sy.dma_start(
                                    scr_wqg[(ci - 1) * 128:ci * 128, :],
                                    wqbq[:, ((ci - 1) % 2) * D:((ci - 1) % 2 + 1) * D]))
                                assert s_q[10 + (ci - 1) % 2].v == dq_s[ci - 1][1]
                        s_ve.wait(sy, dq_v[OC - 1])
                        s_q[10 + (OC - 1) % 2].dinc(sy.dma_start(
                            scr_wqg[(OC - 1) * 128:OC * 128, :],
                            wqbq[:, ((OC - 1) % 2) * D:((OC - 1) % 2 + 1) * D]))
                        assert s_q[10 + (OC - 1) % 2].v == dq_s[OC - 1][1]
                        marks["wqg_done"] = [(s_q[10], s_q[10].v), (s_q[11], s_q[11].v)]

                    @blk.scalar
                    def _(sc):
                        for ci in range(OC):
                            W(sc, dq_l[ci])
                            if ci >= 1:
                                s_ve.wait(sc, dq_v[ci - 1])
                            s_ac.inc(sc.activation(aw[:], win2[:, (ci % 2) * D:(ci % 2 + 1) * D],
                                                   AF.Abs))
                            assert s_ac.v == dq_a[ci]

                    @blk.vector
                    def _(v):
                        for ci in range(OC):
                            s_ac.wait(v, dq_a[ci])
                            if ci >= 2:
                                W(v, dq_s[ci - 2])
                            wi = win2[:, (ci % 2) * D:(ci % 2 + 1) * D]
                            wo = wqbq[:, (ci % 2) * D:(ci % 2 + 1) * D]
                            s_ve.inc(dq_tree(v, wi, wo, aw, tmq, bmq, sbmq, invq, q16q, D))
                            assert s_ve.v == dq_v[ci]

                base_xv = s_ve.v
                xq_v = {ti: base_xv + 4 + ti + 1 for ti in range(NT)}
                base_xa = s_ac.v
                xq_a = {ti: base_xa + ti + 1 for ti in range(NT)}
                _xl_b = [s_q[i].v for i in range(3)]
                _xs_b = [s_q[3].v, s_q[7].v]
                xq_l = {t: (s_q[t % 3], _xl_b[t % 3] + 16 * (t // 3 + 1))
                        for t in range(NT)}
                xq_s = {t: (s_q[3 + 4 * (t % 2)], _xs_b[t % 2] + 16 * (t // 2 + 1))
                        for t in range(NT)}

                with nc.Block("xq") as blk:
                    @blk.vector
                    def _(v):
                        finalize_max(v, None, None, mxcols, NT, 0, "xmx")

                    @blk.sync
                    def _(sy):
                        finalize_max(None, sy, None, mxcols, NT, 0, "xmx")

                    @blk.vector
                    def _(v):
                        finalize_max(None, None, v, mxcols, NT, 0, "xmx")

                    @blk.gpsimd
                    def _(g):
                        barrier_g(g, cc2_in, cc2_out, marks["xmx_ms"], "x")

                    @blk.tensor
                    def _(t):
                        barrier_t(t, "x")

                    @blk.vector
                    def _(v):
                        barrier_v(v, "x", [(0, scol[:, 0:1], scol[:, 1:2])])
                        assert s_ve.v == base_xv + 4
                        for ti in range(NT):
                            W(v, xq_l[ti])
                            if ti >= 2:
                                s_ac.wait(v, xq_a[ti - 2])
                            x1t = x1in[:, (ti % 3) * D:(ti % 3 + 1) * D]
                            s_ve.inc(v.tensor_scalar(
                                xqi[:, (ti % 2) * D:(ti % 2 + 1) * D],
                                x1t, scol[:, 1:2], None, ALU.mult))
                            assert s_ve.v == xq_v[ti]

                    if dbg_x2 is not None:
                        @blk.sync
                        def _(sy):
                            s_ve.wait(sy, marks["x_sc"])
                            s_out.dinc(sy.dma_start(dbg_x2[0:1, 0:16], arst[:]))
                            s_out.dinc(sy.dma_start(dbg_x2[0:1, 16:32], bars[0:1, :]))
                            s_out.dinc(sy.dma_start(dbg_x2[0:1, 32:40], scol[0:1, :]))
                            s_out.dinc(sy.dma_start(dbg_x2[0:1, 40:56], maxstage[:]))
                            s_out.dinc(sy.dma_start(dbg_x2[0:1, 64:192], redrow[:]))

                        @blk.gpsimd
                        def _(g):
                            s_ve.wait(g, marks["x_sc"])
                            s_gd.dinc(g.dma_start(dbg_x2[0:1, 56:64].rearrange("a b -> a b"), cc2_out[0:1, 0:8]))

                    @blk.scalar
                    def _(sc):
                        W(sc, marks["wqg_done"])
                        for k in range(KD):
                            s_q[8].dinc(sc.dma_start(
                                wslot[:, k * I3:(k + 1) * I3],
                                scr_wqg[:, k * 128:(k + 1) * 128],
                                transpose=True))
                        marks["wgT_done"] = [(s_q[8], s_q[8].v)]
                        for ti in range(NT):
                            s_ve.wait(sc, xq_v[ti])
                            if ti >= 2:
                                W(sc, xq_s[ti - 2])
                            s_ac.inc(sc.activation(
                                xqb[:, (ti % 2) * D:(ti % 2 + 1) * D],
                                xqi[:, (ti % 2) * D:(ti % 2 + 1) * D],
                                AF.Copy, bias=0.0, scale=scol[:, 0:1]))
                            assert s_ac.v == xq_a[ti]

                    @blk.sync
                    def _(sy):
                        W(sy, marks["x1_done"])
                        for t in range(NT + 2):
                            if t < NT:
                                if t >= 3:
                                    s_ve.wait(sy, xq_v[t - 3])
                                s_q[t % 3].dinc(sy.dma_start(
                                    x1in[:, (t % 3) * D:(t % 3 + 1) * D],
                                    scr_x1[t * 128:(t + 1) * 128, :]))
                                assert s_q[t % 3].v == xq_l[t][1]
                            if t >= 2:
                                tj = t - 2
                                s_ac.wait(sy, xq_a[tj])
                                s_q[3 + 4 * (tj % 2)].dinc(sy.dma_start(
                                    scr_xq[tj * 128:(tj + 1) * 128, :],
                                    xqb[:, (tj % 2) * D:(tj % 2 + 1) * D]))
                                assert s_q[3 + 4 * (tj % 2)].v == xq_s[tj][1]
                        marks["xq_done"] = [(s_q[3], s_q[3].v), (s_q[7], s_q[7].v)]
                        W(sy, marks["xq_done"])
                        s_mm.wait(sy, base_mm_dmm)
                        for k in range(KD):
                            s_in.dinc(sy.dma_start(utb[:, k * TC:(k + 1) * TC],
                                                   scr_xq[:, k * 128:(k + 1) * 128],
                                                   transpose=True))
                        marks["xqT_done"] = s_in.v

            with ExitStack() as stp1:
                gstage = stp1.enter_context(nc.sbuf_tensor("gstage", [128, 2 * NH], BF16))
                win2b = stp1.enter_context(nc.sbuf_tensor("win2b", [128, 2 * I3], F32))
                awb = stp1.enter_context(nc.sbuf_tensor("awb", [128, I3], F32))
                tmb = stp1.enter_context(nc.sbuf_tensor("tmb", [128, I3 // 2], F32))
                bmb = stp1.enter_context(nc.sbuf_tensor("bmb", [128, 96], F32))
                sbmb = stp1.enter_context(nc.sbuf_tensor("sbmb", [128, 96], F32))
                invb = stp1.enter_context(nc.sbuf_tensor("invb", [128, 96], F32))
                q16b = stp1.enter_context(nc.sbuf_tensor("q16b", [128, I3], I16))
                wqbb = stp1.enter_context(nc.sbuf_tensor("wqbb", [128, 2 * I3], BF16))

                DQ2 = [("u", ci, D) for ci in range(OC)] + [("d", ci, I3) for ci in range(8)]
                p1_mm = {}
                c = s_mm.v
                for idx in range(OC * 8):
                    c += 1
                    p1_mm[idx] = c
                p1_ac = {}
                c = s_ac.v
                for idx in range(OC * 8):
                    c += 1
                    p1_ac[idx] = c
                chunks_for_oc = [[] for _ in range(OC)]
                for i2, item in enumerate(DQ2):
                    chunks_for_oc[min(OC - 1, (i2 * OC) // len(DQ2))].append((i2, item))
                dq2_a = {}; dq2_v = {}; dq2_l = {}; dq2_s = {}
                c = s_ac.v + OC * 8
                for oc in range(OC):
                    for (i2, item) in chunks_for_oc[oc]:
                        pass
                p1_ac = {}; dq2_a = {}
                c = s_ac.v
                for oc in range(OC):
                    for t8 in range(8):
                        c += 1
                        p1_ac[oc * 8 + t8] = c
                    for (i2, item) in chunks_for_oc[oc]:
                        c += 1
                        dq2_a[i2] = c
                p1_ve = {}; dq2_v = {}
                c = s_ve.v
                for oc in range(OC):
                    for hf in range(2):
                        c += 1
                        p1_ve[oc * 2 + hf] = c
                    for (i2, item) in chunks_for_oc[oc]:
                        c += 1
                        dq2_v[i2] = c
                base_p1_ve_end = c
                _gs_b = [s_q[8 + i].v for i in range(2)]
                _dl_b = [s_q[10 + i].v for i in range(2)]
                _ds_b = [s_q[i].v for i in range(2)]
                p1_st = {hc: (s_q[8 + hc % 2], _gs_b[hc % 2] + 16 * (hc // 2 + 1))
                         for hc in range(2 * OC)}
                dq2_l = {i2: (s_q[10 + i2 % 2], _dl_b[i2 % 2] + 16 * (i2 // 2 + 1))
                         for i2 in range(len(DQ2))}
                dq2_s = {i2: (s_q[i2 % 2], _ds_b[i2 % 2] + 16 * (i2 // 2 + 1))
                         for i2 in range(len(DQ2))}

                def dq2_src(i2):
                    kind, ci, I = DQ2[i2]
                    if kind == "u":
                        return d_wu[ci * 128:(ci + 1) * 128, :], scr_wqu[ci * 128:(ci + 1) * 128, :], I
                    return d_wd[ci * 128:(ci + 1) * 128, :], scr_wqd[ci * 128:(ci + 1) * 128, :], I

                with nc.Block("p1") as blk:
                    @blk.tensor
                    def _(t):
                        s_in.wait(t, marks["xqT_done"])
                        W(t, marks["wgT_done"])
                        for oc in range(OC):
                            for t8 in range(8):
                                idx = oc * 8 + t8
                                bank = ps[idx % 8]
                                if idx >= 8:
                                    s_ac.wait(t, p1_ac[idx - 8])
                                for k in range(KD):
                                    mm = t.matmul(bank[:],
                                                  wslot[:, k * I3 + oc * 128: k * I3 + oc * 128 + 128],
                                                  utb[:, k * TC + t8 * 512: k * TC + t8 * 512 + 512],
                                                  start=(k == 0), stop=(k == KD - 1))
                                s_mm.inc(mm)
                                assert s_mm.v == p1_mm[idx]

                    @blk.scalar
                    def _(sc):
                        for oc in range(OC):
                            for t8 in range(8):
                                idx = oc * 8 + t8
                                hc = idx // 4
                                bank = ps[idx % 8]
                                s_mm.wait(sc, p1_mm[idx])
                                if t8 % 4 == 0 and hc >= 2:
                                    W(sc, p1_st[hc - 2])
                                    s_ve.wait(sc, p1_ve[hc - 2])
                                s_ac.inc(sc.activation(
                                    gstage[:, (hc % 2) * NH + (t8 % 4) * 512:
                                           (hc % 2) * NH + (t8 % 4) * 512 + 512],
                                    bank[:], AF.Copy, bias=0.0, scale=1.0))
                                assert s_ac.v == p1_ac[idx]
                            for (i2, item) in chunks_for_oc[oc]:
                                dw, dscr, I = dq2_src(i2)
                                W(sc, dq2_l[i2])
                                if i2 >= 1:
                                    s_ve.wait(sc, dq2_v[i2 - 1])
                                s_ac.inc(sc.activation(awb[:, 0:I],
                                                       win2b[:, (i2 % 2) * I3:(i2 % 2) * I3 + I],
                                                       AF.Abs))
                                assert s_ac.v == dq2_a[i2]

                    @blk.vector
                    def _(v):
                        for oc in range(OC):
                            for hf in range(2):
                                hc = oc * 2 + hf
                                s_ac.wait(v, p1_ac[oc * 8 + hf * 4 + 3])
                                s_ve.inc(v.reduce_max(
                                    gmx[:, hc:hc + 1],
                                    gstage[:, (hc % 2) * NH:(hc % 2 + 1) * NH],
                                    AX.X, apply_absolute_value=True))
                                assert s_ve.v == p1_ve[hc]
                            for (i2, item) in chunks_for_oc[oc]:
                                dw, dscr, I = dq2_src(i2)
                                s_ac.wait(v, dq2_a[i2])
                                if i2 >= 2:
                                    W(v, dq2_s[i2 - 2])
                                wi = win2b[:, (i2 % 2) * I3:(i2 % 2) * I3 + I]
                                wo = wqbb[:, (i2 % 2) * I3:(i2 % 2) * I3 + I]
                                s_ve.inc(dq_tree(v, wi, wo, awb, tmb, bmb, sbmb, invb, q16b, I))
                                assert s_ve.v == dq2_v[i2]

                    @blk.sync
                    def _(sy):
                        for oc in range(OC):
                            for hf in range(2):
                                hc = oc * 2 + hf
                                s_ac.wait(sy, p1_ac[oc * 8 + hf * 4 + 3])
                                s_q[8 + hc % 2].dinc(sy.dma_start(
                                    scr_G.rearrange("(a p) t -> a p t", p=128)[oc, :, hf * NH:(hf + 1) * NH],
                                    gstage[:, (hc % 2) * NH:(hc % 2 + 1) * NH]))
                                assert s_q[8 + hc % 2].v == p1_st[hc][1]
                            for (i2, item) in chunks_for_oc[oc]:
                                dw, dscr, I = dq2_src(i2)
                                if i2 >= 2:
                                    s_ve.wait(sy, dq2_v[i2 - 2])
                                s_q[10 + i2 % 2].dinc(sy.dma_start(
                                    win2b[:, (i2 % 2) * I3:(i2 % 2) * I3 + I], dw))
                                assert s_q[10 + i2 % 2].v == dq2_l[i2][1]
                                if i2 >= 1:
                                    s_ve.wait(sy, dq2_v[i2 - 1])
                                    _, dscr_p, I_p = dq2_src(i2 - 1)
                                    s_q[(i2 - 1) % 2].dinc(sy.dma_start(
                                        dscr_p, wqbb[:, ((i2 - 1) % 2) * I3:((i2 - 1) % 2) * I3 + I_p]))
                                    assert s_q[(i2 - 1) % 2].v == dq2_s[i2 - 1][1]
                        i2 = len(DQ2) - 1
                        s_ve.wait(sy, dq2_v[i2])
                        _, dscr_p, I_p = dq2_src(i2)
                        s_q[i2 % 2].dinc(sy.dma_start(
                            dscr_p, wqbb[:, (i2 % 2) * I3:(i2 % 2) * I3 + I_p]))
                        assert s_q[i2 % 2].v == dq2_s[i2][1]
                        marks["wq_ud_done"] = [(s_q[0], s_q[0].v), (s_q[1], s_q[1].v)]

                with nc.Block("p1fin") as blk:
                    @blk.vector
                    def _(v):
                        s_ve.wait(v, base_p1_ve_end)
                        finalize_max(v, None, None, gmx, 2 * OC, 1, "gmx")

                    @blk.sync
                    def _(sy):
                        finalize_max(None, sy, None, gmx, 2 * OC, 1, "gmx")

                    @blk.vector
                    def _(v):
                        finalize_max(None, None, v, gmx, 2 * OC, 1, "gmx")

                    @blk.gpsimd
                    def _(g):
                        barrier_g(g, cc3_in, cc3_out, marks["gmx_ms"], "g")

                    @blk.tensor
                    def _(t):
                        barrier_t(t, "g")

                    @blk.vector
                    def _(v):
                        barrier_v(v, "g", [(1, scol[:, 2:3], scol[:, 3:4])])

            NHC = 2 * OC
            with ExitStack() as stp2:
                gin = stp2.enter_context(nc.sbuf_tensor("gin", [128, 3 * NH], BF16))
                gi16 = stp2.enter_context(nc.sbuf_tensor("gi16", [128, 3 * NH], I16))
                sigf = stp2.enter_context(nc.sbuf_tensor("sigf", [128, NH], F32))
                u16b = stp2.enter_context(nc.sbuf_tensor("u16b", [128, 2 * NH], U16))
                gatef = stp2.enter_context(nc.sbuf_tensor("gatef", [128, 2 * NH], F32))
                hhb = stp2.enter_context(nc.sbuf_tensor("hhb", [128, 3 * NH], BF16))
                hb = stp2.enter_context(nc.sbuf_tensor("hb", [128, 2 * NH], BF16))
                ubst = stp2.enter_context(nc.sbuf_tensor("ubst", [128, 2 * TC], BF16))

                p2_mm = {}
                c = s_mm.v
                for idx in range(OC * 8):
                    c += 1
                    p2_mm[idx] = c
                p2_ve = {}; f2_i16 = {}; f2_h = {}; f2_hm = {}
                c = s_ve.v
                for oc in range(OC + 1):
                    if oc < OC:
                        c += 1
                        p2_ve[oc] = c
                    if oc >= 1:
                        for hf in range(2):
                            hc = (oc - 1) * 2 + hf
                            c += 1
                            f2_i16[hc] = c
                            c += 1
                            f2_h[hc] = c
                            c += 1
                            f2_hm[hc] = c
                base_p2_ve_end = c
                f2_ev = {}; f2_sig = {}; f2_u16 = {}; f2_gate = {}
                c = s_ac.v
                for oc in range(OC + 1):
                    if oc < OC:
                        for t8 in range(8):
                            c += 1
                            f2_ev[oc * 8 + t8] = c
                    if oc >= 1:
                        for hf in range(2):
                            hc = (oc - 1) * 2 + hf
                            c += 1
                            f2_sig[hc] = c
                            c += 1
                            f2_u16[hc] = c
                            c += 1
                            f2_gate[hc] = c
                f2_hh = {}
                c = s_gp.v
                for hc in range(NHC):
                    c += 1
                    f2_hh[hc] = c
                base_p2_trs = s_in.v + 8 * 16
                _gl_b = [s_q[2 + i].v for i in range(2)]
                _hs_b = [s_q[4 + i].v for i in range(2)]
                _uus_b = [s_q[6 + i].v for i in range(2)]
                f2_l = {t: (s_q[2 + t % 2], _gl_b[t % 2] + 16 * (t // 2 + 1))
                        for t in range(NHC)}
                f2_s = {t: (s_q[4 + t % 2], _hs_b[t % 2] + 16 * (t // 2 + 1))
                        for t in range(NHC)}
                f2_us = {oc: (s_q[6 + oc % 2], _uus_b[oc % 2] + 16 * (oc // 2 + 1))
                         for oc in range(OC)}

                with nc.Block("p2") as blk:
                    @blk.sync
                    def _(sy):
                        s_mm.wait(sy, p1_mm[OC * 8 - 1])
                        W(sy, marks["wq_ud_done"])
                        for k in range(KD):
                            s_in.dinc(sy.dma_start(wslot[:, k * I3:(k + 1) * I3],
                                                   scr_wqu[:, k * 128:(k + 1) * 128],
                                                   transpose=True))
                        assert s_in.v == base_p2_trs
                        marks["wuT_done"] = s_in.v
                        for t in range(NHC + 2):
                            if t < NHC:
                                if t >= 3:
                                    s_ve.wait(sy, f2_h[t - 3])
                                W(sy, p1_st[t])
                                s_q[2 + t % 2].dinc(sy.dma_start(
                                    gin[:, (t % 3) * NH:(t % 3 + 1) * NH],
                                    scr_G.rearrange("(a p) t -> a p t", p=128)[t // 2, :, (t % 2) * NH:(t % 2 + 1) * NH]))
                                assert s_q[2 + t % 2].v == f2_l[t][1]
                                if t % 2 == 1:
                                    oc = t // 2
                                    s_ac.wait(sy, f2_ev[oc * 8 + 7])
                                    s_q[6 + oc % 2].dinc(sy.dma_start(
                                        scr_U[oc * 128:(oc + 1) * 128, :],
                                        ubst[:, (oc % 2) * TC:(oc % 2 + 1) * TC]))
                                    assert s_q[6 + oc % 2].v == f2_us[oc][1]
                            if t >= 2:
                                tj = t - 2
                                s_ve.wait(sy, f2_hm[tj])
                                s_q[4 + tj % 2].dinc(sy.dma_start(
                                    scr_h.rearrange("(a p) t -> a p t", p=128)[tj // 2, :, (tj % 2) * NH:(tj % 2 + 1) * NH],
                                    hb[:, (tj % 2) * NH:(tj % 2 + 1) * NH]))
                                assert s_q[4 + tj % 2].v == f2_s[tj][1]
                        marks["h_done"] = [(s_q[4], s_q[4].v), (s_q[5], s_q[5].v)]

                    @blk.tensor
                    def _(t):
                        s_in.wait(t, marks["wuT_done"])
                        for oc in range(OC):
                            for t8 in range(8):
                                idx = oc * 8 + t8
                                bank = ps[idx % 8]
                                if idx >= 8:
                                    s_ac.wait(t, f2_ev[idx - 8])
                                for k in range(KD):
                                    mm = t.matmul(bank[:],
                                                  wslot[:, k * I3 + oc * 128: k * I3 + oc * 128 + 128],
                                                  utb[:, k * TC + t8 * 512: k * TC + t8 * 512 + 512],
                                                  start=(k == 0), stop=(k == KD - 1))
                                s_mm.inc(mm)
                                assert s_mm.v == p2_mm[idx]

                    @blk.vector
                    def _(v):
                        s_ve.wait(v, marks["g_sc"])
                        for oc in range(OC + 1):
                            if oc < OC:
                                s_ac.wait(v, f2_ev[oc * 8 + 7])
                                s_ve.inc(v.reduce_max(
                                    umxc[:, oc:oc + 1],
                                    ubst[:, (oc % 2) * TC:(oc % 2 + 1) * TC],
                                    AX.X, apply_absolute_value=True))
                                assert s_ve.v == p2_ve[oc]
                            if oc < 1:
                                continue
                            for hf in range(2):
                                hc = (oc - 1) * 2 + hf
                                W(v, f2_l[hc])
                                if hc >= 3:
                                    s_ac.wait(v, f2_gate[hc - 3])
                                g_t = gin[:, (hc % 3) * NH:(hc % 3 + 1) * NH]
                                i_t = gi16[:, (hc % 3) * NH:(hc % 3 + 1) * NH]
                                s_ve.inc(v.tensor_scalar(i_t, g_t, scol[:, 3:4], None, ALU.mult))
                                assert s_ve.v == f2_i16[hc]
                                s_gp.wait(v, f2_hh[hc])
                                if hc >= 2:
                                    W(v, f2_s[hc - 2])
                                hh_t = hhb[:, (hc % 3) * NH:(hc % 3 + 1) * NH]
                                h_t = hb[:, (hc % 2) * NH:(hc % 2 + 1) * NH]
                                s_ve.inc(v.tensor_scalar(h_t, hh_t, 2.0 ** -16, None, ALU.mult))
                                assert s_ve.v == f2_h[hc]
                                s_ve.inc(v.reduce_max(hmx[:, hc:hc + 1], h_t, AX.X,
                                                      apply_absolute_value=True))
                                assert s_ve.v == f2_hm[hc]

                    @blk.scalar
                    def _(sc):
                        for oc in range(OC + 1):
                            if oc < OC:
                                for t8 in range(8):
                                    idx = oc * 8 + t8
                                    bank = ps[idx % 8]
                                    s_mm.wait(sc, p2_mm[idx])
                                    if t8 == 0 and oc >= 2:
                                        W(sc, f2_us[oc - 2])
                                        s_ve.wait(sc, p2_ve[oc - 2])
                                    s_ac.inc(sc.activation(
                                        ubst[:, (oc % 2) * TC + t8 * 512:
                                             (oc % 2) * TC + t8 * 512 + 512],
                                        bank[:], AF.Copy, bias=0.0, scale=1.0))
                                    assert s_ac.v == f2_ev[idx]
                            if oc < 1:
                                continue
                            for hf in range(2):
                                hc = (oc - 1) * 2 + hf
                                s_ve.wait(sc, f2_i16[hc])
                                i_t = gi16[:, (hc % 3) * NH:(hc % 3 + 1) * NH]
                                s_ac.inc(sc.activation(sigf[:], i_t, AF.Sigmoid,
                                                       bias=0.0, scale=scol[:, 2:3]))
                                assert s_ac.v == f2_sig[hc]
                                if hc >= 2:
                                    s_gp.wait(sc, f2_hh[hc - 2])
                                u_t = u16b[:, (hc % 2) * NH:(hc % 2 + 1) * NH]
                                s_ac.inc(sc.activation(u_t, sigf[:], AF.Copy,
                                                       bias=0.0, scale=65536.0))
                                assert s_ac.v == f2_u16[hc]
                                ga_t = gatef[:, (hc % 2) * NH:(hc % 2 + 1) * NH]
                                s_ac.inc(sc.activation(ga_t, i_t, AF.Copy,
                                                       bias=0.0, scale=scol[:, 2:3]))
                                assert s_ac.v == f2_gate[hc]

                    @blk.gpsimd
                    def _(g):
                        for hc in range(NHC):
                            s_ac.wait(g, f2_gate[hc])
                            if hc >= 3:
                                s_ve.wait(g, f2_h[hc - 3])
                            u_t = u16b[:, (hc % 2) * NH:(hc % 2 + 1) * NH]
                            ga_t = gatef[:, (hc % 2) * NH:(hc % 2 + 1) * NH]
                            hh_t = hhb[:, (hc % 3) * NH:(hc % 3 + 1) * NH]
                            s_gp.inc(g.tensor_tensor(hh_t, ga_t, u_t, ALU.mult))
                            assert s_gp.v == f2_hh[hc]

                with nc.Block("p2fin") as blk:
                    @blk.vector
                    def _(v):
                        s_ve.wait(v, base_p2_ve_end)
                        finalize_max(v, None, None, umxc, OC, 2, "umx")

                    @blk.sync
                    def _(sy):
                        finalize_max(None, sy, None, umxc, OC, 2, "umx")

                    @blk.vector
                    def _(v):
                        finalize_max(None, None, v, umxc, OC, 2, "umx")
                        finalize_max(v, None, None, hmx, NHC, 3, "hmx")

                    @blk.sync
                    def _(sy):
                        finalize_max(None, sy, None, hmx, NHC, 3, "hmx")

                    @blk.vector
                    def _(v):
                        finalize_max(None, None, v, hmx, NHC, 3, "hmx")

                    @blk.gpsimd
                    def _(g):
                        barrier_g(g, cc4_in, cc4_out, marks["hmx_ms"], "uh")

                    @blk.tensor
                    def _(t):
                        barrier_t(t, "uh")

                    @blk.vector
                    def _(v):
                        barrier_v(v, "uh", [(2, scol[:, 4:5], scol[:, 5:6]),
                                            (3, scol[:, 6:7], scol[:, 7:8])])

            if stop_after == "p2":
                raise _StopBuild

            st_utb.close()
            with ExitStack() as stp3:
                uib = stp3.enter_context(nc.sbuf_tensor("uib", [128, 2 * TC], I16))
                Uin = stp3.enter_context(
                    nc.sbuf_tensor("Uin", [128, 2 * TC], BF16, side="right"))
                upb = stp3.enter_context(
                    nc.sbuf_tensor("upb", [128, 3 * NH], F32, side="right"))
                hin = stp3.enter_context(nc.sbuf_tensor("hin", [128, 4 * NH], BF16))
                g2ib = stp3.enter_context(nc.sbuf_tensor("g2ib", [128, 4 * NH], I16))
                gate2b = stp3.enter_context(
                    nc.sbuf_tensor("gate2b", [128, 3 * NH], F32, side="right"))
                pb2 = stp3.enter_context(nc.sbuf_tensor("pb2", [128, 2 * NH], F32))

                p3_ui = {}; f3_g2i = {}; f3_pm = {}
                c = s_ve.v
                for oc in range(OC + 1):
                    if oc < OC:
                        c += 1
                        p3_ui[oc] = c
                    if oc >= 1:
                        for hf in range(2):
                            hc = (oc - 1) * 2 + hf
                            c += 1
                            f3_g2i[hc] = c
                            c += 1
                            f3_pm[hc] = c
                base_p2b_ve_end = c
                f3_up = {}; f3_g2 = {}
                c = s_ac.v
                for hc in range(NHC):
                    c += 1
                    f3_up[hc] = c
                    c += 1
                    f3_g2[hc] = c
                f3_p = {}
                c = s_gp.v
                for hc in range(NHC):
                    c += 1
                    f3_p[hc] = c
                _ul_b = [s_q[i].v for i in range(2)]
                _hl_b = [s_q[2 + i].v for i in range(2)]
                _us_b = [s_q[8 + i].v for i in range(2)]
                _gs2_b = [s_q[10 + i].v for i in range(2)]
                f3_ul = {oc: (s_q[oc % 2], _ul_b[oc % 2] + 16 * (oc // 2 + 1))
                         for oc in range(OC)}
                f3_l = {t: (s_q[2 + t % 2], _hl_b[t % 2] + 16 * (t // 2 + 1))
                        for t in range(NHC)}
                f3_us = {oc: (s_q[8 + oc % 2], _us_b[oc % 2] + 16 * (oc // 2 + 1))
                         for oc in range(OC)}
                f3_gs = {t: (s_q[10 + t % 2], _gs2_b[t % 2] + 16 * (t // 2 + 1))
                         for t in range(NHC)}

                with nc.Block("p2b") as blk:
                    @blk.vector
                    def _(v):
                        s_ve.wait(v, marks["uh_sc"])
                        for oc in range(OC + 1):
                            if oc < OC:
                                W(v, f3_ul[oc])
                                if oc >= 2:
                                    W(v, f3_us[oc - 2])
                                    s_ac.wait(v, f3_up[(oc - 2) * 2 + 1])
                                s_ve.inc(v.tensor_scalar(
                                    uib[:, (oc % 2) * TC:(oc % 2 + 1) * TC],
                                    Uin[:, (oc % 2) * TC:(oc % 2 + 1) * TC],
                                    scol[:, 5:6], None, ALU.mult))
                                assert s_ve.v == p3_ui[oc]
                            if oc < 1:
                                continue
                            for hf in range(2):
                                hc = (oc - 1) * 2 + hf
                                W(v, f3_l[hc])
                                if hc >= 4:
                                    s_ac.wait(v, f3_g2[hc - 4])
                                    W(v, f3_gs[hc - 4])
                                hi_t = hin[:, (hc % 4) * NH:(hc % 4 + 1) * NH]
                                g2_t = g2ib[:, (hc % 4) * NH:(hc % 4 + 1) * NH]
                                s_ve.inc(v.tensor_scalar(g2_t, hi_t, scol[:, 7:8], None, ALU.mult))
                                assert s_ve.v == f3_g2i[hc]
                                s_gp.wait(v, f3_p[hc])
                                p_t = pb2[:, (hc % 2) * NH:(hc % 2 + 1) * NH]
                                s_ve.inc(v.reduce_max(pmxc[:, hc:hc + 1], p_t, AX.X,
                                                      apply_absolute_value=True))
                                assert s_ve.v == f3_pm[hc]

                    @blk.scalar
                    def _(sc):
                        for hc in range(NHC):
                            oc, hf = hc // 2, hc % 2
                            s_ve.wait(sc, p3_ui[oc])
                            if hc >= 3:
                                s_gp.wait(sc, f3_p[hc - 3])
                            up_t = upb[:, (hc % 3) * NH:(hc % 3 + 1) * NH]
                            ui_sl = uib[:, (oc % 2) * TC + hf * NH:(oc % 2) * TC + (hf + 1) * NH]
                            s_ac.inc(sc.activation(up_t, ui_sl, AF.Copy,
                                                   bias=0.0, scale=scol[:, 4:5]))
                            assert s_ac.v == f3_up[hc]
                            s_ve.wait(sc, f3_g2i[hc])
                            g2_t = g2ib[:, (hc % 4) * NH:(hc % 4 + 1) * NH]
                            gf_t = gate2b[:, (hc % 3) * NH:(hc % 3 + 1) * NH]
                            s_ac.inc(sc.activation(gf_t, g2_t, AF.Copy,
                                                   bias=0.0, scale=scol[:, 6:7]))
                            assert s_ac.v == f3_g2[hc]

                    @blk.gpsimd
                    def _(g):
                        for hc in range(NHC):
                            s_ac.wait(g, f3_g2[hc])
                            if hc >= 2:
                                s_ve.wait(g, f3_pm[hc - 2])
                            up_t = upb[:, (hc % 3) * NH:(hc % 3 + 1) * NH]
                            gf_t = gate2b[:, (hc % 3) * NH:(hc % 3 + 1) * NH]
                            p_t = pb2[:, (hc % 2) * NH:(hc % 2 + 1) * NH]
                            s_gp.inc(g.tensor_tensor(p_t, gf_t, up_t, ALU.mult))
                            assert s_gp.v == f3_p[hc]

                    @blk.sync
                    def _(sy):
                        W(sy, marks["h_done"])
                        s_mm.wait(sy, p2_mm[OC * 8 - 1])
                        for t in range(NHC + 2):
                            if t < NHC:
                                if t % 2 == 0:
                                    oc = t // 2
                                    W(sy, f2_us[oc])
                                    if oc >= 2:
                                        s_ve.wait(sy, p3_ui[oc - 2])
                                    s_q[oc % 2].dinc(sy.dma_start(
                                        Uin[:, (oc % 2) * TC:(oc % 2 + 1) * TC],
                                        scr_U[oc * 128:(oc + 1) * 128, :]))
                                    assert s_q[oc % 2].v == f3_ul[oc][1]
                                if t >= 4:
                                    s_ve.wait(sy, f3_g2i[t - 4])
                                s_q[2 + t % 2].dinc(sy.dma_start(
                                    hin[:, (t % 4) * NH:(t % 4 + 1) * NH],
                                    scr_h.rearrange("(a p) t -> a p t", p=128)[t // 2, :, (t % 2) * NH:(t % 2 + 1) * NH]))
                                assert s_q[2 + t % 2].v == f3_l[t][1]
                                if t % 2 == 1:
                                    oc = t // 2
                                    s_ve.wait(sy, p3_ui[oc])
                                    s_q[8 + oc % 2].dinc(sy.dma_start(
                                        scr_ui[oc * 128:(oc + 1) * 128, :],
                                        uib[:, (oc % 2) * TC:(oc % 2 + 1) * TC]))
                                    assert s_q[8 + oc % 2].v == f3_us[oc][1]
                            if t >= 2:
                                tj = t - 2
                                s_ac.wait(sy, f3_g2[tj])
                                s_q[10 + tj % 2].dinc(sy.dma_start(
                                    scr_g2i.rearrange("(a p) t -> a p t", p=128)[tj // 2, :, (tj % 2) * NH:(tj % 2 + 1) * NH],
                                    g2ib[:, (tj % 4) * NH:(tj % 4 + 1) * NH]))
                                assert s_q[10 + tj % 2].v == f3_gs[tj][1]
                        marks["ui_done"] = [(s_q[8], s_q[8].v), (s_q[9], s_q[9].v)]
                        marks["g2i_done"] = [(s_q[10], s_q[10].v), (s_q[11], s_q[11].v)]
                        s_mm.wait(sy, p2_mm[OC * 8 - 1])
                        for k in range(24):
                            s_in.dinc(sy.dma_start(wslot[:, k * 1024:(k + 1) * 1024],
                                                   scr_wqd[:, k * 128:(k + 1) * 128],
                                                   transpose=True))
                        marks["wdT_done"] = s_in.v

                with nc.Block("p2bfin") as blk:
                    @blk.vector
                    def _(v):
                        s_ve.wait(v, base_p2b_ve_end)
                        finalize_max(v, None, None, pmxc, NHC, 4, "pmx")

                    @blk.sync
                    def _(sy):
                        finalize_max(None, sy, None, pmxc, NHC, 4, "pmx")

                    @blk.vector
                    def _(v):
                        finalize_max(None, None, v, pmxc, NHC, 4, "pmx")

                    @blk.gpsimd
                    def _(g):
                        barrier_g(g, cc5_in, cc5_out, marks["pmx_ms"], "p")

                    @blk.tensor
                    def _(t):
                        barrier_t(t, "p")

                    @blk.vector
                    def _(v):
                        barrier_v(v, "p", [(4, scol2[:, 0:1], scol2[:, 1:2])])


          if stage >= 4:
            NG = 16
            GW = 256
            with ExitStack() as stp4:
                g2blk = stp4.enter_context(nc.sbuf_tensor("g2blk", [128, 2 * OC * GW], I16))
                uiblk = stp4.enter_context(nc.sbuf_tensor("uiblk", [128, 2 * OC * GW], I16))
                oblk = stp4.enter_context(nc.sbuf_tensor("oblk", [128, 2 * OC * GW], BF16))
                up4 = stp4.enter_context(nc.sbuf_tensor("up4", [128, 2 * 1024], F32))
                p4 = stp4.enter_context(nc.sbuf_tensor("p4", [128, 2 * 1024], F32))
                k4 = stp4.enter_context(nc.sbuf_tensor("k4", [128, 2 * 1024], I16))
                x1in4 = stp4.enter_context(
                    nc.sbuf_tensor("x1in4", [128, 6 * D], F32, side="right"))
                r2b = stp4.enter_context(nc.sbuf_tensor("r2b", [128, 2 * D], F32))
                outst = stp4.enter_context(nc.sbuf_tensor("outst", [128, 2 * D], F32))
                stm2 = stp4.enter_context(nc.sbuf_tensor("stm2", [128, 2], F32))
                stq2 = stp4.enter_context(nc.sbuf_tensor("stq2", [128, 2], F32))
                trash2 = stp4.enter_context(nc.sbuf_tensor("trash2", [128, 520], F32))

                NOP = OC // 4
                NQ = NG * NOP
                f4_up = {}; f4_o = {}; p4_sq = {}; p4_rs = {}
                _sc_items = []
                for qi in range(NQ + 2):
                    if qi < NQ:
                        _sc_items.append(("up", qi))
                    if qi >= 2:
                        qo = qi - 2
                        _sc_items.append(("o", qo))
                        if qo % NOP == NOP - 1 and qo // NOP >= 2:
                            g_t = qo // NOP - 2
                            for tt in range(2):
                                _sc_items.append(("sq", g_t, tt))
                                _sc_items.append(("rs", g_t, tt))
                            for tt in range(2):
                                _sc_items.append(("os", g_t, tt))
                for g_t in range(NG - 2, NG):
                    for tt in range(2):
                        _sc_items.append(("sq", g_t, tt))
                        _sc_items.append(("rs", g_t, tt))
                    for tt in range(2):
                        _sc_items.append(("os", g_t, tt))
                c = s_ac.v
                for it in _sc_items:
                    if it[0] == "os":
                        continue
                    c += 1
                    if it[0] == "up":
                        f4_up[it[1]] = c
                    elif it[0] == "o":
                        f4_o[it[1]] = c
                    elif it[0] == "sq":
                        p4_sq[(it[1], it[2])] = c
                    else:
                        p4_rs[(it[1], it[2])] = c
                f4_p = {}
                c = s_gp.v
                for qi in range(NQ):
                    c += 1
                    f4_p[qi] = c
                f4_k = {}; p4_ev = {}; p4_st = {}; p4_out = {}
                _ve_items = []
                for g in range(NG + 2):
                    if g < NG:
                        for q in range(NOP):
                            _ve_items.append(("k", g * NOP + q))
                    if g >= 2:
                        for tt in range(2):
                            _ve_items.append(("ev", g - 2, tt))
                            _ve_items.append(("st", g - 2, tt))
                            _ve_items.append(("out", g - 2, tt))
                marks["p_sc2"] = s_ve.v + 1
                c = s_ve.v + 1
                for it in _ve_items:
                    c += 1
                    if it[0] == "k":
                        f4_k[it[1]] = c
                    elif it[0] == "ev":
                        p4_ev[(it[1], it[2])] = c
                    elif it[0] == "st":
                        p4_st[(it[1], it[2])] = c
                    else:
                        p4_out[(it[1], it[2])] = c
                p4_mm = {}
                c = s_mm.v
                for g in range(NG):
                    for tt in range(2):
                        for f in range(2):
                            c += 1
                            p4_mm[(g, tt, f)] = c
                _pg_b = [s_q[i].v for i in range(2)]
                _pu_b = [s_q[2 + i].v for i in range(2)]
                _px_b = [s_q[4 + i].v for i in range(3)]
                _os_b = [s_q[10 + i].v for i in range(2)]
                f4_lg = {g: (s_q[g % 2], _pg_b[g % 2] + 64 * (g // 2 + 1))
                         for g in range(NG)}
                f4_lu = {g: (s_q[2 + g % 2], _pu_b[g % 2] + 64 * (g // 2 + 1))
                         for g in range(NG)}
                p4_x1 = {}; p4_os = {}
                for g in range(NG):
                    for tt in range(2):
                        ti = 2 * g + tt
                        p4_x1[(g, tt)] = (s_q[4 + ti % 3],
                                          _px_b[ti % 3] + 16 * (ti // 3 + 1))
                        p4_os[(g, tt)] = (s_q[10 + ti % 2],
                                          _os_b[ti % 2] + 16 * (ti // 2 + 1))
                marks["out_done"] = [(s_q[10], _os_b[0] + 16 * NG),
                                     (s_q[11], _os_b[1] + 16 * NG)]

                with nc.Block("p3") as blk:
                    @blk.sync
                    def _(sy):
                        s_in.wait(sy, marks["wdT_done"])
                        W(sy, marks["ui_done"])
                        W(sy, marks["g2i_done"])
                        for g in range(NG):
                            if g >= 2:
                                s_gp.wait(sy, f4_p[(g - 2) * NOP + NOP - 1])
                                s_ac.wait(sy, f4_up[(g - 2) * NOP + NOP - 1])
                            for part in range(4):
                                src = _ap(scr_g2i.tensor, (part * 6 * 128) * TC + g * GW,
                                          [[TC, 128], [128 * TC, 6], [1, GW]])
                                dst = _ap(g2blk, (g % 2) * (OC * GW) + part * 6 * GW,
                                          [[2 * OC * GW, 128], [GW, 6], [1, GW]])
                                s_q[g % 2].dinc(sy.dma_start(dst, src))
                            assert s_q[g % 2].v == f4_lg[g][1]
                            for part in range(4):
                                src = _ap(scr_ui.tensor, (part * 6 * 128) * TC + g * GW,
                                          [[TC, 128], [128 * TC, 6], [1, GW]])
                                dst = _ap(uiblk, (g % 2) * (OC * GW) + part * 6 * GW,
                                          [[2 * OC * GW, 128], [GW, 6], [1, GW]])
                                s_q[2 + g % 2].dinc(sy.dma_start(dst, src))
                            assert s_q[2 + g % 2].v == f4_lu[g][1]
                            for tt in range(2):
                                ti = g * 2 + tt
                                if ti >= 6:
                                    s_ve.wait(sy, p4_ev[((ti - 6) // 2, (ti - 6) % 2)])
                                s_q[4 + ti % 3].dinc(sy.dma_start(
                                    x1in4[:, (ti % 6) * D:(ti % 6 + 1) * D],
                                    scr_x1[ti * 128:(ti + 1) * 128, :]))
                                assert s_q[4 + ti % 3].v == p4_x1[(g, tt)][1]

                    @blk.scalar
                    def _(sc):
                        s_ve.wait(sc, marks["p_sc2"])
                        def up_(qi):
                            g, q = qi // NOP, qi % NOP
                            if q == 0:
                                W(sc, f4_lu[g])
                            if qi >= 2:
                                s_gp.wait(sc, f4_p[qi - 2])
                            sl = slice((g % 2) * (OC * GW) + q * 1024,
                                       (g % 2) * (OC * GW) + (q + 1) * 1024)
                            s_ac.inc(sc.activation(
                                up4[:, (qi % 2) * 1024:(qi % 2 + 1) * 1024],
                                uiblk[:, sl], AF.Copy,
                                bias=0.0, scale=scol2[:, 2:3]))
                            assert s_ac.v == f4_up[qi]

                        def o_(qo):
                            g, q = qo // NOP, qo % NOP
                            if q == 0 and g >= 2:
                                s_mm.wait(sc, p4_mm[(g - 2, 1, 1)])
                            sl = slice((g % 2) * (OC * GW) + q * 1024,
                                       (g % 2) * (OC * GW) + (q + 1) * 1024)
                            s_ve.wait(sc, f4_k[qo])
                            s_ac.inc(sc.activation(
                                oblk[:, sl], k4[:, (qo % 2) * 1024:(qo % 2 + 1) * 1024],
                                AF.Copy, bias=0.0, scale=scol2[:, 0:1]))
                            assert s_ac.v == f4_o[qo]

                        def tile_(g, tt, kind):
                            if kind == "sq":
                                s_ve.wait(sc, p4_ev[(g, tt)])
                                r2 = r2b[:, (tt % 2) * D:(tt % 2 + 1) * D]
                                sc.activation(trash2[:, 8:520], r2[:, 0:512], AF.Square,
                                              accum_out=statQ[:, 4 * (g % 8) + 2 * tt:4 * (g % 8) + 2 * tt + 1])
                                s_ac.inc(sc.activation(trash2[:, 8:520], r2[:, 512:1024], AF.Square,
                                                       accum_out=statQ[:, 4 * (g % 8) + 2 * tt + 1:4 * (g % 8) + 2 * tt + 2]))
                                assert s_ac.v == p4_sq[(g, tt)]
                            else:
                                s_ve.wait(sc, p4_st[(g, tt)])
                                sq2 = stq2[:, tt:tt + 1]
                                sc.activation(sq2, sq2, AF.Ln)
                                sc.copy(trash2[:, 1:2], trash2[:, 1:2])
                                s_ac.inc(sc.activation(sq2, sq2, AF.Exp, scale=-0.5))
                                assert s_ac.v == p4_rs[(g, tt)]

                        def os_(g, tt):
                            ti = 2 * g + tt
                            s_ve.wait(sc, p4_out[(g, tt)])
                            s_q[10 + ti % 2].dinc(sc.dma_start(
                                d_out[ti * 128:(ti + 1) * 128, :],
                                outst[:, (ti % 2) * D:(ti % 2 + 1) * D]))
                            assert s_q[10 + ti % 2].v == p4_os[(g, tt)][1]

                        for it in _sc_items:
                            if it[0] == "up":
                                up_(it[1])
                            elif it[0] == "o":
                                o_(it[1])
                            elif it[0] == "os":
                                os_(it[1], it[2])
                            else:
                                tile_(it[1], it[2], it[0])

                    @blk.gpsimd
                    def _(g_):
                        for qi in range(NQ):
                            g, q = qi // NOP, qi % NOP
                            s_ac.wait(g_, f4_up[qi])
                            if q == 0:
                                W(g_, f4_lg[g])
                            if qi >= 2:
                                s_ve.wait(g_, f4_k[qi - 2])
                            sl = slice((g % 2) * (OC * GW) + q * 1024,
                                       (g % 2) * (OC * GW) + (q + 1) * 1024)
                            s_gp.inc(g_.tensor_tensor(
                                p4[:, (qi % 2) * 1024:(qi % 2 + 1) * 1024],
                                g2blk[:, sl],
                                up4[:, (qi % 2) * 1024:(qi % 2 + 1) * 1024], ALU.mult))
                            assert s_gp.v == f4_p[qi]

                    @blk.vector
                    def _(v):
                        s_ve.wait(v, marks["p_sc"])
                        s_ve.inc(v.tensor_tensor(scol2[:, 2:3], scol[:, 4:5],
                                                 scol[:, 6:7], ALU.mult))
                        assert s_ve.v == marks["p_sc2"]
                        vpad(v)

                        def k_(qi):
                            s_gp.wait(v, f4_p[qi])
                            if qi >= 2:
                                s_ac.wait(v, f4_o[qi - 2])
                            s_ve.inc(v.tensor_scalar(
                                k4[:, (qi % 2) * 1024:(qi % 2 + 1) * 1024],
                                p4[:, (qi % 2) * 1024:(qi % 2 + 1) * 1024],
                                scol2[:, 1:2], None, ALU.mult))
                            assert s_ve.v == f4_k[qi]

                        def tile_(g, tt, kind):
                            ti = g * 2 + tt
                            if kind == "ev":
                                s_mm.wait(v, p4_mm[(g, tt, 1)])
                                W(v, p4_x1[(g, tt)])
                                if g >= 1:
                                    s_ac.wait(v, p4_sq[(g - 1, tt)])
                                bank0 = ps[(g % 2) * 4 + tt * 2]
                                bank1 = ps[(g % 2) * 4 + tt * 2 + 1]
                                x1t = x1in4[:, (ti % 6) * D:(ti % 6 + 1) * D]
                                r2 = r2b[:, (tt % 2) * D:(tt % 2 + 1) * D]
                                sS = statS[:, 4 * (g % 8) + 2 * tt:4 * (g % 8) + 2 * tt + 1]
                                sS2 = statS[:, 4 * (g % 8) + 2 * tt + 1:4 * (g % 8) + 2 * tt + 2]
                                v.scalar_tensor_tensor(r2[:, 0:512], bank0[:], 1.0,
                                                       x1t[:, 0:512], ALU.mult, ALU.add,
                                                       accum_out=sS)
                                s_ve.inc(v.scalar_tensor_tensor(
                                    r2[:, 512:1024], bank1[:], 1.0,
                                    x1t[:, 512:1024], ALU.mult, ALU.add, accum_out=sS2))
                                assert s_ve.v == p4_ev[(g, tt)]
                            elif kind == "st":
                                s_ac.wait(v, p4_sq[(g, tt)])
                                sm = stm2[:, tt:tt + 1]
                                sq2 = stq2[:, tt:tt + 1]
                                sS = statS[:, 4 * (g % 8) + 2 * tt:4 * (g % 8) + 2 * tt + 1]
                                sS2 = statS[:, 4 * (g % 8) + 2 * tt + 1:4 * (g % 8) + 2 * tt + 2]
                                qQ = statQ[:, 4 * (g % 8) + 2 * tt:4 * (g % 8) + 2 * tt + 1]
                                qQ2 = statQ[:, 4 * (g % 8) + 2 * tt + 1:4 * (g % 8) + 2 * tt + 2]
                                v.tensor_tensor(sm, sS, sS2, ALU.add)
                                v.tensor_tensor(sq2, qQ, qQ2, ALU.add)
                                v.tensor_copy(trash2[:, 2:3], trash2[:, 2:3])
                                v.tensor_tensor(trash2[:, 3:4], sm, sm, ALU.mult)
                                v.tensor_scalar(sq2, sq2, 1.0 / D, LNEPS, ALU.mult, ALU.add)
                                v.tensor_scalar(sm, sm, 1.0 / D, None, ALU.mult)
                                v.tensor_scalar(trash2[:, 3:4], trash2[:, 3:4],
                                                1.0 / (D * D), None, ALU.mult)
                                v.tensor_copy(trash2[:, 4:5], trash2[:, 4:5])
                                s_ve.inc(v.tensor_tensor(sq2, sq2, trash2[:, 3:4], ALU.subtract))
                                assert s_ve.v == p4_st[(g, tt)]
                            else:
                                s_ac.wait(v, p4_rs[(g, tt)])
                                sm = stm2[:, tt:tt + 1]
                                sq2 = stq2[:, tt:tt + 1]
                                r2 = r2b[:, (tt % 2) * D:(tt % 2 + 1) * D]
                                tj = ti
                                if ti >= 2:
                                    W(v, p4_os[((ti - 2) // 2, (ti - 2) % 2)])
                                s_ve.inc(v.tensor_scalar(
                                    outst[:, (tj % 2) * D:(tj % 2 + 1) * D], r2,
                                    sm, sq2, ALU.subtract, ALU.mult))
                                assert s_ve.v == p4_out[(g, tt)]

                        for it in _ve_items:
                            if it[0] == "k":
                                k_(it[1])
                            else:
                                tile_(it[1], it[2], it[0])

                    @blk.tensor
                    def _(t):
                        for g in range(NG):
                            for tt in range(2):
                                for f in range(2):
                                    bank = ps[(g % 2) * 4 + tt * 2 + f]
                                    if g >= 2:
                                        s_ve.wait(t, p4_ev[(g - 2, tt)])
                                    for oc in range(OC):
                                        q = oc // 4
                                        if oc % 4 == 0:
                                            s_ac.wait(t, f4_o[g * NOP + q])
                                        mm = t.matmul(
                                            bank[:],
                                            oblk[:, (g % 2) * (OC * GW) + oc * GW + tt * 128:
                                                  (g % 2) * (OC * GW) + oc * GW + tt * 128 + 128],
                                            wslot[:, oc * 1024 + f * 512: oc * 1024 + f * 512 + 512],
                                            start=(oc == 0), stop=(oc == OC - 1))
                                    s_mm.inc(mm)
                                    assert s_mm.v == p4_mm[(g, tt, f)]

        except _StopBuild:
            pass



        if stage >= 4 and (dbg_ffn is not None or dbg_wq is not None or dbg_xq is not None or dbg_G is not None):
            with nc.Block("dbgffn") as blk:
                @blk.sync
                def _(sy):
                    W(sy, marks["out_done"])
                    if dbg_ffn is None:
                        return
                    s_out.dinc(sy.dma_start(dbg_ffn[0:1, 0:16], maxstage[:]))
                    s_out.dinc(sy.dma_start(dbg_ffn[0:1, 16:24], scol[0:1, :]))
                    s_out.dinc(sy.dma_start(dbg_ffn[0:1, 24:26], scol2[0:1, 0:2]))
                    s_out.dinc(sy.dma_start(dbg_ffn[0:1, 32:48], bars[0:1, :]))

                @blk.gpsimd
                def _(g):
                    W(g, marks["out_done"])
                    if dbg_wq is not None:
                        s_gd.dinc(g.dma_start(dbg_wq, scr_wqg))
                    if dbg_xq is not None:
                        s_gd.dinc(g.dma_start(dbg_xq, scr_xq))
                    if dbg_G is not None:
                        s_gd.dinc(g.dma_start(dbg_G, scr_G[0:128, :]))
                        s_gd.dinc(g.dma_start(dbg_h, scr_h[0:128, :]))
                        s_gd.dinc(g.dma_start(dbg_ui, scr_ui[0:128, :]))
                        s_gd.dinc(g.dma_start(dbg_g2, scr_g2i[0:128, :]))
                    g.wait_ge(s_gd.h, s_gd.v)

        with nc.Block("fin") as blk:
            @blk.sync
            def _(sy):
                if "out_done" in marks:
                    W(sy, marks["out_done"])
                if s_out.v:
                    s_out.wait(sy)
                s_in.wait(sy)
                for sq_ in s_q:
                    if sq_.v:
                        sq_.wait(sy)

        st.pop_all()
    return nc


def _host_inputs(inputs, core):
    b, h = core // 2, core % 2
    u = inputs["L0_attn_u"]
    woz = np.concatenate([inputs["L0_attn_W_omega"], inputs["L0_attn_W_zeta"]], axis=0)
    bom = inputs["L0_attn_b_omega"]
    bze = inputs["L0_attn_b_zeta"]
    bb = inputs["L0_attn_b_B"]
    slot = np.zeros((1, 1024), np.float32)
    slot[0, core * 128:(core + 1) * 128] = 1.0
    neig = np.zeros((1, 1024), np.float32)
    if core % 2 == 1:
        neig[0, (core - 1) * 128:core * 128] = 1.0
    return {
        "u_s": np.ascontiguousarray(u[b, h * TC:(h + 1) * TC, :]),
        "Woz": np.ascontiguousarray(woz),
        "WB": np.ascontiguousarray(inputs["L0_attn_W_B"]),
        "C_": np.ascontiguousarray(inputs["L0_attn_C"]),
        "D_": np.ascontiguousarray(inputs["L0_attn_D_mat"]),
        "w_gate": np.ascontiguousarray(inputs["L0_ffn_w_gate"]),
        "w_up": np.ascontiguousarray(inputs["L0_ffn_w_up"]),
        "w_down": np.ascontiguousarray(inputs["L0_ffn_w_down"]),
        "bias_om2": np.ascontiguousarray(np.concatenate([bom, bom])[:, None].astype(np.float32)),
        "bias_ze2": np.ascontiguousarray(np.concatenate([bze, bze])[:, None].astype(np.float32)),
        "bias_Bz2": np.ascontiguousarray(np.concatenate([bb[0:64], bb[0:64]])[:, None].astype(np.float32)),
        "bias_By2": np.ascontiguousarray(np.concatenate([bb[64:128], bb[64:128]])[:, None].astype(np.float32)),
        "slotmask": slot,
        "neighmask": neig,
        "ones_row": np.ones((1, 128), np.float32),
    }


def kernel(**inputs):
    key = (STAGE, tuple(sorted(DEBUG)), STOP_AFTER)
    if key not in _BUILT:
        _BUILT[key] = build(STAGE, tuple(DEBUG), STOP_AFTER)
    nc = _BUILT[key]
    in_maps = [_host_inputs(inputs, k) for k in range(8)]
    res = run_bass_kernel_spmd(nc, in_maps, core_ids=list(range(8)))
    out = np.zeros((B, T, D), np.float32)
    for k in range(8):
        b, h = k // 2, k % 2
        if "out_s" in res.results[k]:
            out[b, h * TC:(h + 1) * TC, :] = np.asarray(res.results[k]["out_s"])
    kernel.last_results = res.results
    return out
